# version 17
# baseline (speedup 1.0000x reference)
import numpy as np
import ml_dtypes
from contextlib import ExitStack

import concourse.bass as bass
import concourse.mybir as mybir
from concourse.bass_utils import run_bass_kernel_spmd

F32 = mybir.dt.float32
BF16 = mybir.dt.bfloat16
AF = mybir.ActivationFunctionType
ALU = mybir.AluOpType
AX = mybir.AxisListType.X

D = 1024
C8 = 8
TPC = 2048
NTT = 4
TT = 512
NH = 16
HD = 64
NSLOT = 4096
BIG = 1.0e30

bf16 = ml_dtypes.bfloat16


_SEM_GROUPS = {
    "tms": ["mm_stats", "mm_rms", "bc_ready", "rmsbc_ready", "ss_mm", "mm_band",
            "mm_bandT", "mm_negmT", "mm_bc1", "mm_bc2", "mm_wt", "mm_rb",
            "glog_mm", "zbc", "zrms_mm"],
    "vms": ["vinit", "var_ready", "ab_ready", "xn_ready", "rms_ready",
            "xn2_ready", "rs_ready", "band_sb", "negm", "rowtmp_sb", "mtscl",
            "setup_v", "wt_done", "gmax", "goh", "recip",
            "patch_done", "vt_ready", "z2_done", "zrstd", "om_done",
            "fmix", "fcomb", "e0zero", "e1zero", "xbt_ready", "gcarup",
            "gpos", "gsc1", "ysumv", "gcar"],
    "sms": ["sqk", "sqk2", "sqk3", "std_ready", "rms_std", "ss_sqrt", "sc_abs",
            "gsm", "zsq", "zstd", "fsil3"],
    "dmisc": ["cld", "gwld", "bld", "gst", "muld", "hank", "dbgst"],
}
_SEM_ALIAS = {n: g for g, names in _SEM_GROUPS.items() for n in names}


class Ctx:

    def __init__(self, nc, es, block):
        self.nc, self.es, self.block = nc, es, block
        self.scope = es
        self.sems = {}
        self.cnt = {}
        self.xmarks = {}

    def sem(self, name):
        name = _SEM_ALIAS.get(name, name)
        if name not in self.sems:
            self.sems[name] = self.es.enter_context(self.nc.semaphore(name))
            self.cnt[name] = 0
        return self.sems[name]

    def inc(self, inst, name, k=1):
        name = _SEM_ALIAS.get(name, name)
        s = self.sem(name)
        inst.then_inc(s, k)
        self.cnt[name] += k
        return inst

    def wait(self, eng, name):
        name = _SEM_ALIAS.get(name, name)
        s = self.sem(name)
        if self.cnt[name] > 0:
            eng.wait_ge(s, self.cnt[name])

    def cnt_of(self, name):
        name = _SEM_ALIAS.get(name, name)
        self.sem(name)
        return self.cnt[name]

    def chain(self, eng, inst, name=None):
        if name is None:
            name = "chn_" + str(getattr(eng, "engine", "x")).split(".")[-1]
        self.inc(inst, name)
        eng.wait_ge(self.sem(name), self.cnt[name])


def sb(ctx, name, p, f, dt=F32):
    return ctx.scope.enter_context(ctx.nc.sbuf_tensor(name, [p, f], dt,
                                                      allow_name_mangling=True))


def psum(ctx, name, p, f, dt=F32):
    return ctx.scope.enter_context(ctx.nc.psum_tensor(name, [p, f], dt,
                                                      allow_name_mangling=True))


class Fused:

    def __init__(self, nc, ctx, io, entry=()):
        self.nc, self.ctx, self.io, self.entry = nc, ctx, io, entry


def stage_begin(F, builder_locals):
    if F is None:
        nc = bass.Bass("TRN2", target_bir_lowering=False, debug=False)
        es = ExitStack()
        block = es.enter_context(nc.Block())
        ctx = Ctx(nc, es, block)

        def dt(name, shape, dtp, kind):
            return nc.dram_tensor(name, shape, dtp, kind=kind).ap()

        def fin():
            es.close()
        return nc, es, block, ctx, dt, fin
    nc, ctx = F.nc, F.ctx
    es = ExitStack()
    old_scope = ctx.scope
    ctx.scope = es
    block = es.enter_context(nc.Block())
    ctx.block = block

    def dt(name, shape, dtp, kind):
        return F.io[name]

    def fin():
        es.close()
        ctx.scope = old_scope

    if F.entry:
        def _barrier(e):
            for s_ in F.entry:
                ctx.wait(e, s_)
        for en in ("sync", "gpsimd", "tensor", "vector", "scalar"):
            getattr(block, en)(_barrier)
    return nc, es, block, ctx, dt, fin


def emit_weight_load(ctx, g, w_dram, w_sb, sem="wld"):
    Kd, Nd = w_dram.shape
    kcn = Kd // 128
    src = w_dram.rearrange("(kc p) n -> p kc n", p=128)
    dst = w_sb[:].rearrange("p (kc n) -> p kc n", kc=kcn)
    ctx.inc(g.dma_start(out=dst, in_=src), sem, 16)


def emit_proj(ctx, name, w_sb, x_sb, x_cols, out_cb, kcn=C8, nd=D, dout_chunks=C8,
              xw=TT * NTT // NTT, pbanks=None, wait_before=(), wait_before_ev=(),
              extra_mm=None):
    nc_mm = name + "_mm"
    nc_ev = name + "_ev"
    mm_base = ctx.cnt[nc_mm] if nc_mm in ctx.cnt else 0
    ev_base = ctx.cnt[nc_ev] if nc_ev in ctx.cnt else 0
    ctx.sem(nc_mm), ctx.sem(nc_ev)
    mm_base = ctx.cnt[nc_mm]
    ev_base = ctx.cnt[nc_ev]
    s0, w = x_cols

    def _(t):
        for wsem in wait_before:
            ctx.wait(t, wsem)
        for c in range(dout_chunks):
            vtarget = ev_base + c - 1
            if vtarget > 0:
                t.wait_ge(ctx.sem(nc_ev), vtarget)
            pb = pbanks[c % 2]
            mi = None
            for kc in range(kcn):
                mi = t.matmul(pb[:, 0:w],
                              w_sb[:, kc * nd + c * 128: kc * nd + c * 128 + 128],
                              x_sb[:, kc * xw + s0: kc * xw + s0 + w],
                              start=(kc == 0), stop=(kc == kcn - 1))
            if extra_mm is not None:
                mi = extra_mm(t, c, pb)
            ctx.inc(mi, nc_mm)
    ctx.block.tensor(_)

    def _(v):
        for wsem in wait_before_ev:
            ctx.wait(v, wsem)
        for c in range(dout_chunks):
            v.wait_ge(ctx.sem(nc_mm), mm_base + c + 1)
            li = out_cb(v, c, pbanks[c % 2])
            ctx.inc(li, nc_ev)
    ctx.block.vector(_)


def build_launch1(upto="all", ntt=NTT, F=None):
    nc, es, block, ctx, dt, fin = stage_begin(F, None)
    xT = dt("xT", [D, TPC], F32, "ExternalInput")
    Wq = dt("Wq", [D, D], F32, "ExternalInput")
    Wk = dt("Wk", [D, D], F32, "ExternalInput")
    Wv = dt("Wv", [D, D], F32, "ExternalInput")
    ln1w = dt("ln1w", [128, C8], F32, "ExternalInput")
    selw = dt("selw", [D, NH], F32, "ExternalInput")
    selT = dt("selT", [NH, D], F32, "ExternalInput")
    eyeb = dt("eyeb", [128, 128], BF16, "ExternalInput")

    O_xn = dt("xn", [D, TPC], F32, "ExternalOutput")
    O_qh = dt("qh", [D, TPC], BF16, "ExternalOutput")
    O_kh = dt("kh", [D, TPC], BF16, "ExternalOutput")
    O_vt = dt("vt", [TPC, NH * 65], BF16, "ExternalOutput")
    O_vs = dt("vsum", [128, C8], F32, "ExternalOutput")
    O_xtm = dt("xn_tm", [TPC, D], BF16, "ExternalOutput")

    with es:
        x_tt = sb(ctx, "x_tt", 128, C8 * TT)
        xn_tt = sb(ctx, "xn_tt", 128, C8 * TT)
        sq = sb(ctx, "sq", 128, C8 * TT)
        praw = sb(ctx, "praw", 128, C8 * TT)
        rows = sb(ctx, "rows", 1, 4 * TT)
        rs16 = sb(ctx, "rs16", 16, TT)
        xn2 = sb(ctx, "xn2", 128, C8 * TT, BF16)
        ob_tt = sb(ctx, "ob_tt", 128, C8 * TT, BF16)
        vt_tt = sb(ctx, "vt_tt", 128, (TT // 128) * NH * 65, BF16)
        xn_bt = sb(ctx, "xn_bt", 128, C8 * TT, BF16)
        xtm_tt = sb(ctx, "xtm_tt", 128, (TT // 128) * D, BF16)
        wq_sb = sb(ctx, "wq_sb", 128, C8 * D, BF16)
        wk_sb = sb(ctx, "wk_sb", 128, C8 * D, BF16)
        wv_sb = sb(ctx, "wv_sb", 128, C8 * D, BF16)
        ln1_sb = sb(ctx, "ln1_sb", 128, C8)
        sel_sb = sb(ctx, "sel_sb", 128, C8 * NH)
        selT_sb = sb(ctx, "selT_sb", 16, 128 * C8)
        eye_sb = sb(ctx, "eye_sb", 128, 128, BF16)
        ones_sb = sb(ctx, "ones_sb", 128, 128)
        vsum_sb = sb(ctx, "vsum_sb", 128, C8)
        vtmp = sb(ctx, "vtmp", 128, C8)
        cst = sb(ctx, "cst", 128, 2)

        ps_a = psum(ctx, "ps_a", 128, TT)
        ps_b = psum(ctx, "ps_b", 128, TT)
        ps_r1 = psum(ctx, "ps_r1", 1, TT)
        ps_r2 = psum(ctx, "ps_r2", 1, TT)
        ps_bc1 = psum(ctx, "ps_bc1", 128, TT)
        ps_bc2 = psum(ctx, "ps_bc2", 128, TT)
        ps_ss = psum(ctx, "ps_ss", 16, TT)
        ps_t1 = psum(ctx, "ps_t1", 128, 128, BF16)

        def _(g):
            emit_weight_load(ctx, g, Wq, wq_sb)
            emit_weight_load(ctx, g, Wk, wk_sb)
            emit_weight_load(ctx, g, Wv, wv_sb)
        block.gpsimd(_)

        if F is not None:
            zsb = sb(ctx, "e0zsb", 128, 4160, BF16)
            zsf = sb(ctx, "e0zsf", 128, 32)
            emit_exchange_zero(ctx, block, F, "e0", "khs", "vtms", zsb, zsf, 0)

        def _(sy):
            ctx.inc(sy.dma_start(out=ln1_sb[:], in_=ln1w), "cld", 16)
            ctx.inc(sy.dma_start(out=sel_sb[:].rearrange("p (c h) -> p c h", c=C8),
                                 in_=selw.rearrange("(c p) h -> p c h", p=128)), "cld", 16)
            ctx.inc(sy.dma_start(out=selT_sb[:].rearrange("h (c p) -> h c p", c=C8),
                                 in_=selT.rearrange("h (c p) -> h c p", p=128)), "cld", 16)
            ctx.inc(sy.dma_start(out=eye_sb[:], in_=eyeb), "cld", 16)
        block.sync(_)

        def _(v):
            v.memset(ones_sb[:], 1.0)
            v.memset(vsum_sb[:], 0.0)
            v.memset(cst[:, 0:1], 1e-5)
            ctx.inc(v.memset(cst[:, 1:2], 1e-6), "vinit")
        block.vector(_)

        for tt in range(ntt):
            s0 = tt * TT

            def _(sy, s0=s0):
                ctx.wait(sy, "xn_ready")
                src = xT.rearrange("(c p) t -> p c t", p=128)[:, :, s0:s0 + TT]
                ctx.inc(sy.dma_start(out=x_tt[:].rearrange("p (c t) -> p c t", c=C8), in_=src), "xld", 16)
            block.sync(_)

            def _(s):
                ctx.wait(s, "xld")
                ctx.wait(s, "ss_mm")
                ctx.inc(s.activation(sq[:], x_tt[:], AF.Square, bias=0.0, scale=1.0), "sqk")
            block.scalar(_)

            def _(t):
                ctx.wait(t, "xld")
                ctx.wait(t, "vinit")
                ctx.wait(t, "var_ready")
                for c in range(C8):
                    t.matmul(ps_r1[:], ones_sb[:, 0:1], x_tt[:, c * TT:(c + 1) * TT],
                             start=(c == 0), stop=(c == C8 - 1))
                ctx.wait(t, "sqk")
                mi = None
                for c in range(C8):
                    mi = t.matmul(ps_r2[:], ones_sb[:, 0:1], sq[:, c * TT:(c + 1) * TT],
                                  start=(c == 0), stop=(c == C8 - 1))
                ctx.inc(mi, "mm_stats")
            block.tensor(_)

            def _(v):
                ctx.wait(v, "mm_stats")
                S_, SS = rows[:, 0:TT], rows[:, TT:2 * TT]
                A_, B_ = rows[:, 2 * TT:3 * TT], rows[:, 3 * TT:4 * TT]
                ctx.chain(v, v.tensor_single_scalar(S_, ps_r1[:], 1.0 / D, ALU.mult))
                ctx.chain(v, v.tensor_single_scalar(SS, ps_r2[:], 1.0 / D, ALU.mult))
                ctx.chain(v, v.tensor_tensor(A_, S_, S_, ALU.mult))
                ctx.inc(v.tensor_tensor(A_, SS, A_, ALU.subtract), "var_ready")
            block.vector(_)

            def _(s):
                ctx.wait(s, "var_ready")
                A_ = rows[:, 2 * TT:3 * TT]
                ctx.inc(s.activation(A_, A_, AF.Sqrt, bias=cst[0:1, 0:1], scale=1.0), "std_ready")
            block.scalar(_)

            def _(v):
                ctx.wait(v, "std_ready")
                S_ = rows[:, 0:TT]
                A_, B_ = rows[:, 2 * TT:3 * TT], rows[:, 3 * TT:4 * TT]
                ctx.chain(v, v.reciprocal(A_, A_))
                ctx.chain(v, v.tensor_tensor(B_, S_, A_, ALU.mult))
                ctx.inc(v.tensor_single_scalar(B_, B_, -1.0, ALU.mult), "ab_ready")
            block.vector(_)

            def _(t):
                ctx.wait(t, "ab_ready")
                ctx.wait(t, "xn2_ready")
                t.matmul(ps_bc1[:], ones_sb[0:1, 0:128], rows[0:1, 2 * TT:3 * TT], start=True, stop=True)
                ctx.inc(t.matmul(ps_bc2[:], ones_sb[0:1, 0:128], rows[0:1, 3 * TT:4 * TT], start=True, stop=True), "bc_ready")
            block.tensor(_)

            def _(v):
                ctx.wait(v, "bc_ready")
                ctx.wait(v, "xnst")
                ctx.wait(v, "xbt_ready")
                li = None
                for c in range(C8):
                    xc = x_tt[:, c * TT:(c + 1) * TT]
                    xnc = xn_tt[:, c * TT:(c + 1) * TT]
                    ctx.chain(v, v.tensor_tensor(xnc, xc, ps_bc1[:], ALU.mult))
                    li = v.tensor_tensor(xnc, xnc, ps_bc2[:], ALU.add)
                ctx.inc(li, "xn_ready")
            block.vector(_)

            def _(sy, s0=s0):
                ctx.wait(sy, "xn_ready")
                dst = O_xn.rearrange("(c p) t -> p c t", p=128)[:, :, s0:s0 + TT]
                ctx.inc(sy.dma_start(out=dst, in_=xn_tt[:].rearrange("p (c t) -> p c t", c=C8)), "xnst", 16)
            block.sync(_)

            def _(s):
                ctx.wait(s, "xn_ready")
                ctx.inc(s.activation(sq[:], xn_tt[:], AF.Square, bias=0.0, scale=1.0), "sqk2")
            block.scalar(_)

            def _(t):
                ctx.wait(t, "sqk2")
                mi = None
                for c in range(C8):
                    mi = t.matmul(ps_r2[:], ones_sb[:, 0:1], sq[:, c * TT:(c + 1) * TT],
                                  start=(c == 0), stop=(c == C8 - 1))
                ctx.inc(mi, "mm_rms")
            block.tensor(_)

            def _(s):
                ctx.wait(s, "mm_rms")
                ctx.inc(s.activation(rows[:, 2 * TT:3 * TT], ps_r2[:], AF.Sqrt, bias=cst[0:1, 1:2], scale=1.0 / D), "rms_std")
            block.scalar(_)

            def _(v):
                ctx.wait(v, "rms_std")
                ctx.inc(v.reciprocal(rows[:, 2 * TT:3 * TT], rows[:, 2 * TT:3 * TT]), "rms_ready")
            block.vector(_)

            def _(t):
                ctx.wait(t, "rms_ready")
                ctx.inc(t.matmul(ps_bc1[:], ones_sb[0:1, 0:128], rows[0:1, 2 * TT:3 * TT], start=True, stop=True), "rmsbc_ready")
            block.tensor(_)

            def _(v):
                ctx.wait(v, "rmsbc_ready")
                ctx.wait(v, "cld")
                li = None
                for c in range(C8):
                    li = v.scalar_tensor_tensor(xn2[:, c * TT:(c + 1) * TT],
                                                xn_tt[:, c * TT:(c + 1) * TT],
                                                ln1_sb[:, c:c + 1], ps_bc1[:],
                                                ALU.mult, ALU.mult)
                ctx.inc(li, "xn2_ready")
            block.vector(_)

            def _(v):
                ctx.wait(v, "xn_ready")
                ctx.wait(v, "xtp_mm")
                ctx.inc(v.tensor_copy(xn_bt[:], xn_tt[:]), "xbt_ready")
            block.vector(_)

            ctx.sem("xtp_mm"), ctx.sem("xtp_ev")
            xt_base = ctx.cnt["xtp_mm"]
            xe_base = ctx.cnt["xtp_ev"]

            def _(t, xt_base=xt_base, xe_base=xe_base):
                ctx.wait(t, "xbt_ready")
                ctx.wait(t, "vtp_ev")
                ctx.wait(t, "xtmst")
                n = 0
                for tc in range(TT // 128):
                    for c in range(C8):
                        if xe_base + n > 0:
                            t.wait_ge(ctx.sem("xtp_ev"), xe_base + n)
                        ctx.inc(t.transpose(ps_t1[:], xn_bt[:, c * TT + tc * 128: c * TT + tc * 128 + 128],
                                            eye_sb[:]), "xtp_mm")
                        n += 1
            block.tensor(_)

            def _(v, xt_base=xt_base):
                n = 0
                for tc in range(TT // 128):
                    for c in range(C8):
                        n += 1
                        v.wait_ge(ctx.sem("xtp_mm"), xt_base + n)
                        ctx.inc(v.tensor_copy(xtm_tt[:, tc * 1024 + c * 128: tc * 1024 + c * 128 + 128],
                                              ps_t1[:]), "xtp_ev")
            block.vector(_)

            def _(sy, s0=s0):
                ctx.wait(sy, "xtp_ev")
                dst = O_xtm[s0:s0 + TT, :].rearrange("(tc p) f -> p tc f", p=128)
                src = xtm_tt[:].rearrange("p (tc f) -> p tc f", tc=TT // 128)
                ctx.inc(sy.dma_start(out=dst, in_=src), "xtmst", 16)
            block.sync(_)

            qk_list = [] if upto == "xn" else [("q", wq_sb, O_qh), ("k", wk_sb, O_kh)]
            for nm, w_sb_, out_dram in qk_list:
                def evac(v, c, pb):
                    return v.tensor_copy(praw[:, c * TT:(c + 1) * TT], pb[:, 0:TT])
                emit_proj(ctx, "p" + nm, w_sb_, xn2, (0, TT), evac,
                          xw=TT, pbanks=(ps_a, ps_b),
                          wait_before=("xn2_ready", "wld", "qh_ev"),
                          wait_before_ev=("sqk3",))

                def _(s, nm=nm):
                    ctx.wait(s, "p" + nm + "_ev")
                    ctx.inc(s.activation(sq[:], praw[:], AF.Square, bias=0.0, scale=1.0), "sqk3")
                block.scalar(_)

                def _(t, nm=nm):
                    ctx.wait(t, "sqk3")
                    ctx.wait(t, "qh_ev")
                    mi = None
                    for c in range(C8):
                        mi = t.matmul(ps_ss[:], sel_sb[:, c * NH:(c + 1) * NH],
                                      sq[:, c * TT:(c + 1) * TT],
                                      start=(c == 0), stop=(c == C8 - 1))
                    ctx.inc(mi, "ss_mm")
                block.tensor(_)

                def _(s, nm=nm):
                    ctx.wait(s, "ss_mm")
                    ctx.inc(s.activation(rs16[:], ps_ss[:], AF.Sqrt, bias=cst[0:16, 1:2], scale=1.0), "ss_sqrt")
                block.scalar(_)

                def _(v, nm=nm):
                    ctx.wait(v, "ss_sqrt")
                    ctx.inc(v.reciprocal(rs16[:], rs16[:]), "rs_ready")
                block.vector(_)

                ctx.sem("rsbc_mm"), ctx.sem("qh_ev")
                rb_base = ctx.cnt["rsbc_mm"]
                qe_base = ctx.cnt["qh_ev"]

                def _(t, rb_base=rb_base, qe_base=qe_base):
                    ctx.wait(t, "rs_ready")
                    for c in range(C8):
                        if c >= 2:
                            t.wait_ge(ctx.sem("qh_ev"), qe_base + c - 1)
                        pb = ps_bc1 if c % 2 == 0 else ps_bc2
                        ctx.inc(t.matmul(pb[:], selT_sb[:, c * 128:(c + 1) * 128], rs16[:],
                                         start=True, stop=True), "rsbc_mm")
                block.tensor(_)

                def _(v, rb_base=rb_base):
                    ctx.wait(v, "qhst")
                    for c in range(C8):
                        v.wait_ge(ctx.sem("rsbc_mm"), rb_base + c + 1)
                        pb = ps_bc1 if c % 2 == 0 else ps_bc2
                        li = v.tensor_tensor(ob_tt[:, c * TT:(c + 1) * TT],
                                             praw[:, c * TT:(c + 1) * TT], pb[:], ALU.mult)
                        ctx.inc(li, "qh_ev")
                block.vector(_)

                def _(sy, s0=s0, out_dram=out_dram, nm=nm):
                    ctx.wait(sy, "qh_ev")
                    if F is not None and nm == "k":
                        dst = F.io[f"kh_t{s0 // TT}"].rearrange("(c p) q -> p c q", p=128)
                    else:
                        dst = out_dram.rearrange("(c p) t -> p c t", p=128)[:, :, s0:s0 + TT]
                    ctx.inc(sy.dma_start(out=dst, in_=ob_tt[:].rearrange("p (c t) -> p c t", c=C8)), "qhst", 16)
                block.sync(_)

            if upto in ("xn", "qk"):
                continue
            def vevac(v, c, pb):
                v.tensor_copy(ob_tt[:, c * TT:(c + 1) * TT], pb[:, 0:TT])
                ctx.chain(v, v.tensor_reduce(vtmp[:, c:c + 1], pb[:, 0:TT], axis=AX, op=ALU.add))
                return v.tensor_tensor(vsum_sb[:, c:c + 1], vsum_sb[:, c:c + 1], vtmp[:, c:c + 1], ALU.add)
            emit_proj(ctx, "pv", wv_sb, xn2, (0, TT), vevac,
                      xw=TT, pbanks=(ps_a, ps_b),
                      wait_before=("xn2_ready", "wld", "qhst"),
                      wait_before_ev=("qhst",))

            ctx.sem("vtp_mm"), ctx.sem("vtp_ev")
            tp_base = ctx.cnt["vtp_mm"]
            te_base = ctx.cnt["vtp_ev"]

            def _(t, tp_base=tp_base, te_base=te_base):
                ctx.wait(t, "pv_ev")
                ctx.wait(t, "vtst")
                ctx.wait(t, "xtp_ev")
                n = 0
                for tc in range(TT // 128):
                    for c in range(C8):
                        if n >= 1:
                            t.wait_ge(ctx.sem("vtp_ev"), te_base + n)
                        ctx.inc(t.transpose(ps_t1[:], ob_tt[:, c * TT + tc * 128: c * TT + tc * 128 + 128], eye_sb[:]), "vtp_mm")
                        n += 1
            block.tensor(_)

            def _(v, tp_base=tp_base):
                n = 0
                for tc in range(TT // 128):
                    base = tc * NH * 65
                    for c in range(C8):
                        n += 1
                        v.wait_ge(ctx.sem("vtp_mm"), tp_base + n)
                        h0, h1 = 2 * c, 2 * c + 1
                        v.tensor_copy(vt_tt[:, base + h0 * 65: base + h0 * 65 + 64], ps_t1[:, 0:64])
                        ctx.inc(v.tensor_copy(vt_tt[:, base + h1 * 65: base + h1 * 65 + 64], ps_t1[:, 64:128]), "vtp_ev")
                    li = v.tensor_copy(
                        vt_tt[:].rearrange("p (tc h x) -> p tc h x", tc=TT // 128, h=NH)[:, tc, :, 64:65],
                        ones_sb[:, 0:NH].rearrange("p (h x) -> p h x", x=1))
                ctx.inc(li, "vt_ready")
            block.vector(_)

            def _(sy, s0=s0):
                ctx.wait(sy, "vt_ready")
                if F is not None:
                    dst = F.io[f"vt_t{s0 // TT}"].rearrange("(tc p) f -> p tc f", p=128)
                else:
                    dst = O_vt[s0:s0 + TT, :].rearrange("(tc p) f -> p tc f", p=128)
                src = vt_tt[:].rearrange("p (tc f) -> p tc f", tc=TT // 128)
                ctx.inc(sy.dma_start(out=dst, in_=src), "vtst", 16)
            block.sync(_)
            if F is not None:
                emit_exchange_tile(ctx, block, F, "e0", tt, "cc0", "khs", "vtms",
                                   ctx.cnt["qhst"], ctx.cnt["vtst"])

        def _(sy):
            ctx.wait(sy, "vt_ready")
            if F is not None:
                for c in range(C8):
                    ctx.inc(sy.dma_start(out=bass.AP(tensor=O_vs.tensor, offset=O_vs.offset + c * 128,
                                                     ap=[[1, 128], [1, 1]]),
                                         in_=vsum_sb[:, c:c + 1]), "vsst", 16)
            else:
                ctx.inc(sy.dma_start(out=O_vs, in_=vsum_sb[:]), "vsst", 16)
            ctx.wait(sy, "xnst")
            ctx.wait(sy, "qhst")
            ctx.wait(sy, "vtst")
            ctx.wait(sy, "vsst")
        block.sync(_)

    fin()
    return nc



def build_launch2(F=None):
    nc, es, block, ctx, dt, fin = stage_begin(F, None)
    I_xn = dt("xn", [D, TPC], F32, "ExternalInput")
    I_qh = dt("qh", [D, TPC], BF16, "ExternalInput")
    I_khs = dt("khs", [D, NSLOT], BF16, "ExternalInput")
    I_vtm = dt("vtms", [NSLOT, NH * 65], BF16, "ExternalInput")
    I_vs2 = dt("vsum2", [128, 16], F32, "ExternalInput")
    I_sclr = dt("scaler", [1, 16], F32, "ExternalInput")
    I_sclc = dt("scalec", [16, 1], F32, "ExternalInput")
    I_relb = dt("relb", [32, 16], F32, "ExternalInput")
    I_oh = dt("OH", [32, 112], F32, "ExternalInput")
    I_j = dt("J128", [128, 128], F32, "ExternalInput")
    I_eyef = dt("eyef", [128, 128], F32, "ExternalInput")
    I_alpha = dt("alpha", [128, 1], F32, "ExternalInput")
    I_beta = dt("beta", [128, 1], F32, "ExternalInput")
    I_wo = dt("Wo", [D, D], F32, "ExternalInput")
    O_y = dt("y", [D, TPC], F32, "ExternalOutput")
    gdram = dt("gscratch", [16, 384], F32, "Internal")

    with es:
        kh_g = sb(ctx, "kh_g", 128, 4 * NSLOT, BF16)
        qh_g = sb(ctx, "qh_g", 128, 4 * TPC, BF16)
        vt_g = sb(ctx, "vt_g", 128, 32 * 520, BF16)
        ao = sb(ctx, "ao", 128, C8 * TPC, BF16)
        WT = sb(ctx, "WT", 128, NH * 256)
        Pt = sb(ctx, "Pt", 128, 2 * 512, BF16)
        rr = sb(ctx, "rr", 128, 512)
        rb_sb = sb(ctx, "rb_sb", 64, 512)
        wo_sb = sb(ctx, "wo_sb", 128, C8 * D, BF16)
        xnt = sb(ctx, "xnt", 128, C8 * TT)
        yst = sb(ctx, "yst", 128, C8 * TT)
        ones_sb = sb(ctx, "ones_sb", 128, 128)
        scl_row = sb(ctx, "scl_row", 1, 16)
        scl_col = sb(ctx, "scl_col", 16, 4)
        relb_sb = sb(ctx, "relb_sb", 32, 16)
        oh_sb = sb(ctx, "oh_sb", 32, 112)
        j_sb = sb(ctx, "j_sb", 128, 128)
        eyef_sb = sb(ctx, "eyef_sb", 128, 128)
        bandsb = sb(ctx, "bandsb", 112, 16)
        G_sb = sb(ctx, "G_sb", 16, 384)
        rowtmp = sb(ctx, "rowtmp", 1, 16)
        SCL = sb(ctx, "SCL", 128, 16)
        MT = sb(ctx, "MT", 128, 16)
        vs2 = sb(ctx, "vs2", 128, 16)
        vmb = sb(ctx, "vmb", 128, 8)
        ab_sb = sb(ctx, "ab_sb", 128, 2)
        wflip = sb(ctx, "wflip", 128, 256)

        psS_a = psum(ctx, "psS_a", 128, 512)
        psS_b = psum(ctx, "psS_b", 128, 512)
        O_a = psum(ctx, "O_a", 65, 512)
        O_b = psum(ctx, "O_b", 65, 512)
        O_c = psum(ctx, "O_c", 65, 512)
        O_d = psum(ctx, "O_d", 65, 512)
        psB = psum(ctx, "psB", 64, 512)
        ps_m = psum(ctx, "ps_m", 128, 512)

        def _(sy):
            if F is not None:
                sy.wait_ge(ctx.sem("e0asm"), 32)
            ctx.inc(sy.dma_start(out=scl_row[:], in_=I_sclr), "cld", 16)
            ctx.inc(sy.dma_start(out=scl_col[:, 0:1], in_=I_sclc), "cld", 16)
            ctx.inc(sy.dma_start(out=relb_sb[:], in_=I_relb), "cld", 16)
            ctx.inc(sy.dma_start(out=oh_sb[:], in_=I_oh), "cld", 16)
            ctx.inc(sy.dma_start(out=j_sb[:], in_=I_j), "cld", 16)
            ctx.inc(sy.dma_start(out=eyef_sb[:], in_=I_eyef), "cld", 16)
            if F is None:
                ctx.inc(sy.dma_start(out=vs2[:], in_=I_vs2), "cld", 16)
            ctx.inc(sy.dma_start(out=ab_sb[:, 0:1], in_=I_alpha), "cld", 16)
            ctx.inc(sy.dma_start(out=ab_sb[:, 1:2], in_=I_beta), "cld", 16)
        block.sync(_)

        def _(g):
            if F is not None:
                g.wait_ge(ctx.sem("e0asm"), 32)
            emit_weight_load(ctx, g, I_wo, wo_sb, sem="wld")
        block.gpsimd(_)

        if F is not None:
            def _(g):
                ctx.wait(g, "e0asm")
                for cc_ in range(16):
                    ctx.inc(g.dma_start(out=vs2[:, cc_:cc_ + 1],
                                        in_=bass.AP(tensor=I_vs2.tensor,
                                                    offset=I_vs2.offset + cc_ * 128,
                                                    ap=[[1, 128], [1, 1]])), "vs2ld", 16)
            block.gpsimd(_)

        def _(v):
            if F is not None:
                v.wait_ge(ctx.sem("e0asm"), 32)
            v.memset(ones_sb[:], 1.0)
            ctx.inc(v.memset(G_sb[:], 0.0), "vinit")
        block.vector(_)

        def _(t):
            ctx.wait(t, "cld")
            ctx.wait(t, "vinit")
            ctx.inc(t.matmul(ps_m[0:112, 0:16], oh_sb[:], relb_sb[:], start=True, stop=True), "mm_band")
        block.tensor(_)

        def _(v):
            ctx.wait(v, "mm_band")
            ctx.inc(v.tensor_copy(bandsb[:], ps_m[0:112, 0:16]), "band_sb")
        block.vector(_)

        def _(t):
            ctx.wait(t, "band_sb")
            ctx.inc(t.transpose(O_a[0:16, 0:112], bandsb[:], eyef_sb[0:112, 0:112]), "mm_bandT")
        block.tensor(_)

        def _(s):
            ctx.wait(s, "cld")
            ctx.chain(s, s.activation(scl_col[:, 1:2], scl_col[:, 0:1], AF.Sign, bias=0.0, scale=1.0))
            ctx.inc(s.activation(scl_col[:, 2:3], scl_col[:, 0:1], AF.Abs, bias=0.0, scale=1.0), "sc_abs")
        block.scalar(_)

        def _(v):
            ctx.wait(v, "mm_bandT")
            ctx.wait(v, "sc_abs")
            ctx.chain(v, v.tensor_copy(G_sb[:, 128:240], O_a[0:16, 0:112]))
            ctx.chain(v, v.tensor_single_scalar(scl_col[:, 1:2], scl_col[:, 1:2], -BIG, ALU.mult))
            ctx.chain(v, v.tensor_scalar(G_sb[:, 0:128], ones_sb[0:16, 0:128], scl_col[:, 1:2], None, ALU.mult))
            ctx.chain(v, v.tensor_reduce(scl_col[:, 3:4], G_sb[:, 128:240], axis=AX, op=ALU.max,
                                         apply_absolute_value=True))
            ctx.chain(v, v.tensor_single_scalar(scl_col[:, 3:4], scl_col[:, 3:4], 1.0, ALU.add))
            ctx.chain(v, v.tensor_tensor(scl_col[:, 3:4], scl_col[:, 3:4], scl_col[:, 2:3], ALU.mult))
            ctx.inc(v.tensor_single_scalar(scl_col[:, 3:4], scl_col[:, 3:4], -1.0, ALU.mult), "negm")
        block.vector(_)

        def _(t):
            ctx.wait(t, "negm")
            ctx.inc(t.transpose(O_b[0:1, 0:16], scl_col[:, 3:4], eyef_sb[0:16, 0:16]), "mm_negmT")
        block.tensor(_)

        def _(v):
            ctx.wait(v, "mm_negmT")
            ctx.inc(v.tensor_copy(rowtmp[:], O_b[0:1, 0:16]), "rowtmp_sb")
        block.vector(_)

        def _(t):
            ctx.wait(t, "rowtmp_sb")
            ctx.inc(t.matmul(ps_m[:, 0:16], ones_sb[0:1, 0:128], rowtmp[:], start=True, stop=True), "mm_bc1")
            ctx.inc(t.matmul(ps_m[:, 16:32], ones_sb[0:1, 0:128], scl_row[:], start=True, stop=True), "mm_bc2")
        block.tensor(_)

        def _(v):
            ctx.wait(v, "mm_bc2")
            ctx.wait(v, "vs2ld")
            ctx.inc(v.tensor_copy(MT[:], ps_m[:, 0:16]), "mtscl")
            ctx.inc(v.tensor_copy(SCL[:], ps_m[:, 16:32]), "mtscl")
            ctx.chain(v, v.tensor_tensor(vmb[:], vs2[:, 0:8], vs2[:, 8:16], ALU.add))
            ctx.chain(v, v.tensor_scalar(vmb[:], vmb[:], ab_sb[:, 1:2], None, ALU.mult))
            ctx.inc(v.tensor_single_scalar(vmb[:], vmb[:], 1.0 / NSLOT, ALU.mult), "setup_v")
        block.vector(_)

        def _(sy):
            ctx.wait(sy, "negm")
            ctx.inc(sy.dma_start(out=gdram, in_=G_sb[:]), "gst", 16)
        block.sync(_)

        for h in range(NH):
            def _(sy, h=h):
                ctx.wait(sy, "gst")
                ctx.wait(sy, "wt_done")
                src = bass.AP(tensor=gdram.tensor, offset=h * 384, ap=[[1, 128], [1, 256]])
                ctx.inc(sy.dma_start(out=wflip[:], in_=src), "hank", 16)
            block.sync(_)

            def _(t, h=h):
                ctx.wait(t, "hank")
                ctx.wait(t, "wt_done")
                ctx.wait(t, "mtscl")
                ctx.inc(t.matmul(ps_m[:, 0:256], j_sb[:], wflip[:], start=True, stop=True), "mm_wt")
            block.tensor(_)

            def _(v, h=h):
                ctx.wait(v, "mm_wt")
                ctx.inc(v.tensor_copy(WT[:, h * 256:(h + 1) * 256], ps_m[:, 0:256]), "wt_done")
            block.vector(_)

        ctx.sem("smm"), ctx.sem("wadd"), ctx.sem("pexp"), ctx.sem("ppv"), ctx.sem("rb_used")

        rb_stage_b = ctx.cnt["rb_used"] if "rb_used" in ctx.cnt else 0
        ctx.sem("rb_used")
        rb_stage_b = ctx.cnt["rb_used"]
        def emit_epilogue2(ent):
            Ob_, h_, j_, pr_, ppv_end = ent

            rec_b = ctx.cnt_of("recip")

            def _(v):
                v.wait_ge(ctx.sem("ppv"), ppv_end)
                ctx.chain(v, v.tensor_single_scalar(rr[64:65, :], Ob_[64:65, :], 1e-30, ALU.add))
                ctx.inc(v.reciprocal(rr[64:65, :], rr[64:65, :]), "recip")
            block.vector(_)

            rbu_b = ctx.cnt_of("rb_used")

            def _(t):
                t.wait_ge(ctx.sem("recip"), rec_b + 1)
                if rbu_b > 0:
                    t.wait_ge(ctx.sem("rb_used"), rbu_b)
                ctx.inc(t.matmul(psB[:], ones_sb[64:65, 0:64], rr[64:65, :],
                                 start=True, stop=True), "mm_rb")
            block.tensor(_)

            mrb_b = ctx.cnt_of("mm_rb")

            def _(v):
                v.wait_ge(ctx.sem("mm_rb"), mrb_b)
                ctx.chain(v, v.tensor_copy(rb_sb[:], psB[:]))
                dst = ao[pr_:pr_ + 64, (h_ // 2) * TPC + j_ * 512: (h_ // 2) * TPC + j_ * 512 + 512]
                ctx.inc(v.tensor_tensor(dst, Ob_[0:64, :], rb_sb[:], ALU.mult), "rb_used")
            block.vector(_)

        NG = 2
        for g in range(NG):
            pend = []

            def _(sy, g=g):
                ctx.wait(sy, "rb_used")
                if F is not None:
                    sy.wait_ge(ctx.sem("e0asm"), 160)
                    sy.wait_ge(ctx.sem("e0asm2"), 128)
                src = I_khs.rearrange("(c p) s -> p c s", p=128)[:, 4 * g:4 * g + 4, :]
                ctx.inc(sy.dma_start(out=kh_g[:].rearrange("p (c s) -> p c s", c=4), in_=src), "gld", 16)
                srcq = I_qh.rearrange("(c p) s -> p c s", p=128)[:, 4 * g:4 * g + 4, :]
                ctx.inc(sy.dma_start(out=qh_g[:].rearrange("p (c s) -> p c s", c=4), in_=srcq), "gld", 16)
                srcv = I_vtm[:, g * 520:(g + 1) * 520].rearrange("(kc p) f -> p kc f", p=128)
                ctx.inc(sy.dma_start(out=vt_g[:].rearrange("p (kc f) -> p kc f", kc=32), in_=srcv), "gld", 16)
            block.sync(_)

            for hl in range(8):
                h = 8 * g + hl
                cl = hl // 2
                pr = (hl % 2) * 64
                for j in range(4):
                    G0 = 512 * (2 * j + 1)
                    nkc = 8 * j + 8
                    it = g * 32 + hl * 4 + j
                    Ob = (O_a, O_b, O_c, O_d)[it % 4]
                    smm_b = ctx.cnt["smm"]
                    wadd_b = ctx.cnt["wadd"]
                    pexp_b = ctx.cnt["pexp"]
                    ppv_b = ctx.cnt["ppv"]
                    geo = []
                    for kc in range(nkc):
                        c0 = 128 * kc - G0
                        geo.append((c0, max(0, c0), c0 >= -128))
                    nw = [0]
                    for kc in range(nkc):
                        nw.append(nw[-1] + (1 if geo[kc][2] else 0))

                    if len(pend) == 2:
                        emit_epilogue2(pend.pop(0))

                    def _(t, g=g, hl=hl, cl=cl, pr=pr, j=j, nkc=nkc, Ob=Ob, it=it,
                          smm_b=smm_b, pexp_b=pexp_b, ppv_b=ppv_b, geo=geo):
                        ctx.wait(t, "gld")
                        ctx.wait(t, "wt_done")
                        if pexp_b > 0:
                            t.wait_ge(ctx.sem("pexp"), pexp_b)
                        if it >= 4:
                            t.wait_ge(ctx.sem("rb_used"), rb_stage_b + it - 3)
                        for kc in range(nkc):
                            c0, qq0, has_w = geo[kc]
                            pb = psS_a if kc % 2 == 0 else psS_b
                            if kc >= 2:
                                t.wait_ge(ctx.sem("pexp"), pexp_b + kc - 1)
                            mi = t.matmul(pb[:, qq0:512],
                                          kh_g[pr:pr + 64, cl * NSLOT + 128 * kc: cl * NSLOT + 128 * kc + 128],
                                          qh_g[pr:pr + 64, cl * TPC + j * 512 + qq0: cl * TPC + (j + 1) * 512],
                                          start=True, stop=True)
                            ctx.inc(mi, "smm")
                            if kc >= 1:
                                pkc = kc - 1
                                qq0p = geo[pkc][1]
                                t.wait_ge(ctx.sem("pexp"), pexp_b + pkc + 1)
                                mi = t.matmul(Ob[:, qq0p:512],
                                              vt_g[:, pkc * 520 + hl * 65: pkc * 520 + hl * 65 + 65],
                                              Pt[:, (pkc % 2) * 512 + qq0p: (pkc % 2) * 512 + 512],
                                              start=(pkc == 0), stop=False)
                                ctx.inc(mi, "ppv")
                        kc = nkc - 1
                        qq0p = geo[kc][1]
                        t.wait_ge(ctx.sem("pexp"), pexp_b + kc + 1)
                        mi = t.matmul(Ob[:, qq0p:512],
                                      vt_g[:, kc * 520 + hl * 65: kc * 520 + hl * 65 + 65],
                                      Pt[:, (kc % 2) * 512 + qq0p: (kc % 2) * 512 + 512],
                                      start=(kc == 0), stop=True)
                        ctx.inc(mi, "ppv")
                    block.tensor(_)

                    def _(v, h=h, nkc=nkc, smm_b=smm_b, geo=geo):
                        for kc in range(nkc):
                            c0, qq0, has_w = geo[kc]
                            if not has_w:
                                continue
                            v.wait_ge(ctx.sem("smm"), smm_b + kc + 1)
                            pb = psS_a if kc % 2 == 0 else psS_b
                            u0 = max(0, -c0)
                            q0 = max(0, c0)
                            wdt = min(256 - u0, 512 - q0)
                            li = v.tensor_tensor(pb[:, q0:q0 + wdt], pb[:, q0:q0 + wdt],
                                                 WT[:, h * 256 + u0: h * 256 + u0 + wdt], ALU.add)
                            ctx.inc(li, "wadd")
                    block.vector(_)

                    def _(s, h=h, nkc=nkc, smm_b=smm_b, wadd_b=wadd_b,
                          pexp_b=pexp_b, ppv_b=ppv_b, geo=geo, nw=nw):
                        ctx.wait(s, "mtscl")
                        if ppv_b > 0:
                            s.wait_ge(ctx.sem("ppv"), ppv_b)
                        for kc in range(nkc):
                            c0, qq0, has_w = geo[kc]
                            pb = psS_a if kc % 2 == 0 else psS_b
                            if has_w:
                                s.wait_ge(ctx.sem("wadd"), wadd_b + nw[kc + 1])
                            else:
                                s.wait_ge(ctx.sem("smm"), smm_b + kc + 1)
                            if kc >= 2:
                                s.wait_ge(ctx.sem("ppv"), ppv_b + kc - 1)
                            li = s.activation(Pt[:, (kc % 2) * 512 + qq0: (kc % 2) * 512 + 512],
                                              pb[:, qq0:512], AF.Exp,
                                              bias=MT[:, h:h + 1], scale=SCL[:, h:h + 1])
                            ctx.inc(li, "pexp")
                    block.scalar(_)

                    pend.append((Ob, h, j, pr, ctx.cnt["ppv"]))

            while pend:
                emit_epilogue2(pend.pop(0))

        def _(v):
            ctx.wait(v, "rb_used")
            ctx.wait(v, "setup_v")
            li = None
            for c in range(C8):
                li = v.scalar_tensor_tensor(ao[:, c * TPC: c * TPC + 1],
                                            ao[:, c * TPC: c * TPC + 1],
                                            ab_sb[:, 0:1], vmb[:, c:c + 1],
                                            ALU.mult, ALU.add)
            ctx.inc(li, "patch_done")
        block.vector(_)

        for tt in range(NTT):
            s0 = tt * TT

            def _(sy, s0=s0):
                ctx.wait(sy, "wo_ev")
                src = I_xn.rearrange("(c p) t -> p c t", p=128)[:, :, s0:s0 + TT]
                ctx.inc(sy.dma_start(out=xnt[:].rearrange("p (c t) -> p c t", c=C8), in_=src), "xnld", 16)
            block.sync(_)

            def evac(v, c, pb, s0=s0):
                return v.tensor_tensor(yst[:, c * TT:(c + 1) * TT], pb[:, 0:TT],
                                       xnt[:, c * TT:(c + 1) * TT], ALU.add)
            emit_proj(ctx, "wo", wo_sb, ao, (s0, TT), evac,
                      xw=TPC, pbanks=(psS_a, psS_b),
                      wait_before=("patch_done", "wld", "pexp", "xnld"),
                      wait_before_ev=("xnld", "yst_st"))

            def _(sy, s0=s0):
                ctx.wait(sy, "wo_ev")
                dst = O_y.rearrange("(c p) t -> p c t", p=128)[:, :, s0:s0 + TT]
                ctx.inc(sy.dma_start(out=dst, in_=yst[:].rearrange("p (c t) -> p c t", c=C8)), "yst_st", 16)
            block.sync(_)

        def _(sy):
            ctx.wait(sy, "yst_st")
        block.sync(_)

    return nc



_DBG3 = False


def build_launch3(F=None):
    nc, es, block, ctx, dt, fin = stage_begin(F, None)
    I_xT = dt("xT", [D, TPC], F32, "ExternalInput")
    I_xn = dt("xn", [D, TPC], F32, "ExternalInput")
    I_y = dt("y", [D, TPC], F32, "ExternalInput")
    I_ln2 = dt("ln2w", [128, C8], F32, "ExternalInput")
    I_w1 = dt("W1", [D, 4 * D], F32, "ExternalInput")
    I_w2 = dt("W2", [4 * D, D], F32, "ExternalInput")
    I_gw = dt("gateT", [D, 8], F32, "ExternalInput")
    I_mw1 = dt("mw1T", [8 * D, D], F32, "ExternalInput")
    I_mw2 = dt("mw2T", [8 * D, D], F32, "ExternalInput")
    I_mw3 = dt("mw3T", [8 * D, D], F32, "ExternalInput")
    I_b1 = dt("B1", [128, 64], F32, "ExternalInput")
    I_b2 = dt("B2", [128, 64], F32, "ExternalInput")
    I_b3 = dt("B3", [128, 64], F32, "ExternalInput")
    I_sw1 = dt("sw1T", [D, 2 * D], F32, "ExternalInput")
    I_sw2 = dt("sw2T", [D, 2 * D], F32, "ExternalInput")
    I_sw3 = dt("sw3T", [2 * D, D], F32, "ExternalInput")
    I_eyef = dt("eyef", [128, 128], F32, "ExternalInput")
    I_iota8 = dt("iota8", [128, 8], F32, "ExternalInput")
    I_xtm = dt("xn_tm", [TPC, D], BF16, "ExternalInput")
    I_tri = dt("TRI", [128, 128], F32, "ExternalInput")
    I_eyebm = dt("eyebm", [128, 128], BF16, "ExternalInput")
    O_x1 = dt("x1", [D, TPC], F32, "ExternalOutput")
    sdram = dt("s_scr", [D, TPC], F32, "Internal")
    xbkt = dt("xbkt", [8 * 768, D], BF16, "Internal")
    prodbkt = dt("prodbkt", [8 * 768, D], BF16, "Internal")

    with es:
        rows = sb(ctx, "rows", 1, 2 * TT)
        ones_sb = sb(ctx, "ones_sb", 128, 128)
        ln2_sb = sb(ctx, "ln2_sb", 128, C8)
        cst = sb(ctx, "cst", 128, 2)
        eyef_sb = sb(ctx, "eyef_sb", 128, 128)
        iota8_sb = sb(ctx, "iota8_sb", 128, 8)

        psA = psum(ctx, "psA", 128, 512)
        psB_ = psum(ctx, "psB_", 128, 512)
        psC = psum(ctx, "psC", 128, 512)
        psD = psum(ctx, "psD", 128, 512)
        psE = psum(ctx, "psE", 128, 512)
        psF = psum(ctx, "psF", 128, 512)
        psG = psum(ctx, "psG", 128, 512)
        addru = sb(ctx, "addru", 128, 32, mybir.dt.uint32)
        eye_bs = sb(ctx, "eye_bs", 128, 128, BF16)

        def _(sy):
            ctx.inc(sy.dma_start(out=ln2_sb[:], in_=I_ln2), "cld", 16)
            ctx.inc(sy.dma_start(out=eyef_sb[:], in_=I_eyef), "cld", 16)
            ctx.inc(sy.dma_start(out=iota8_sb[:], in_=I_iota8), "cld", 16)
            ctx.inc(sy.dma_start(out=eye_bs[:], in_=I_eyebm), "cld", 16)
        block.sync(_)

        def _(v):
            v.memset(ones_sb[:], 1.0)
            v.memset(cst[:, 0:1], 1e-6)
            ctx.inc(v.memset(cst[:, 1:2], 0.0), "vinit")
        block.vector(_)

        zs = ExitStack()
        psR = zs.enter_context(nc.psum_tensor("psR", [1, 512], F32, allow_name_mangling=True))
        w1_sb = zs.enter_context(nc.sbuf_tensor("w1_sb", [128, C8 * 4 * D], BF16, allow_name_mangling=True))
        w2_sb = zs.enter_context(nc.sbuf_tensor("w2_sb", [128, 32 * D], BF16, allow_name_mangling=True))
        y_tt = zs.enter_context(nc.sbuf_tensor("y_tt", [128, C8 * TT], F32, allow_name_mangling=True))
        sq_z = zs.enter_context(nc.sbuf_tensor("sq_z", [128, C8 * TT], F32, allow_name_mangling=True))
        z2_tt = zs.enter_context(nc.sbuf_tensor("z2_tt", [128, C8 * TT], BF16, allow_name_mangling=True))
        h_tt = zs.enter_context(nc.sbuf_tensor("h_tt", [128, 32 * TT], BF16, allow_name_mangling=True))

        def _(g):
            emit_weight_load(ctx, g, I_w1, w1_sb, sem="wldz")
            emit_weight_load(ctx, g, I_w2, w2_sb, sem="wldz")
        block.gpsimd(_)

        ctx.sem("h_mm"), ctx.sem("h_act"), ctx.sem("smlp_mm"), ctx.sem("smlp_ev")
        for tt in range(NTT):
            s0 = tt * TT

            def _(sy, s0=s0):
                ctx.wait(sy, "sst")
                src = I_y.rearrange("(c p) t -> p c t", p=128)[:, :, s0:s0 + TT]
                ctx.inc(sy.dma_start(out=y_tt[:].rearrange("p (c t) -> p c t", c=C8), in_=src), "yld", 16)
            block.sync(_)

            def _(s):
                ctx.wait(s, "yld")
                ctx.wait(s, "zrms_mm")
                ctx.inc(s.activation(sq_z[:], y_tt[:], AF.Square, bias=0.0, scale=1.0), "zsq")
            block.scalar(_)

            def _(t):
                ctx.wait(t, "zsq")
                ctx.wait(t, "vinit")
                ctx.wait(t, "zstd")
                mi = None
                for c in range(C8):
                    mi = t.matmul(psR[:], ones_sb[:, 0:1], sq_z[:, c * TT:(c + 1) * TT],
                                  start=(c == 0), stop=(c == C8 - 1))
                ctx.inc(mi, "zrms_mm")
            block.tensor(_)

            def _(s):
                ctx.wait(s, "zrms_mm")
                ctx.inc(s.activation(rows[:, 0:TT], psR[:], AF.Sqrt, bias=cst[0:1, 0:1], scale=1.0 / D), "zstd")
            block.scalar(_)

            def _(v):
                ctx.wait(v, "zstd")
                ctx.inc(v.reciprocal(rows[:, 0:TT], rows[:, 0:TT]), "zrstd")
            block.vector(_)

            def _(t):
                ctx.wait(t, "zrstd")
                ctx.wait(t, "z2_done")
                ctx.inc(t.matmul(psE[:], ones_sb[0:1, 0:128], rows[0:1, 0:TT], start=True, stop=True), "zbc")
            block.tensor(_)

            def _(v):
                ctx.wait(v, "zbc")
                ctx.wait(v, "cld")
                li = None
                for c in range(C8):
                    li = v.scalar_tensor_tensor(z2_tt[:, c * TT:(c + 1) * TT],
                                                y_tt[:, c * TT:(c + 1) * TT],
                                                ln2_sb[:, c:c + 1], psE[:],
                                                ALU.mult, ALU.mult)
                ctx.inc(li, "z2_done")
            block.vector(_)

            hmm_b = ctx.cnt["h_mm"]
            hact_b = ctx.cnt["h_act"]

            def _(t, hmm_b=hmm_b, hact_b=hact_b):
                ctx.wait(t, "z2_done")
                ctx.wait(t, "wldz")
                ctx.wait(t, "smlp_mm")
                for hc in range(32):
                    if hact_b + hc - 1 > 0:
                        t.wait_ge(ctx.sem("h_act"), hact_b + hc - 1)
                    pb = psA if hc % 2 == 0 else psB_
                    mi = None
                    for kc in range(C8):
                        mi = t.matmul(pb[:, 0:TT],
                                      w1_sb[:, kc * 4 * D + hc * 128: kc * 4 * D + hc * 128 + 128],
                                      z2_tt[:, kc * TT:(kc + 1) * TT],
                                      start=(kc == 0), stop=(kc == C8 - 1))
                    ctx.inc(mi, "h_mm")
            block.tensor(_)

            def _(s, hmm_b=hmm_b):
                for hc in range(32):
                    s.wait_ge(ctx.sem("h_mm"), hmm_b + hc + 1)
                    pb = psA if hc % 2 == 0 else psB_
                    li = s.activation(h_tt[:, hc * TT:(hc + 1) * TT], pb[:, 0:TT],
                                      AF.Relu, bias=0.0, scale=1.0)
                    ctx.inc(li, "h_act")
            block.scalar(_)

            smm_b = ctx.cnt["smlp_mm"]
            sev_b = ctx.cnt["smlp_ev"]

            def _(t, smm_b=smm_b, sev_b=sev_b, hact_b=hact_b):
                t.wait_ge(ctx.sem("h_act"), hact_b + 32)
                for c in range(C8):
                    if sev_b + c - 1 > 0:
                        t.wait_ge(ctx.sem("smlp_ev"), sev_b + c - 1)
                    pb = psC if c % 2 == 0 else psD
                    mi = None
                    for hc in range(32):
                        mi = t.matmul(pb[:, 0:TT],
                                      w2_sb[:, hc * D + c * 128: hc * D + c * 128 + 128],
                                      h_tt[:, hc * TT:(hc + 1) * TT],
                                      start=(hc == 0), stop=(hc == 31))
                    ctx.inc(mi, "smlp_mm")
            block.tensor(_)

            def _(v, smm_b=smm_b):
                for c in range(C8):
                    v.wait_ge(ctx.sem("smlp_mm"), smm_b + c + 1)
                    pb = psC if c % 2 == 0 else psD
                    li = v.tensor_tensor(y_tt[:, c * TT:(c + 1) * TT], pb[:, 0:TT],
                                         y_tt[:, c * TT:(c + 1) * TT], ALU.add)
                    ctx.inc(li, "smlp_ev")
            block.vector(_)

            def _(sy, s0=s0):
                ctx.wait(sy, "smlp_ev")
                dst = sdram.rearrange("(c p) t -> p c t", p=128)[:, :, s0:s0 + TT]
                ctx.inc(sy.dma_start(out=dst, in_=y_tt[:].rearrange("p (c t) -> p c t", c=C8)), "sst", 16)
            block.sync(_)

        def _(sy):
            ctx.wait(sy, "sst")
        block.sync(_)
        zs.close()

        gacc = sb(ctx, "gacc", 128, C8 * TPC)

        CAP = 768
        gs = ExitStack()
        xn32 = gs.enter_context(nc.sbuf_tensor("xn32", [128, C8 * 128], F32, allow_name_mangling=True))
        gw_sb = gs.enter_context(nc.sbuf_tensor("gw_sb", [128, C8 * 8], F32, allow_name_mangling=True))
        lg = gs.enter_context(nc.sbuf_tensor("lg", [128, 8], F32, allow_name_mangling=True))
        stat = gs.enter_context(nc.sbuf_tensor("stat", [128, 4], F32, allow_name_mangling=True))
        mx8 = gs.enter_context(nc.sbuf_tensor("mx8", [128, 8], F32, allow_name_mangling=True))
        mi8 = gs.enter_context(nc.sbuf_tensor("mi8", [128, 8], mybir.dt.uint32, allow_name_mangling=True))
        mi8f = gs.enter_context(nc.sbuf_tensor("mi8f", [128, 8], F32, allow_name_mangling=True))
        oh2 = gs.enter_context(nc.sbuf_tensor("oh2", [128, 16], F32, allow_name_mangling=True))
        tmp8 = gs.enter_context(nc.sbuf_tensor("tmp8", [128, 8], F32, allow_name_mangling=True))
        posl = gs.enter_context(nc.sbuf_tensor("posl", [128, 8], F32, allow_name_mangling=True))
        carry = gs.enter_context(nc.sbuf_tensor("carry", [1, 16], F32, allow_name_mangling=True))
        tri_sb = gs.enter_context(nc.sbuf_tensor("tri_sb", [128, 128], F32, allow_name_mangling=True))
        xtm_sb = gs.enter_context(nc.sbuf_tensor("xtm_sb", [128, 1024], BF16, allow_name_mangling=True))
        xsc = gs.enter_context(nc.sbuf_tensor("xsc", [128, 2 * 1024], BF16, allow_name_mangling=True))

        def _(sy):
            ctx.wait(sy, "sst")
            ctx.wait(sy, "smlp_mm")
            src = I_gw.rearrange("(c p) e -> p c e", p=128)
            ctx.inc(sy.dma_start(out=gw_sb[:].rearrange("p (c e) -> p c e", c=C8), in_=src), "gwld", 16)
            ctx.inc(sy.dma_start(out=tri_sb[:], in_=I_tri), "gwld", 16)
        block.sync(_)

        def _(v):
            ctx.inc(v.memset(carry[:], 0.0), "gcar")
        block.vector(_)

        for tc in range(16):
            def _(sy, tc=tc):
                ctx.wait(sy, "glog_mm")
                ctx.wait(sy, "sst")
                ctx.wait(sy, "smlp_mm")
                src = I_xn.rearrange("(c p) t -> p c t", p=128)[:, :, tc * 128:(tc + 1) * 128]
                ctx.inc(sy.dma_start(out=xn32[:].rearrange("p (c t) -> p c t", c=C8), in_=src), "xn32ld", 16)
                ctx.wait(sy, "gsc1")
                ctx.inc(sy.dma_start(out=xtm_sb[:], in_=I_xtm[tc * 128:(tc + 1) * 128, :]), "xn32ld", 16)
            block.sync(_)

            def _(t, tc=tc):
                ctx.wait(t, "xn32ld")
                ctx.wait(t, "gwld")
                ctx.wait(t, "gsm")
                mi = None
                for c in range(C8):
                    mi = t.matmul(psG[0:128, 0:8], xn32[:, c * 128:(c + 1) * 128],
                                  gw_sb[:, c * 8:(c + 1) * 8],
                                  start=(c == 0), stop=(c == C8 - 1))
                ctx.inc(mi, "glog_mm")
            block.tensor(_)

            def _(v, tc=tc):
                ctx.wait(v, "glog_mm")
                ctx.inc(v.tensor_reduce(stat[:, 0:1], psG[0:128, 0:8], axis=AX, op=ALU.max, negate=True), "gmax")
            block.vector(_)

            def _(s, tc=tc):
                ctx.wait(s, "gmax")
                ctx.inc(s.activation(lg[:], psG[0:128, 0:8], AF.Exp,
                                     bias=stat[:, 0:1], scale=1.0, accum_out=stat[:, 1:2]), "gsm")
            block.scalar(_)

            def _(v, tc=tc):
                ctx.wait(v, "gsm")
                ctx.wait(v, "rt_mm")
                ctx.chain(v, v.reciprocal(stat[:, 1:2], stat[:, 1:2]))
                ctx.chain(v, v.tensor_scalar(lg[:], lg[:], stat[:, 1:2], None, ALU.mult))
                ctx.chain(v, v.max(mx8[:], lg[:]))
                ctx.chain(v, v.max_index(mi8[:], mx8[:], lg[:]))
                ctx.chain(v, v.tensor_copy(mi8f[:], mi8[:]))
                ctx.chain(v, v.tensor_scalar(oh2[:, 0:8], iota8_sb[:], mi8f[:, 0:1], None, ALU.is_equal))
                ctx.inc(v.tensor_scalar(oh2[:, 8:16], iota8_sb[:], mi8f[:, 1:2], None, ALU.is_equal), "goh")
            block.vector(_)

            def _(t, tc=tc):
                ctx.wait(t, "goh")
                ctx.wait(t, "gpos")
                ctx.wait(t, "gcarup")
                ctx.wait(t, "z2_done")
                t.matmul(psE[:, 0:16], tri_sb[:], oh2[:], start=True, stop=False)
                mi = t.matmul(psE[:, 0:16], ones_sb[0:1, 0:128], carry[:], start=False, stop=True)
                ctx.inc(mi, "rt_mm")
                ctx.inc(t.matmul(psF[0:1, 0:16], ones_sb[:, 0:1], oh2[:], start=True, stop=True), "rt_cs")
            block.tensor(_)

            def _(v, tc=tc):
                ctx.wait(v, "rt_mm")
                ctx.wait(v, "rt_cs")
                ctx.chain(v, v.tensor_tensor(tmp8[:], psE[:, 0:8], oh2[:, 0:8], ALU.mult))
                ctx.chain(v, v.tensor_reduce(posl[:, 0:1], tmp8[:], axis=AX, op=ALU.add))
                ctx.chain(v, v.tensor_tensor(tmp8[:], psE[:, 8:16], oh2[:, 8:16], ALU.mult))
                ctx.chain(v, v.tensor_reduce(posl[:, 1:2], tmp8[:], axis=AX, op=ALU.add))
                ctx.chain(v, v.tensor_single_scalar(posl[:, 2:3], mi8f[:, 0:1], float(CAP), ALU.mult))
                ctx.chain(v, v.tensor_tensor(posl[:, 3:4], posl[:, 2:3], posl[:, 0:1], ALU.add))
                ctx.chain(v, v.tensor_single_scalar(posl[:, 2:3], posl[:, 2:3], 383.0, ALU.add))
                ctx.chain(v, v.tensor_tensor(posl[:, 3:4], posl[:, 3:4], posl[:, 2:3], ALU.min))
                ctx.chain(v, v.tensor_copy(addru[:, 2 * tc: 2 * tc + 1], posl[:, 3:4]))
                ctx.chain(v, v.tensor_single_scalar(posl[:, 4:5], mi8f[:, 1:2], float(CAP), ALU.mult))
                ctx.chain(v, v.tensor_tensor(posl[:, 5:6], posl[:, 4:5], posl[:, 1:2], ALU.add))
                ctx.chain(v, v.tensor_single_scalar(posl[:, 5:6], posl[:, 5:6], 384.0, ALU.add))
                ctx.chain(v, v.tensor_single_scalar(posl[:, 4:5], posl[:, 4:5], 767.0, ALU.add))
                ctx.chain(v, v.tensor_tensor(posl[:, 5:6], posl[:, 5:6], posl[:, 4:5], ALU.min))
                ctx.chain(v, v.tensor_copy(addru[:, 2 * tc + 1: 2 * tc + 2], posl[:, 5:6]))
                ctx.inc(v.tensor_tensor(carry[:], carry[:], psF[0:1, 0:16], ALU.add), "gcarup")
                ctx.inc(v.tensor_copy(posl[:, 6:7], posl[:, 5:6]), "gpos")
                ctx.wait(v, "gsct")
                ctx.chain(v, v.tensor_scalar(xsc[:, 0:1024], xtm_sb[:], mx8[:, 0:1], None, ALU.mult))
                ctx.inc(v.tensor_scalar(xsc[:, 1024:2048], xtm_sb[:], mx8[:, 1:2], None, ALU.mult), "gsc1")
            block.vector(_)

            def _(g, tc=tc):
                ctx.wait(g, "gsc1")
                g.indirect_dma_start(
                    xbkt, bass.IndirectOffsetOnAxis(ap=addru[:, 2 * tc: 2 * tc + 1], axis=0),
                    xsc[:, 0:1024], None).then_inc(ctx.sem("gsct"), 16)
                g.indirect_dma_start(
                    xbkt, bass.IndirectOffsetOnAxis(ap=addru[:, 2 * tc + 1: 2 * tc + 2], axis=0),
                    xsc[:, 1024:2048], None).then_inc(ctx.sem("gsct"), 16)
                ctx.cnt[_SEM_ALIAS.get("gsct", "gsct")] += 32
            block.gpsimd(_)

        gs.close()

        if _DBG3:
            def _(sy):
                ctx.wait(sy, "gsct")
                o1 = nc.dram_tensor("dbg_addru", [128, 32], mybir.dt.uint32, kind="ExternalOutput").ap()
                ctx.inc(sy.dma_start(out=o1, in_=addru[:]), "dbgst", 16)
                o2 = nc.dram_tensor("dbg_xbkt", [8 * 768, D], BF16, kind="ExternalOutput").ap()
                ctx.inc(sy.dma_start(out=o2, in_=xbkt), "dbgst", 16)
            block.sync(_)

        ss_ = ExitStack()
        xn_bf = ss_.enter_context(nc.sbuf_tensor("xn_bf", [128, C8 * TPC], BF16, allow_name_mangling=True))
        swc = ss_.enter_context(nc.sbuf_tensor("swc", [128, C8 * 128 * 2], BF16, allow_name_mangling=True))
        zg = ss_.enter_context(nc.sbuf_tensor("zg", [128, 16 * TPC], BF16, allow_name_mangling=True))
        ztmp = ss_.enter_context(nc.sbuf_tensor("ztmp", [128, TPC], F32, allow_name_mangling=True))
        sw3c = ss_.enter_context(nc.sbuf_tensor("sw3c", [128, 16 * 128], BF16, allow_name_mangling=True))

        def _(g):
            ctx.wait(g, "sst")
            ctx.wait(g, "gcarup")
            ctx.wait(g, "gsct")
            src = I_xn.rearrange("(c p) t -> p c t", p=128)
            ctx.inc(g.dma_start(out=xn_bf[:].rearrange("p (c t) -> p c t", c=C8), in_=src), "xnbf", 16)
        block.gpsimd(_)

        ctx.sem("z1mm"), ctx.sem("z1act"), ctx.sem("z1ev"), ctx.sem("swld")
        for ic in range(16):
            def _(g, ic=ic):
                ctx.wait(g, "z1mm")
                ctx.wait(g, "gsct")
                ctx.wait(g, "gcarup")
                src1 = I_sw1[:, ic * 128:(ic + 1) * 128].rearrange("(c p) n -> p c n", p=128)
                ctx.inc(g.dma_start(out=swc[:, 0:C8 * 128].rearrange("p (c n) -> p c n", c=C8), in_=src1), "swld", 16)
                src2 = I_sw2[:, ic * 128:(ic + 1) * 128].rearrange("(c p) n -> p c n", p=128)
                ctx.inc(g.dma_start(out=swc[:, C8 * 128:].rearrange("p (c n) -> p c n", c=C8), in_=src2), "swld", 16)
            block.gpsimd(_)

            z1_b = ctx.cnt["z1mm"]
            z1a_b = ctx.cnt["z1act"]
            z1e_b = ctx.cnt["z1ev"]

            def _(t, ic=ic, z1_b=z1_b, z1a_b=z1a_b, z1e_b=z1e_b):
                ctx.wait(t, "swld")
                ctx.wait(t, "xnbf")
                ctx.wait(t, "h_act")
                ctx.wait(t, "smlp_ev")
                ctx.wait(t, "rt_mm")
                for tt in range(NTT):
                    if z1a_b + tt > 0:
                        t.wait_ge(ctx.sem("z1act"), z1a_b + tt)
                    if z1e_b + tt > 0:
                        t.wait_ge(ctx.sem("z1ev"), z1e_b + tt)
                    for w_ in range(2):
                        pb = psA if w_ == 0 else psB_
                        mi = None
                        for kc in range(C8):
                            mi = t.matmul(pb[:, 0:TT],
                                          swc[:, w_ * C8 * 128 + kc * 128: w_ * C8 * 128 + kc * 128 + 128],
                                          xn_bf[:, kc * TPC + tt * TT: kc * TPC + (tt + 1) * TT],
                                          start=(kc == 0), stop=(kc == C8 - 1))
                        ctx.inc(mi, "z1mm")
            block.tensor(_)

            def _(s, ic=ic, z1_b=z1_b):
                for tt in range(NTT):
                    s.wait_ge(ctx.sem("z1mm"), z1_b + tt * 2 + 1)
                    li = s.activation(ztmp[:, tt * TT:(tt + 1) * TT], psA[:, 0:TT],
                                      AF.Silu, bias=0.0, scale=1.0)
                    ctx.inc(li, "z1act")
            block.scalar(_)

            def _(v, ic=ic, z1_b=z1_b, z1a_b=z1a_b):
                for tt in range(NTT):
                    v.wait_ge(ctx.sem("z1act"), z1a_b + tt + 1)
                    v.wait_ge(ctx.sem("z1mm"), z1_b + tt * 2 + 2)
                    li = v.tensor_tensor(zg[:, ic * TPC + tt * TT: ic * TPC + (tt + 1) * TT],
                                         ztmp[:, tt * TT:(tt + 1) * TT], psB_[:, 0:TT], ALU.mult)
                    ctx.inc(li, "z1ev")
            block.vector(_)

        ctx.sem("s3mm"), ctx.sem("s3ev"), ctx.sem("s3ld")
        for c in range(C8):
            def _(g, c=c):
                ctx.wait(g, "s3mm")
                src = I_sw3[:, c * 128:(c + 1) * 128].rearrange("(ic p) n -> p ic n", p=128)
                ctx.inc(g.dma_start(out=sw3c[:].rearrange("p (ic n) -> p ic n", ic=16), in_=src), "s3ld", 16)
            block.gpsimd(_)

            s3_b = ctx.cnt["s3mm"]
            s3e_b = ctx.cnt["s3ev"]

            def _(t, c=c, s3_b=s3_b, s3e_b=s3e_b):
                ctx.wait(t, "s3ld")
                ctx.wait(t, "z1ev")
                ctx.wait(t, "smlp_ev")
                for tt in range(NTT):
                    if s3e_b + tt - 1 > 0:
                        t.wait_ge(ctx.sem("s3ev"), s3e_b + tt - 1)
                    pb = psC if tt % 2 == 0 else psD
                    mi = None
                    for ic in range(16):
                        mi = t.matmul(pb[:, 0:TT],
                                      sw3c[:, ic * 128:(ic + 1) * 128],
                                      zg[:, ic * TPC + tt * TT: ic * TPC + (tt + 1) * TT],
                                      start=(ic == 0), stop=(ic == 15))
                    ctx.inc(mi, "s3mm")
            block.tensor(_)

            def _(v, c=c, s3_b=s3_b):
                for tt in range(NTT):
                    v.wait_ge(ctx.sem("s3mm"), s3_b + tt + 1)
                    pb = psC if tt % 2 == 0 else psD
                    li = v.tensor_copy(gacc[:, c * TPC + tt * TT: c * TPC + (tt + 1) * TT], pb[:, 0:TT])
                    ctx.inc(li, "s3ev")
            block.vector(_)
        ss_.close()

        if _DBG3:
            def _(sy):
                ctx.wait(sy, "s3ev")
                oS = nc.dram_tensor("dbg_gaccS", [128, C8 * TPC], F32, kind="ExternalOutput").ap()
                ctx.inc(sy.dma_start(out=oS, in_=gacc[:]), "dbgS", 16)
            block.sync(_)

        ms = ExitStack()
        psT = ms.enter_context(nc.psum_tensor("psT", [128, 128], BF16, allow_name_mangling=True))
        we1 = ms.enter_context(nc.sbuf_tensor("we1", [128, C8 * D], BF16, allow_name_mangling=True))
        we3 = ms.enter_context(nc.sbuf_tensor("we3", [128, C8 * D], BF16, allow_name_mangling=True))
        we2 = ms.enter_context(nc.sbuf_tensor("we2", [128, C8 * D], BF16, allow_name_mangling=True))
        xg_tm = ms.enter_context(nc.sbuf_tensor("xg_tm", [128, 6 * 1024], BF16, allow_name_mangling=True))
        xgT = ms.enter_context(nc.sbuf_tensor("xgT", [128, C8 * CAP], BF16, allow_name_mangling=True))
        hsb = ms.enter_context(nc.sbuf_tensor("hsb", [128, C8 * CAP], BF16, allow_name_mangling=True))
        x2sb = ms.enter_context(nc.sbuf_tensor("x2sb", [128, CAP], F32, allow_name_mangling=True))
        prodF = ms.enter_context(nc.sbuf_tensor("prodF", [128, CAP], BF16, allow_name_mangling=True))
        ptm = ms.enter_context(nc.sbuf_tensor("ptm", [128, 6 * 1024], BF16, allow_name_mangling=True))
        b123 = ms.enter_context(nc.sbuf_tensor("b123", [128, 3 * 64], F32, allow_name_mangling=True))

        def _(sy):
            ctx.wait(sy, "s3mm")
            ctx.wait(sy, "s3ev")
            ctx.wait(sy, "z1ev")
            ctx.wait(sy, "gsct")
            ctx.wait(sy, "gcarup")
            ctx.inc(sy.dma_start(out=b123[:, 0:64], in_=I_b1), "bld", 16)
            ctx.inc(sy.dma_start(out=b123[:, 64:128], in_=I_b2), "bld", 16)
            ctx.inc(sy.dma_start(out=b123[:, 128:192], in_=I_b3), "bld", 16)
        block.sync(_)

        for sem in ("ewld", "ew2ld", "p1mm", "hact", "x2mm", "x2ev", "x3mm",
                    "pev", "ptp_mm", "ptp_ev", "pst", "xgld", "xg2_mm", "xg2_ev"):
            ctx.sem(sem)
        SUBW = (512, 256)
        SUBO = (0, 512)

        for e in range(8):
            def _(g, e=e):
                ctx.wait(g, "x2mm")
                ctx.wait(g, "x3mm")
                if e == 0:
                    ctx.wait(g, "s3mm")
                    ctx.wait(g, "s3ev")
                    ctx.wait(g, "gsct")
                    ctx.wait(g, "gcarup")
                ctx.wait(g, "p1mm")
                emit_weight_load(ctx, g, bass.AP(tensor=I_mw1.tensor, offset=e * D * D,
                                                 ap=[[D, D], [1, D]]), we1, sem="ewld")
                emit_weight_load(ctx, g, bass.AP(tensor=I_mw3.tensor, offset=e * D * D,
                                                 ap=[[D, D], [1, D]]), we3, sem="ewld")
                emit_weight_load(ctx, g, bass.AP(tensor=I_mw2.tensor, offset=e * D * D,
                                                 ap=[[D, D], [1, D]]), we2, sem="ew2ld")
            block.gpsimd(_)

            def _(sy, e=e):
                ctx.wait(sy, "gsct")
                ctx.wait(sy, "xg2_mm")
                src = xbkt[e * CAP:(e + 1) * CAP, :].rearrange("(tc p) f -> p tc f", p=128)
                ctx.inc(sy.dma_start(out=xg_tm[:].rearrange("p (tc f) -> p tc f", tc=6), in_=src), "xgld", 16)
            block.sync(_)

            x2_b0 = ctx.cnt["xg2_mm"]
            x2e_b0 = ctx.cnt["xg2_ev"]

            def _(t, e=e, x2_b0=x2_b0, x2e_b0=x2e_b0):
                ctx.wait(t, "xgld")
                ctx.wait(t, "x3mm")
                n = 0
                for tc6 in range(6):
                    for c in range(C8):
                        if x2e_b0 + n > 0:
                            t.wait_ge(ctx.sem("xg2_ev"), x2e_b0 + n)
                        ctx.inc(t.transpose(psT[:], xg_tm[:, tc6 * 1024 + c * 128: tc6 * 1024 + c * 128 + 128],
                                            eye_bs[:]), "xg2_mm")
                        n += 1
            block.tensor(_)

            def _(v, e=e, x2_b0=x2_b0):
                n = 0
                for tc6 in range(6):
                    for c in range(C8):
                        n += 1
                        v.wait_ge(ctx.sem("xg2_mm"), x2_b0 + n)
                        ctx.inc(v.tensor_copy(xgT[:, c * CAP + tc6 * 128: c * CAP + tc6 * 128 + 128],
                                              psT[:]), "xg2_ev")
            block.vector(_)

            p1_b = ctx.cnt["p1mm"]
            ha_b = ctx.cnt["hact"]

            def _(t, e=e, p1_b=p1_b, ha_b=ha_b):
                ctx.wait(t, "ewld")
                ctx.wait(t, "xg2_ev")
                ctx.wait(t, "hact")
                ctx.wait(t, "z1act")
                ctx.wait(t, "z1ev")
                n = 0
                for hc in range(C8):
                    for sub in range(2):
                        if ha_b + n - 1 > 0:
                            t.wait_ge(ctx.sem("hact"), ha_b + n - 1)
                        pb = psA if n % 2 == 0 else psB_
                        mi = None
                        for kc in range(C8):
                            mi = t.matmul(pb[:, 0:SUBW[sub]],
                                          we1[:, kc * D + hc * 128: kc * D + hc * 128 + 128],
                                          xgT[:, kc * CAP + SUBO[sub]: kc * CAP + SUBO[sub] + SUBW[sub]],
                                          start=(kc == 0), stop=(kc == C8 - 1))
                        ctx.inc(mi, "p1mm")
                        n += 1
            block.tensor(_)

            def _(s, e=e, p1_b=p1_b):
                ctx.wait(s, "bld")
                n = 0
                for hc in range(C8):
                    for sub in range(2):
                        n += 1
                        s.wait_ge(ctx.sem("p1mm"), p1_b + n)
                        pb = psA if (n - 1) % 2 == 0 else psB_
                        li = s.activation(hsb[:, hc * CAP + SUBO[sub]: hc * CAP + SUBO[sub] + SUBW[sub]],
                                          pb[:, 0:SUBW[sub]], AF.Silu,
                                          bias=b123[:, e * 8 + hc: e * 8 + hc + 1], scale=1.0)
                        ctx.inc(li, "hact")
            block.scalar(_)


            x2_b = ctx.cnt["x2mm"]
            x3_b = ctx.cnt["x3mm"]
            pe_b = ctx.cnt["pev"]
            pt_b = ctx.cnt["ptp_mm"]
            pte_b = ctx.cnt["ptp_ev"]

            def _(t, e=e, x2_b=x2_b, x3_b=x3_b, pe_b=pe_b, pt_b=pt_b, pte_b=pte_b, ha_b=ha_b):
                ctx.wait(t, "ew2ld")
                t.wait_ge(ctx.sem("hact"), ha_b + 16)
                n = 0
                npt = 0
                for c in range(C8):
                    for sub in range(2):
                        if x2_b + n - 1 > 0:
                            t.wait_ge(ctx.sem("x2ev"), x2_b + n - 1)
                        pba = psC if n % 2 == 0 else psD
                        mi = None
                        for hc in range(C8):
                            mi = t.matmul(pba[:, 0:SUBW[sub]],
                                          we2[:, hc * D + c * 128: hc * D + c * 128 + 128],
                                          hsb[:, hc * CAP + SUBO[sub]: hc * CAP + SUBO[sub] + SUBW[sub]],
                                          start=(hc == 0), stop=(hc == C8 - 1))
                        ctx.inc(mi, "x2mm")
                        if pe_b + n - 1 > 0:
                            t.wait_ge(ctx.sem("pev"), pe_b + n - 1)
                        pbb = psE if n % 2 == 0 else psF
                        mi = None
                        for kc in range(C8):
                            mi = t.matmul(pbb[:, 0:SUBW[sub]],
                                          we3[:, kc * D + c * 128: kc * D + c * 128 + 128],
                                          xgT[:, kc * CAP + SUBO[sub]: kc * CAP + SUBO[sub] + SUBW[sub]],
                                          start=(kc == 0), stop=(kc == C8 - 1))
                        ctx.inc(mi, "x3mm")
                        n += 1
                    t.wait_ge(ctx.sem("pev"), pe_b + n)
                    for tc6 in range(6):
                        if pte_b + npt > 0:
                            t.wait_ge(ctx.sem("ptp_ev"), pte_b + npt)
                        ctx.inc(t.transpose(psT[:], prodF[:, tc6 * 128: tc6 * 128 + 128], eye_bs[:]),
                                "ptp_mm")
                        npt += 1
            block.tensor(_)

            def _(v, e=e, x2_b=x2_b, x3_b=x3_b, pe_b=pe_b, pt_b=pt_b):
                ctx.wait(v, "pst")
                n = 0
                npt = 0
                for c in range(C8):
                    for sub in range(2):
                        v.wait_ge(ctx.sem("x2mm"), x2_b + n + 1)
                        pba = psC if n % 2 == 0 else psD
                        li = v.tensor_scalar(x2sb[:, SUBO[sub]: SUBO[sub] + SUBW[sub]],
                                             pba[:, 0:SUBW[sub]],
                                             b123[:, 64 + e * 8 + c: 64 + e * 8 + c + 1], None, ALU.add)
                        ctx.inc(li, "x2ev")
                        v.wait_ge(ctx.sem("x2ev"), ctx.cnt["x2ev"])
                        v.wait_ge(ctx.sem("x3mm"), x3_b + n + 1)
                        pbb = psE if n % 2 == 0 else psF
                        li = v.scalar_tensor_tensor(prodF[:, SUBO[sub]: SUBO[sub] + SUBW[sub]],
                                                    pbb[:, 0:SUBW[sub]],
                                                    b123[:, 128 + e * 8 + c: 128 + e * 8 + c + 1],
                                                    x2sb[:, SUBO[sub]: SUBO[sub] + SUBW[sub]],
                                                    ALU.add, ALU.mult)
                        ctx.inc(li, "pev")
                        n += 1
                    for tc6 in range(6):
                        npt += 1
                        v.wait_ge(ctx.sem("ptp_mm"), pt_b + npt)
                        ctx.inc(v.tensor_copy(ptm[:, tc6 * 1024 + c * 128: tc6 * 1024 + c * 128 + 128],
                                              psT[:]), "ptp_ev")
            block.vector(_)

            def _(sy, e=e):
                ctx.wait(sy, "ptp_ev")
                dst = prodbkt[e * CAP:(e + 1) * CAP, :].rearrange("(tc p) f -> p tc f", p=128)
                ctx.inc(sy.dma_start(out=dst, in_=ptm[:].rearrange("p (tc f) -> p tc f", tc=6)), "pst", 16)
            block.sync(_)

        if _DBG3:
            def _(sy):
                ctx.wait(sy, "pst")
                o3 = nc.dram_tensor("dbg_prodbkt", [8 * 768, D], BF16, kind="ExternalOutput").ap()
                ctx.inc(sy.dma_start(out=o3, in_=prodbkt), "dbgst", 16)
            block.sync(_)

        ys = ExitStack()
        g0 = ys.enter_context(nc.sbuf_tensor("g0", [128, 1024], BF16, allow_name_mangling=True))
        g1 = ys.enter_context(nc.sbuf_tensor("g1", [128, 1024], BF16, allow_name_mangling=True))
        ysum = ys.enter_context(nc.sbuf_tensor("ysum", [128, 1024], F32, allow_name_mangling=True))
        for sem in ("ygth", "ysumv", "ytp_mm", "yacc"):
            ctx.sem(sem)
        for tc in range(16):
            def _(g_, tc=tc):
                ctx.wait(g_, "pst")
                ctx.wait(g_, "ysumv")
                g_.indirect_dma_start(
                    g0[:], None, prodbkt,
                    bass.IndirectOffsetOnAxis(ap=addru[:, 2 * tc: 2 * tc + 1], axis=0)).then_inc(ctx.sem("ygth"), 16)
                g_.indirect_dma_start(
                    g1[:], None, prodbkt,
                    bass.IndirectOffsetOnAxis(ap=addru[:, 2 * tc + 1: 2 * tc + 2], axis=0)).then_inc(ctx.sem("ygth"), 16)
                ctx.cnt[_SEM_ALIAS.get("ygth", "ygth")] += 32
            block.gpsimd(_)

            def _(v, tc=tc):
                ctx.wait(v, "ygth")
                ctx.wait(v, "ytp_mm")
                ctx.inc(v.tensor_tensor(ysum[:], g0[:], g1[:], ALU.add), "ysumv")
            block.vector(_)

            yt_b = ctx.cnt["ytp_mm"]
            ya_b = ctx.cnt["yacc"]

            def _(t, tc=tc, ya_b=ya_b):
                ctx.wait(t, "ysumv")
                n = 0
                for c in range(C8):
                    if ya_b + n > 0:
                        t.wait_ge(ctx.sem("yacc"), ya_b + n)
                    ctx.inc(t.transpose(psG[:, 0:128], ysum[:, c * 128:(c + 1) * 128], eyef_sb[:]),
                            "ytp_mm")
                    n += 1
            block.tensor(_)

            def _(v, tc=tc, yt_b=yt_b):
                if _DBG3 and tc == 0:
                    ctx.wait(v, "dbgS")
                n = 0
                for c in range(C8):
                    n += 1
                    v.wait_ge(ctx.sem("ytp_mm"), yt_b + n)
                    gsl = slice(c * TPC + tc * 128, c * TPC + (tc + 1) * 128)
                    li = v.tensor_tensor(gacc[:, gsl], gacc[:, gsl], psG[:, 0:128], ALU.add)
                    ctx.inc(li, "yacc")
            block.vector(_)
        if _DBG3:
            def _(sy):
                ctx.wait(sy, "yacc")
                o4 = nc.dram_tensor("dbg_gaccY", [128, C8 * TPC], F32, kind="ExternalOutput").ap()
                ctx.inc(sy.dma_start(out=o4, in_=gacc[:]), "dbgst", 16)
            block.sync(_)
        ys.close()
        ms.close()

        fs = ExitStack()
        x_f = fs.enter_context(nc.sbuf_tensor("x_f", [128, C8 * TT], F32, allow_name_mangling=True))
        s_f = fs.enter_context(nc.sbuf_tensor("s_f", [128, C8 * TT], F32, allow_name_mangling=True))
        o_f = fs.enter_context(nc.sbuf_tensor("o_f", [128, C8 * TT], F32, allow_name_mangling=True))

        ctx.sem("fsil3"), ctx.sem("fcomb"), ctx.sem("fst")
        def _(s):
            ctx.wait(s, "yacc")
            li = None
            for c in range(C8):
                for tt in range(NTT):
                    li = s.activation(gacc[:, c * TPC + tt * TT: c * TPC + (tt + 1) * TT],
                                      gacc[:, c * TPC + tt * TT: c * TPC + (tt + 1) * TT],
                                      AF.Silu, bias=0.0, scale=1.0)
            ctx.inc(li, "fsil3")
        block.scalar(_)

        for tt in range(NTT):
            s0 = tt * TT

            def _(sy, s0=s0):
                ctx.wait(sy, "fcomb")
                ctx.wait(sy, "yacc")
                srcx = I_xT.rearrange("(c p) t -> p c t", p=128)[:, :, s0:s0 + TT]
                ctx.inc(sy.dma_start(out=x_f[:].rearrange("p (c t) -> p c t", c=C8), in_=srcx), "fxld", 16)
                srcs = sdram.rearrange("(c p) t -> p c t", p=128)[:, :, s0:s0 + TT]
                ctx.inc(sy.dma_start(out=s_f[:].rearrange("p (c t) -> p c t", c=C8), in_=srcs), "fxld", 16)
            block.sync(_)

            def _(v, s0=s0, tt=tt):
                ctx.wait(v, "fxld")
                ctx.wait(v, "fsil3")
                ctx.wait(v, "fst")
                li = None
                for c in range(C8):
                    ctx.chain(v, v.tensor_tensor(o_f[:, c * TT:(c + 1) * TT],
                                                 s_f[:, c * TT:(c + 1) * TT],
                                                 gacc[:, c * TPC + s0: c * TPC + s0 + TT], ALU.mult))
                    li = v.tensor_tensor(o_f[:, c * TT:(c + 1) * TT],
                                         o_f[:, c * TT:(c + 1) * TT],
                                         x_f[:, c * TT:(c + 1) * TT], ALU.add)
                ctx.inc(li, "fcomb")
            block.vector(_)

            def _(sy, s0=s0):
                ctx.wait(sy, "fcomb")
                dst = O_x1.rearrange("(c p) t -> p c t", p=128)[:, :, s0:s0 + TT]
                ctx.inc(sy.dma_start(out=dst, in_=o_f[:].rearrange("p (c t) -> p c t", c=C8)), "fst", 16)
            block.sync(_)

        def _(sy):
            ctx.wait(sy, "fst")
        block.sync(_)
        fs.close()

    fin()
    return nc



def build_launch4(F=None):
    nc, es, block, ctx, dt, fin = stage_begin(F, None)
    xT = dt("x1T", [D, TPC], F32, "ExternalInput")
    Wq = dt("Wq1", [D, D], F32, "ExternalInput")
    Wk = dt("Wk1", [D, D], F32, "ExternalInput")
    Wv = dt("Wv1", [D, D], F32, "ExternalInput")
    Bq = dt("bq1", [128, C8], F32, "ExternalInput")
    Bk = dt("bk1", [128, C8], F32, "ExternalInput")
    Bv = dt("bv1", [128, C8], F32, "ExternalInput")
    eyeb = dt("eyeb1", [128, 128], BF16, "ExternalInput") if F is not None else dt("eyeb", [128, 128], BF16, "ExternalInput")

    O_xn = dt("xn1", [D, TPC], F32, "ExternalOutput")
    O_qh = dt("Q1", [D, TPC], BF16, "ExternalOutput")
    O_kh = dt("K1", [D, TPC], BF16, "ExternalOutput")
    O_vt = dt("V1t", [TPC, NH * 65], BF16, "ExternalOutput")
    ccf1 = dt("ccf1in", [1, 4096], F32, "Internal") if F is not None else None

    with es:
        x_tt = sb(ctx, "x_tt", 128, C8 * TT)
        xn_tt = sb(ctx, "xn_tt", 128, C8 * TT)
        sq = sb(ctx, "sq", 128, C8 * TT)
        rows = sb(ctx, "rows", 1, 4 * TT)
        xn2 = sb(ctx, "xn2", 128, C8 * TT, BF16)
        ob_tt = sb(ctx, "ob_tt", 128, C8 * TT, BF16)
        vt_tt = sb(ctx, "vt_tt", 128, (TT // 128) * NH * 65, BF16)
        xn_bt = sb(ctx, "xn_bt", 128, C8 * TT, BF16)
        xtm_tt = sb(ctx, "xtm_tt", 128, (TT // 128) * D, BF16)
        wq_sb = sb(ctx, "wq_sb", 128, C8 * D, BF16)
        wk_sb = sb(ctx, "wk_sb", 128, C8 * D, BF16)
        wv_sb = sb(ctx, "wv_sb", 128, C8 * D, BF16)
        bqkv = sb(ctx, "bqkv", 128, 3 * C8)
        eye_sb = sb(ctx, "eye_sb", 128, 128, BF16)
        ones_sb = sb(ctx, "ones_sb", 128, 128)
        cst = sb(ctx, "cst", 128, 1)

        ps_a = psum(ctx, "ps_a", 128, TT)
        ps_b = psum(ctx, "ps_b", 128, TT)
        ps_r1 = psum(ctx, "ps_r1", 1, TT)
        ps_r2 = psum(ctx, "ps_r2", 1, TT)
        ps_bc1 = psum(ctx, "ps_bc1", 128, TT)
        ps_bc2 = psum(ctx, "ps_bc2", 128, TT)
        ps_t1 = psum(ctx, "ps_t1", 128, 128, BF16)

        def _(g):
            emit_weight_load(ctx, g, Wq, wq_sb)
            emit_weight_load(ctx, g, Wk, wk_sb)
            emit_weight_load(ctx, g, Wv, wv_sb)
        block.gpsimd(_)

        if F is not None:
            zsb = sb(ctx, "e1zsb", 128, 4160, BF16)
            zsf = sb(ctx, "e1zsf", 128, 32)
            emit_exchange_zero(ctx, block, F, "e1", "K1s", "V1ts", zsb, zsf, 1)

        def _(sy):
            ctx.inc(sy.dma_start(out=bqkv[:, 0:C8], in_=Bq), "cld", 16)
            ctx.inc(sy.dma_start(out=bqkv[:, C8:2 * C8], in_=Bk), "cld", 16)
            ctx.inc(sy.dma_start(out=bqkv[:, 2 * C8:3 * C8], in_=Bv), "cld", 16)
            ctx.inc(sy.dma_start(out=eye_sb[:], in_=eyeb), "cld", 16)
        block.sync(_)

        def _(v):
            v.memset(ones_sb[:], 1.0)
            ctx.inc(v.memset(cst[:, 0:1], 1e-5), "vinit")
        block.vector(_)

        for tt in range(NTT):
            s0 = tt * TT

            def _(sy, s0=s0):
                ctx.wait(sy, "xn2_ready")
                src = xT.rearrange("(c p) t -> p c t", p=128)[:, :, s0:s0 + TT]
                ctx.inc(sy.dma_start(out=x_tt[:].rearrange("p (c t) -> p c t", c=C8), in_=src), "xld", 16)
            block.sync(_)

            def _(s):
                ctx.wait(s, "xld")
                ctx.wait(s, "mm_stats")
                ctx.inc(s.activation(sq[:], x_tt[:], AF.Square, bias=0.0, scale=1.0), "sqk")
            block.scalar(_)

            def _(t):
                ctx.wait(t, "xld")
                ctx.wait(t, "vinit")
                ctx.wait(t, "var_ready")
                for c in range(C8):
                    t.matmul(ps_r1[:], ones_sb[:, 0:1], x_tt[:, c * TT:(c + 1) * TT],
                             start=(c == 0), stop=(c == C8 - 1))
                ctx.wait(t, "sqk")
                mi = None
                for c in range(C8):
                    mi = t.matmul(ps_r2[:], ones_sb[:, 0:1], sq[:, c * TT:(c + 1) * TT],
                                  start=(c == 0), stop=(c == C8 - 1))
                ctx.inc(mi, "mm_stats")
            block.tensor(_)

            def _(v):
                ctx.wait(v, "mm_stats")
                S_, SS = rows[:, 0:TT], rows[:, TT:2 * TT]
                A_, B_ = rows[:, 2 * TT:3 * TT], rows[:, 3 * TT:4 * TT]
                ctx.chain(v, v.tensor_single_scalar(S_, ps_r1[:], 1.0 / D, ALU.mult))
                ctx.chain(v, v.tensor_single_scalar(SS, ps_r2[:], 1.0 / D, ALU.mult))
                ctx.chain(v, v.tensor_tensor(A_, S_, S_, ALU.mult))
                ctx.inc(v.tensor_tensor(A_, SS, A_, ALU.subtract), "var_ready")
            block.vector(_)

            def _(s):
                ctx.wait(s, "var_ready")
                ctx.inc(s.activation(rows[:, 2 * TT:3 * TT], rows[:, 2 * TT:3 * TT],
                                     AF.Sqrt, bias=cst[0:1, 0:1], scale=1.0), "std_ready")
            block.scalar(_)

            def _(v):
                ctx.wait(v, "std_ready")
                S_ = rows[:, 0:TT]
                A_, B_ = rows[:, 2 * TT:3 * TT], rows[:, 3 * TT:4 * TT]
                ctx.chain(v, v.reciprocal(A_, A_))
                ctx.chain(v, v.tensor_tensor(B_, S_, A_, ALU.mult))
                ctx.inc(v.tensor_single_scalar(B_, B_, -1.0, ALU.mult), "ab_ready")
            block.vector(_)

            def _(t):
                ctx.wait(t, "ab_ready")
                ctx.wait(t, "xn2_ready")
                t.matmul(ps_bc1[:], ones_sb[0:1, 0:128], rows[0:1, 2 * TT:3 * TT], start=True, stop=True)
                ctx.inc(t.matmul(ps_bc2[:], ones_sb[0:1, 0:128], rows[0:1, 3 * TT:4 * TT],
                                 start=True, stop=True), "bc_ready")
            block.tensor(_)

            def _(v):
                ctx.wait(v, "bc_ready")
                ctx.wait(v, "xnst")
                li = None
                for c in range(C8):
                    xc = x_tt[:, c * TT:(c + 1) * TT]
                    xnc = xn_tt[:, c * TT:(c + 1) * TT]
                    ctx.chain(v, v.tensor_tensor(xnc, xc, ps_bc1[:], ALU.mult))
                    li = v.tensor_tensor(xnc, xnc, ps_bc2[:], ALU.add)
                ctx.inc(li, "xn_ready")
            block.vector(_)

            def _(sy, s0=s0, tt=tt):
                ctx.wait(sy, "xn_ready")
                dst = O_xn.rearrange("(c p) t -> p c t", p=128)[:, :, s0:s0 + TT]
                ctx.inc(sy.dma_start(out=dst, in_=xn_tt[:].rearrange("p (c t) -> p c t", c=C8)), "xnst", 16)
                if F is not None:
                    for c in range(C8):
                        dstb = bass.AP(tensor=ccf1.tensor, offset=ccf1.offset + tt * 1024 + c * 128,
                                       ap=[[1, 128], [1, 1]])
                        ctx.inc(sy.dma_start(out=dstb, in_=xn_tt[:, c * TT + TT - 1: c * TT + TT]), "xnst", 16)
            block.sync(_)

            def _(v):
                ctx.wait(v, "xn_ready")
                ctx.inc(v.tensor_copy(xn2[:], xn_tt[:]), "xn2_ready")
            block.vector(_)

            for idx, (nm, w_sb_, out_dram) in enumerate((("q", wq_sb, O_qh), ("k", wk_sb, O_kh))):
                def evac(v, c, pb, idx=idx):
                    return v.tensor_scalar(ob_tt[:, c * TT:(c + 1) * TT], pb[:, 0:TT],
                                           bqkv[:, idx * C8 + c: idx * C8 + c + 1], None, ALU.add)
                emit_proj(ctx, "p" + nm, w_sb_, xn2, (0, TT), evac,
                          xw=TT, pbanks=(ps_a, ps_b),
                          wait_before=("xn2_ready", "wld", "qhst"),
                          wait_before_ev=("qhst", "cld"))

                def _(sy, s0=s0, out_dram=out_dram, nm=nm):
                    ctx.wait(sy, "p" + nm + "_ev")
                    if F is not None and nm == "k":
                        dst = F.io[f"K1_t{s0 // TT}"].rearrange("(c p) q -> p c q", p=128)
                    else:
                        dst = out_dram.rearrange("(c p) t -> p c t", p=128)[:, :, s0:s0 + TT]
                    ctx.inc(sy.dma_start(out=dst, in_=ob_tt[:].rearrange("p (c t) -> p c t", c=C8)), "qhst", 16)
                block.sync(_)

            def vevac(v, c, pb):
                return v.tensor_scalar(ob_tt[:, c * TT:(c + 1) * TT], pb[:, 0:TT],
                                       bqkv[:, 2 * C8 + c: 2 * C8 + c + 1], None, ALU.add)
            emit_proj(ctx, "pv", wv_sb, xn2, (0, TT), vevac,
                      xw=TT, pbanks=(ps_a, ps_b),
                      wait_before=("xn2_ready", "wld", "qhst"),
                      wait_before_ev=("qhst", "cld"))

            ctx.sem("vtp_mm"), ctx.sem("vtp_ev")
            tp_base = ctx.cnt["vtp_mm"]
            te_base = ctx.cnt["vtp_ev"]

            def _(t, tp_base=tp_base, te_base=te_base):
                ctx.wait(t, "pv_ev")
                ctx.wait(t, "vtst")
                n = 0
                for tc in range(TT // 128):
                    for c in range(C8):
                        if n >= 1:
                            t.wait_ge(ctx.sem("vtp_ev"), te_base + n)
                        ctx.inc(t.transpose(ps_t1[:], ob_tt[:, c * TT + tc * 128: c * TT + tc * 128 + 128],
                                            eye_sb[:]), "vtp_mm")
                        n += 1
            block.tensor(_)

            def _(v, tp_base=tp_base):
                n = 0
                for tc in range(TT // 128):
                    base = tc * NH * 65
                    for c in range(C8):
                        n += 1
                        v.wait_ge(ctx.sem("vtp_mm"), tp_base + n)
                        h0, h1 = 2 * c, 2 * c + 1
                        v.tensor_copy(vt_tt[:, base + h0 * 65: base + h0 * 65 + 64], ps_t1[:, 0:64])
                        ctx.inc(v.tensor_copy(vt_tt[:, base + h1 * 65: base + h1 * 65 + 64],
                                              ps_t1[:, 64:128]), "vtp_ev")
                    li = v.tensor_copy(
                        vt_tt[:].rearrange("p (tc h x) -> p tc h x", tc=TT // 128, h=NH)[:, tc, :, 64:65],
                        ones_sb[:, 0:NH].rearrange("p (h x) -> p h x", x=1))
                ctx.inc(li, "vt_ready")
            block.vector(_)

            def _(sy, s0=s0):
                ctx.wait(sy, "vt_ready")
                if F is not None:
                    dst = F.io[f"V1t_t{s0 // TT}"].rearrange("(tc p) f -> p tc f", p=128)
                else:
                    dst = O_vt[s0:s0 + TT, :].rearrange("(tc p) f -> p tc f", p=128)
                src = vt_tt[:].rearrange("p (tc f) -> p tc f", tc=TT // 128)
                ctx.inc(sy.dma_start(out=dst, in_=src), "vtst", 16)
            block.sync(_)
            if F is not None:
                emit_exchange_tile(ctx, block, F, "e1", tt, "cc1", "K1s", "V1ts",
                                   ctx.cnt["qhst"], ctx.cnt["vtst"])

        def _(sy):
            ctx.wait(sy, "xnst")
            ctx.wait(sy, "qhst")
            ctx.wait(sy, "vtst")
        block.sync(_)

    fin()
    return nc




def build_launch5(F=None):
    nc, es, block, ctx, dt, fin = stage_begin(F, None)
    I_x1 = dt("x1T", [D, TPC], F32, "ExternalInput")
    I_xn = dt("xn1", [D, TPC], F32, "ExternalInput")
    I_xp = dt("xnprev", [D, 4], F32, "ExternalInput")
    I_qh = dt("Q1", [D, TPC], BF16, "ExternalInput")
    I_khs = dt("K1s", [D, NSLOT], BF16, "ExternalInput")
    I_vtm = dt("V1ts", [NSLOT, NH * 65], BF16, "ExternalInput")
    I_wo = dt("Wo1", [D, D], F32, "ExternalInput")
    I_bo = dt("bo1", [128, C8], F32, "ExternalInput")
    I_wr = dt("WrT", [D, D], F32, "ExternalInput")
    I_wk = dt("WkT", [D, D], F32, "ExternalInput")
    I_wv = dt("WvT", [D, D], F32, "ExternalInput")
    I_mur = dt("mur", [128, C8], F32, "ExternalInput")
    I_muk = dt("muk", [128, C8], F32, "ExternalInput")
    I_mask = dt("mask1", [128, 128], F32, "ExternalInput")
    O_out = dt("out", [D, TPC], F32, "ExternalOutput")

    with es:
        s1 = sb(ctx, "s1", 128, C8 * TPC)
        bo_sb = sb(ctx, "bo_sb", 128, C8)
        ones_sb = sb(ctx, "ones_sb", 128, 128)
        ats = ExitStack()
        kh_g = ats.enter_context(nc.sbuf_tensor("kh_g", [128, 4 * NSLOT], BF16, allow_name_mangling=True))
        qh_g = ats.enter_context(nc.sbuf_tensor("qh_g", [128, 4 * TPC], BF16, allow_name_mangling=True))
        vt_g = ats.enter_context(nc.sbuf_tensor("vt_g", [128, 32 * 520], BF16, allow_name_mangling=True))
        ao = ats.enter_context(nc.sbuf_tensor("ao", [128, C8 * TPC], BF16, allow_name_mangling=True))
        Pt = ats.enter_context(nc.sbuf_tensor("Pt", [128, 2 * 512], BF16, allow_name_mangling=True))
        rr = ats.enter_context(nc.sbuf_tensor("rr", [128, 512], F32, allow_name_mangling=True))
        rb_sb = ats.enter_context(nc.sbuf_tensor("rb_sb", [64, 512], F32, allow_name_mangling=True))
        wo_sb = ats.enter_context(nc.sbuf_tensor("wo_sb", [128, C8 * D], BF16, allow_name_mangling=True))
        mask_sb = ats.enter_context(nc.sbuf_tensor("mask_sb", [128, 128], F32, allow_name_mangling=True))

        psS_a = psum(ctx, "psS_a", 128, 512)
        psS_b = psum(ctx, "psS_b", 128, 512)
        O_a = psum(ctx, "O_a", 65, 512)
        O_b = psum(ctx, "O_b", 65, 512)
        O_c = psum(ctx, "O_c", 65, 512)
        O_d = psum(ctx, "O_d", 65, 512)
        psB = psum(ctx, "psB", 64, 512)
        ps_m = psum(ctx, "ps_m", 128, 512)

        def _(sy):
            if F is not None:
                sy.wait_ge(ctx.sem("e1asm"), 48)
            ctx.inc(sy.dma_start(out=mask_sb[:], in_=I_mask), "cld", 16)
            ctx.inc(sy.dma_start(out=bo_sb[:], in_=I_bo), "cld", 16)
        block.sync(_)

        def _(g):
            if F is not None:
                g.wait_ge(ctx.sem("e1asm"), 48)
            emit_weight_load(ctx, g, I_wo, wo_sb, sem="wld")
        block.gpsimd(_)

        def _(v):
            if F is not None:
                v.wait_ge(ctx.sem("e1asm"), 48)
            ctx.inc(v.memset(ones_sb[:], 1.0), "vinit")
        block.vector(_)

        for sem in ("smm", "madd", "pexp", "ppv", "rb_used", "fr_mm", "fk_mm",
                    "fst", "fcomb", "wo1_ev", "fk_ev", "fr_ev"):
            ctx.sem(sem)
        rb_stage_b = ctx.cnt["rb_used"]

        def emit_epilogue5(ent):
            Ob_, h_, j_, pr_, ppv_end = ent

            rec_b = ctx.cnt_of("recip")

            def _(v):
                v.wait_ge(ctx.sem("ppv"), ppv_end)
                ctx.chain(v, v.tensor_single_scalar(rr[64:65, :], Ob_[64:65, :], 1e-30, ALU.add))
                ctx.inc(v.reciprocal(rr[64:65, :], rr[64:65, :]), "recip")
            block.vector(_)

            rbu_b = ctx.cnt_of("rb_used")

            def _(t):
                t.wait_ge(ctx.sem("recip"), rec_b + 1)
                if rbu_b > 0:
                    t.wait_ge(ctx.sem("rb_used"), rbu_b)
                ctx.inc(t.matmul(psB[:], ones_sb[64:65, 0:64], rr[64:65, :],
                                 start=True, stop=True), "mm_rb")
            block.tensor(_)

            mrb_b = ctx.cnt_of("mm_rb")

            def _(v):
                v.wait_ge(ctx.sem("mm_rb"), mrb_b)
                ctx.chain(v, v.tensor_copy(rb_sb[:], psB[:]))
                dst = ao[pr_:pr_ + 64, (h_ // 2) * TPC + j_ * 512: (h_ // 2) * TPC + j_ * 512 + 512]
                ctx.inc(v.tensor_tensor(dst, Ob_[0:64, :], rb_sb[:], ALU.mult), "rb_used")
            block.vector(_)

        NG = 2
        for g in range(NG):
            pend = []

            def _(sy, g=g):
                ctx.wait(sy, "rb_used")
                if F is not None:
                    sy.wait_ge(ctx.sem("e1asm"), 176)
                    sy.wait_ge(ctx.sem("e1asm2"), 128)
                src = I_khs.rearrange("(c p) s -> p c s", p=128)[:, 4 * g:4 * g + 4, :]
                ctx.inc(sy.dma_start(out=kh_g[:].rearrange("p (c s) -> p c s", c=4), in_=src), "gld", 16)
                srcq = I_qh.rearrange("(c p) s -> p c s", p=128)[:, 4 * g:4 * g + 4, :]
                ctx.inc(sy.dma_start(out=qh_g[:].rearrange("p (c s) -> p c s", c=4), in_=srcq), "gld", 16)
                srcv = I_vtm[:, g * 520:(g + 1) * 520].rearrange("(kc p) f -> p kc f", p=128)
                ctx.inc(sy.dma_start(out=vt_g[:].rearrange("p (kc f) -> p kc f", kc=32), in_=srcv), "gld", 16)
            block.sync(_)

            for hl in range(8):
                h = 8 * g + hl
                cl = hl // 2
                pr = (hl % 2) * 64
                for j in range(4):
                    G0 = 512 * (2 * j + 1)
                    nkc = 8 * j + 8
                    it = g * 32 + hl * 4 + j
                    Ob = (O_a, O_b, O_c, O_d)[it % 4]
                    smm_b = ctx.cnt["smm"]
                    madd_b = ctx.cnt["madd"]
                    pexp_b = ctx.cnt["pexp"]
                    ppv_b = ctx.cnt["ppv"]
                    geo = []
                    for kc in range(nkc):
                        c0 = 128 * kc - G0
                        geo.append((c0, max(0, c0), 0 <= c0 <= 384))
                    nw = [0]
                    for kc in range(nkc):
                        nw.append(nw[-1] + (1 if geo[kc][2] else 0))

                    if len(pend) == 2:
                        emit_epilogue5(pend.pop(0))

                    def _(t, g=g, hl=hl, cl=cl, pr=pr, j=j, nkc=nkc, Ob=Ob, it=it,
                          smm_b=smm_b, pexp_b=pexp_b, ppv_b=ppv_b, geo=geo):
                        ctx.wait(t, "gld")
                        ctx.wait(t, "vinit")
                        if pexp_b > 0:
                            t.wait_ge(ctx.sem("pexp"), pexp_b)
                        if it >= 4:
                            t.wait_ge(ctx.sem("rb_used"), rb_stage_b + it - 3)
                        for kc in range(nkc):
                            c0, qq0, has_w = geo[kc]
                            pb = psS_a if kc % 2 == 0 else psS_b
                            if kc >= 2:
                                t.wait_ge(ctx.sem("pexp"), pexp_b + kc - 1)
                            mi = t.matmul(pb[:, qq0:512],
                                          kh_g[pr:pr + 64, cl * NSLOT + 128 * kc: cl * NSLOT + 128 * kc + 128],
                                          qh_g[pr:pr + 64, cl * TPC + j * 512 + qq0: cl * TPC + (j + 1) * 512],
                                          start=True, stop=True)
                            ctx.inc(mi, "smm")
                            if kc >= 1:
                                pkc = kc - 1
                                qq0p = geo[pkc][1]
                                t.wait_ge(ctx.sem("pexp"), pexp_b + pkc + 1)
                                mi = t.matmul(Ob[:, qq0p:512],
                                              vt_g[:, pkc * 520 + hl * 65: pkc * 520 + hl * 65 + 65],
                                              Pt[:, (pkc % 2) * 512 + qq0p: (pkc % 2) * 512 + 512],
                                              start=(pkc == 0), stop=False)
                                ctx.inc(mi, "ppv")
                        kc = nkc - 1
                        qq0p = geo[kc][1]
                        t.wait_ge(ctx.sem("pexp"), pexp_b + kc + 1)
                        mi = t.matmul(Ob[:, qq0p:512],
                                      vt_g[:, kc * 520 + hl * 65: kc * 520 + hl * 65 + 65],
                                      Pt[:, (kc % 2) * 512 + qq0p: (kc % 2) * 512 + 512],
                                      start=(kc == 0), stop=True)
                        ctx.inc(mi, "ppv")
                    block.tensor(_)

                    def _(v, nkc=nkc, smm_b=smm_b, geo=geo):
                        ctx.wait(v, "cld")
                        for kc in range(nkc):
                            c0, qq0, has_w = geo[kc]
                            if not has_w:
                                continue
                            v.wait_ge(ctx.sem("smm"), smm_b + kc + 1)
                            pb = psS_a if kc % 2 == 0 else psS_b
                            li = v.tensor_tensor(pb[:, c0:c0 + 128], pb[:, c0:c0 + 128],
                                                 mask_sb[:], ALU.add)
                            ctx.inc(li, "madd")
                    block.vector(_)

                    def _(s, nkc=nkc, smm_b=smm_b, madd_b=madd_b,
                          pexp_b=pexp_b, ppv_b=ppv_b, geo=geo, nw=nw):
                        if ppv_b > 0:
                            s.wait_ge(ctx.sem("ppv"), ppv_b)
                        for kc in range(nkc):
                            c0, qq0, has_w = geo[kc]
                            pb = psS_a if kc % 2 == 0 else psS_b
                            if has_w:
                                s.wait_ge(ctx.sem("madd"), madd_b + nw[kc + 1])
                            else:
                                s.wait_ge(ctx.sem("smm"), smm_b + kc + 1)
                            if kc >= 2:
                                s.wait_ge(ctx.sem("ppv"), ppv_b + kc - 1)
                            li = s.activation(Pt[:, (kc % 2) * 512 + qq0: (kc % 2) * 512 + 512],
                                              pb[:, qq0:512], AF.Exp, bias=0.0, scale=0.125)
                            ctx.inc(li, "pexp")
                    block.scalar(_)

                    pend.append((Ob, h, j, pr, ctx.cnt["ppv"]))

            while pend:
                emit_epilogue5(pend.pop(0))

        for tt in range(NTT):
            s0 = tt * TT

            def evac(v, c, pb, s0=s0):
                return v.tensor_scalar(s1[:, c * TPC + s0: c * TPC + s0 + TT], pb[:, 0:TT],
                                       bo_sb[:, c:c + 1], None, ALU.add)
            emit_proj(ctx, "wo1", wo_sb, ao, (s0, TT), evac,
                      xw=TPC, pbanks=(psS_a, psS_b),
                      wait_before=("rb_used", "wld", "pexp"),
                      wait_before_ev=("cld",))

        ats.close()

        fs = ExitStack()
        wr_sb = fs.enter_context(nc.sbuf_tensor("wr_sb", [128, C8 * D], BF16, allow_name_mangling=True))
        wk_sb = fs.enter_context(nc.sbuf_tensor("wk_sb", [128, C8 * D], BF16, allow_name_mangling=True))
        wv_sb = fs.enter_context(nc.sbuf_tensor("wv_sb", [128, C8 * D], BF16, allow_name_mangling=True))
        xn_tt = fs.enter_context(nc.sbuf_tensor("xn_tt", [128, C8 * (TT + 1)], F32, allow_name_mangling=True))
        mus = fs.enter_context(nc.sbuf_tensor("mus", [128, 4 * C8], F32, allow_name_mangling=True))
        xr_tt = fs.enter_context(nc.sbuf_tensor("xr_tt", [128, C8 * TT], BF16, allow_name_mangling=True))
        xk_tt = fs.enter_context(nc.sbuf_tensor("xk_tt", [128, C8 * TT], BF16, allow_name_mangling=True))
        r_tt = fs.enter_context(nc.sbuf_tensor("r_tt", [128, C8 * TT], F32, allow_name_mangling=True))
        ka_tt = fs.enter_context(nc.sbuf_tensor("ka_tt", [128, C8 * TT], BF16, allow_name_mangling=True))
        tmp_tt = fs.enter_context(nc.sbuf_tensor("tmp_tt", [128, C8 * TT], F32, allow_name_mangling=True))
        x1_tt = fs.enter_context(nc.sbuf_tensor("x1_tt", [128, C8 * TT], F32, allow_name_mangling=True))

        def _(g_):
            ctx.wait(g_, "rb_used")
            ctx.wait(g_, "wo1_mm")
            emit_weight_load(ctx, g_, I_wr, wr_sb, sem="fwld")
            emit_weight_load(ctx, g_, I_wk, wk_sb, sem="fwld")
            emit_weight_load(ctx, g_, I_wv, wv_sb, sem="fwld")
        block.gpsimd(_)

        def _(sy):
            ctx.wait(sy, "wo1_mm")
            ctx.inc(sy.dma_start(out=mus[:, 0:C8], in_=I_mur), "muld", 16)
            ctx.inc(sy.dma_start(out=mus[:, C8:2 * C8], in_=I_muk), "muld", 16)
        block.sync(_)

        def _(v):
            ctx.wait(v, "muld")
            ctx.chain(v, v.tensor_scalar(mus[:, 2 * C8:3 * C8], mus[:, 0:C8], -1.0, 1.0, ALU.mult, ALU.add))
            ctx.inc(v.tensor_scalar(mus[:, 3 * C8:4 * C8], mus[:, C8:2 * C8], -1.0, 1.0, ALU.mult, ALU.add), "om_done")
        block.vector(_)

        for tt in range(NTT):
            s0 = tt * TT

            def _(sy, s0=s0, tt=tt):
                ctx.wait(sy, "fcomb")
                ctx.wait(sy, "fst")
                ctx.wait(sy, "e1asm")
                ctx.wait(sy, "e1asm2")
                if F is not None:
                    srcp = bass.AP(tensor=I_xp.tensor, offset=I_xp.offset + tt * 1024,
                                   ap=[[1, 128], [128, C8], [1, 1]])
                else:
                    srcp = I_xp.rearrange("(c p) j -> p c j", p=128)[:, :, tt:tt + 1]
                dstp = xn_tt[:].rearrange("p (c t) -> p c t", c=C8)[:, :, 0:1]
                ctx.inc(sy.dma_start(out=dstp, in_=srcp, allow_slow_non_contiguous=True), "fxnld", 16)
                src = I_xn.rearrange("(c p) t -> p c t", p=128)[:, :, s0:s0 + TT]
                dst = xn_tt[:].rearrange("p (c t) -> p c t", c=C8)[:, :, 1:TT + 1]
                ctx.inc(sy.dma_start(out=dst, in_=src), "fxnld", 16)
                srcx = I_x1.rearrange("(c p) t -> p c t", p=128)[:, :, s0:s0 + TT]
                ctx.inc(sy.dma_start(out=x1_tt[:].rearrange("p (c t) -> p c t", c=C8), in_=srcx), "fxnld", 16)
            block.sync(_)

            def _(v):
                ctx.wait(v, "fxnld")
                ctx.wait(v, "om_done")
                ctx.wait(v, "fr_mm")
                ctx.wait(v, "fk_mm")
                ctx.wait(v, "fst")
                li = None
                for c in range(C8):
                    xc = xn_tt[:, c * (TT + 1) + 1: c * (TT + 1) + 1 + TT]
                    xp = xn_tt[:, c * (TT + 1): c * (TT + 1) + TT]
                    ctx.chain(v, v.tensor_scalar(tmp_tt[:, c * TT:(c + 1) * TT], xc,
                                                 mus[:, c:c + 1], None, ALU.mult))
                    ctx.chain(v, v.scalar_tensor_tensor(xr_tt[:, c * TT:(c + 1) * TT], xp,
                                                        mus[:, 2 * C8 + c: 2 * C8 + c + 1],
                                                        tmp_tt[:, c * TT:(c + 1) * TT],
                                                        ALU.mult, ALU.add))
                    ctx.chain(v, v.tensor_scalar(tmp_tt[:, c * TT:(c + 1) * TT], xc,
                                                 mus[:, C8 + c: C8 + c + 1], None, ALU.mult))
                    li = v.scalar_tensor_tensor(xk_tt[:, c * TT:(c + 1) * TT], xp,
                                                mus[:, 3 * C8 + c: 3 * C8 + c + 1],
                                                tmp_tt[:, c * TT:(c + 1) * TT],
                                                ALU.mult, ALU.add)
                    if c < C8 - 1:
                        ctx.chain(v, li)
                ctx.inc(li, "fmix")
            block.vector(_)

            def evr(s_, c, pb):
                return s_.activation(r_tt[:, c * TT:(c + 1) * TT], pb[:, 0:TT],
                                     AF.Sigmoid, bias=0.0, scale=1.0)
            emit_proj_act(ctx, "fr", wr_sb, xr_tt, (0, TT), evr,
                          xw=TT, pbanks=(psS_a, psS_b),
                          wait_before=("fmix", "fwld", "wo1_ev"))

            def evk(s_, c, pb):
                ctx.chain(s_, s_.activation(tmp_tt[:, c * TT:(c + 1) * TT], pb[:, 0:TT],
                                            AF.Relu, bias=0.0, scale=1.0))
                return s_.activation(ka_tt[:, c * TT:(c + 1) * TT], tmp_tt[:, c * TT:(c + 1) * TT],
                                     AF.Square, bias=0.0, scale=1.0)
            emit_proj_act(ctx, "fk", wk_sb, xk_tt, (0, TT), evk,
                          xw=TT, pbanks=(psS_a, psS_b),
                          wait_before=("fr_ev",))

            ctx.sem("fv_mm"), ctx.sem("fv_ev"), ctx.sem("fsil")
            fv_b = ctx.cnt["fv_mm"]
            fve_b = ctx.cnt["fv_ev"]
            fsil_b = ctx.cnt["fsil"]

            def _(t, fv_b=fv_b, fve_b=fve_b):
                ctx.wait(t, "fk_ev")
                for c in range(C8):
                    if fve_b + c - 1 > 0:
                        t.wait_ge(ctx.sem("fv_ev"), fve_b + c - 1)
                    pb = psS_a if c % 2 == 0 else psS_b
                    mi = None
                    for kc in range(C8):
                        mi = t.matmul(pb[:, 0:TT],
                                      wv_sb[:, kc * D + c * 128: kc * D + c * 128 + 128],
                                      ka_tt[:, kc * TT:(kc + 1) * TT],
                                      start=(kc == 0), stop=(kc == C8 - 1))
                    ctx.inc(mi, "fv_mm")
            block.tensor(_)

            def _(v, fv_b=fv_b):
                for c in range(C8):
                    v.wait_ge(ctx.sem("fv_mm"), fv_b + c + 1)
                    pb = psS_a if c % 2 == 0 else psS_b
                    li = v.tensor_tensor(tmp_tt[:, c * TT:(c + 1) * TT], pb[:, 0:TT],
                                         r_tt[:, c * TT:(c + 1) * TT], ALU.mult)
                    ctx.inc(li, "fv_ev")
            block.vector(_)

            def _(s_, fve_b=fve_b):
                for c in range(C8):
                    s_.wait_ge(ctx.sem("fv_ev"), fve_b + c + 1)
                    li = s_.activation(tmp_tt[:, c * TT:(c + 1) * TT], tmp_tt[:, c * TT:(c + 1) * TT],
                                       AF.Silu, bias=0.0, scale=1.0)
                    ctx.inc(li, "fsil")
            block.scalar(_)

            def _(v, s0=s0, fsil_b=fsil_b):
                ctx.wait(v, "fst")
                li = None
                for c in range(C8):
                    v.wait_ge(ctx.sem("fsil"), fsil_b + c + 1)
                    ctx.chain(v, v.tensor_tensor(tmp_tt[:, c * TT:(c + 1) * TT],
                                                 tmp_tt[:, c * TT:(c + 1) * TT],
                                                 s1[:, c * TPC + s0: c * TPC + s0 + TT], ALU.mult))
                    li = v.tensor_tensor(x1_tt[:, c * TT:(c + 1) * TT],
                                         x1_tt[:, c * TT:(c + 1) * TT],
                                         tmp_tt[:, c * TT:(c + 1) * TT], ALU.add)
                    if c < C8 - 1:
                        ctx.chain(v, li)
                ctx.inc(li, "fcomb")
            block.vector(_)

            def _(sy, s0=s0):
                ctx.wait(sy, "fcomb")
                dst = O_out.rearrange("(c p) t -> p c t", p=128)[:, :, s0:s0 + TT]
                ctx.inc(sy.dma_start(out=dst, in_=x1_tt[:].rearrange("p (c t) -> p c t", c=C8)), "fst", 16)
            block.sync(_)

        def _(sy):
            ctx.wait(sy, "fst")
        block.sync(_)
        fs.close()

    fin()
    return nc


def emit_proj_act(ctx, name, w_sb, x_sb, x_cols, out_cb, kcn=C8, nd=D, dout_chunks=C8,
                  xw=TT, pbanks=None, wait_before=()):
    nc_mm = name + "_mm"
    nc_ev = name + "_ev"
    ctx.sem(nc_mm), ctx.sem(nc_ev)
    mm_base = ctx.cnt[nc_mm]
    ev_base = ctx.cnt[nc_ev]
    s0, w = x_cols

    def _(t):
        for wsem in wait_before:
            ctx.wait(t, wsem)
        for c in range(dout_chunks):
            vtarget = ev_base + c - 1
            if vtarget > 0:
                t.wait_ge(ctx.sem(nc_ev), vtarget)
            pb = pbanks[c % 2]
            mi = None
            for kc in range(kcn):
                mi = t.matmul(pb[:, 0:w],
                              w_sb[:, kc * nd + c * 128: kc * nd + c * 128 + 128],
                              x_sb[:, kc * xw + s0: kc * xw + s0 + w],
                              start=(kc == 0), stop=(kc == kcn - 1))
            ctx.inc(mi, nc_mm)
    ctx.block.tensor(_)

    def _(s_):
        for c in range(dout_chunks):
            s_.wait_ge(ctx.sem(nc_mm), mm_base + c + 1)
            li = out_cb(s_, c, pbanks[c % 2])
            ctx.inc(li, nc_ev)
    ctx.block.scalar(_)




KOFF = 0
VOFF = D * TPC
NB = D * TPC + TPC * NH * 65
KTILE = D * TT
VTILE = TT * NH * 65
TBLK = KTILE + VTILE


def emit_exchange_zero(ctx, block, F, pfx, khs_name, vtms_name, zsb, zsf, layer):
    khs, vtms = F.io[khs_name], F.io[vtms_name]

    def _(v):
        v.memset(zsb[:], 0.0)
        ctx.inc(v.memset(zsf[:], 0.0), pfx + "zero")
    block.vector(_)

    def _(g):
        ctx.wait(g, pfx + "zero")
        ctx.inc(g.dma_start(
            out=khs.rearrange("(c p) s -> p c s", p=128)[:, :, 0:512],
            in_=zsb[:].rearrange("p (c t) -> p c t", c=C8)[:, :, 0:512]), pfx + "asm", 16)
        ctx.inc(g.dma_start(
            out=vtms[0:512, :].rearrange("(tc p) f -> p tc f", p=128),
            in_=zsb[:].rearrange("p (tc f) -> p tc f", tc=4)), pfx + "asm", 16)
        if layer == 1:
            xp = F.io["xnprev"]
            ctx.inc(g.dma_start(
                out=bass.AP(tensor=xp.tensor, offset=xp.offset,
                            ap=[[8, 128], [1, 8]]),
                in_=zsf[:, 0:8]), pfx + "asm", 16)
    block.gpsimd(_)


def emit_exchange_tile(ctx, block, F, pfx, t, ccin_pfx, khs_name, vtms_name, qm, vm):
    nc = ctx.nc
    khs, vtms = F.io[khs_name], F.io[vtms_name]
    cin = F.io[f"{ccin_pfx}in{t}"]
    cout = F.io[f"{ccin_pfx}out{t}"]
    zcnt = 32 if pfx == "e0" else 48

    def _(g):
        g.wait_ge(ctx.sem("qhst"), qm)
        g.wait_ge(ctx.sem("vtst"), vm)
        g.wait_ge(ctx.sem(pfx + "asm"), zcnt)
        ctx.inc(g.collective_compute(
            "AllGather", ALU.bypass,
            replica_groups=[[0, 1], [2, 3], [4, 5], [6, 7]],
            ins=[cin.opt()], outs=[cout.opt()]), pfx + "cct", 1)
        ctx.inc(g.dma_start(
            out=bass.AP(tensor=khs.tensor, offset=khs.offset + 512 + 1024 * t,
                        ap=[[NSLOT, D], [1, 512]]),
            in_=bass.AP(tensor=cin.tensor, offset=cin.offset,
                        ap=[[512, D], [1, 512]])), pfx + "asm", 16)
        ctx.inc(g.dma_start(
            out=bass.AP(tensor=vtms.tensor, offset=vtms.offset + (512 + 1024 * t) * 1040,
                        ap=[[1, VTILE]]),
            in_=bass.AP(tensor=cin.tensor, offset=cin.offset + KTILE,
                        ap=[[1, VTILE]])), pfx + "asm", 16)
    block.gpsimd(_)

    def _(g):
        for arm in g.switch_core_id(n=128):
            av = arm.value if hasattr(arm, "value") else int(arm)
            par = av % 2
            prow = TBLK if par == 0 else 0
            if par == 0:
                if t >= 3:
                    continue
                g.wait_ge(ctx.sem(pfx + "asm"), zcnt)
                g.wait_ge(ctx.sem(pfx + "cct"), t + 1)
                kdst = khs.offset + 1024 * (t + 1)
                vdst = vtms.offset + (1024 * (t + 1)) * 1040
                if t < 2:
                    g.dma_start(
                        out=bass.AP(tensor=khs.tensor, offset=kdst,
                                    ap=[[NSLOT, D], [1, 512]]),
                        in_=bass.AP(tensor=cout.tensor, offset=prow,
                                    ap=[[512, D], [1, 512]])).then_inc(ctx.sem(pfx + "asm2"), 16)
                    g.dma_start(
                        out=bass.AP(tensor=vtms.tensor, offset=vdst,
                                    ap=[[1, VTILE]]),
                        in_=bass.AP(tensor=cout.tensor, offset=prow + KTILE,
                                    ap=[[1, VTILE]])).then_inc(ctx.sem(pfx + "asm2"), 16)
                else:
                    for h in range(2):
                        g.dma_start(
                            out=bass.AP(tensor=khs.tensor, offset=kdst + 512 * h * NSLOT,
                                        ap=[[NSLOT, 512], [1, 512]]),
                            in_=bass.AP(tensor=cout.tensor, offset=prow + 512 * h * 512,
                                        ap=[[512, 512], [1, 512]])).then_inc(ctx.sem(pfx + "asm2"), 16)
                        g.dma_start(
                            out=bass.AP(tensor=vtms.tensor, offset=vdst + h * (VTILE // 2),
                                        ap=[[1, VTILE // 2]]),
                            in_=bass.AP(tensor=cout.tensor, offset=prow + KTILE + h * (VTILE // 2),
                                        ap=[[1, VTILE // 2]])).then_inc(ctx.sem(pfx + "asm2"), 16)
            else:
                g.wait_ge(ctx.sem(pfx + "asm"), zcnt)
                g.wait_ge(ctx.sem(pfx + "cct"), t + 1)
                g.dma_start(
                    out=bass.AP(tensor=khs.tensor, offset=khs.offset + 1024 * t,
                                ap=[[NSLOT, D], [1, 512]]),
                    in_=bass.AP(tensor=cout.tensor, offset=prow,
                                ap=[[512, D], [1, 512]])).then_inc(ctx.sem(pfx + "asm2"), 16)
                g.dma_start(
                    out=bass.AP(tensor=vtms.tensor, offset=vtms.offset + (1024 * t) * 1040,
                                ap=[[1, VTILE]]),
                    in_=bass.AP(tensor=cout.tensor, offset=prow + KTILE,
                                ap=[[1, VTILE]])).then_inc(ctx.sem(pfx + "asm2"), 16)
        ctx.sem(pfx + "asm2")
        ctx.cnt[_SEM_ALIAS.get(pfx + "asm2", pfx + "asm2")] += 32
    block.gpsimd(_)


def emit_exchange(ctx, nc, pfx, cfin, cfout, extra_out, layer, sidecar_entry):
    es = ExitStack()
    old_scope = ctx.scope
    ctx.scope = es
    block = es.enter_context(nc.Block())
    ctx.block = block

    def _(g):
        for s_ in sidecar_entry:
            ctx.wait(g, s_)
        ctx.inc(g.collective_compute(
            "AllGather", ALU.bypass,
            replica_groups=[[0, 1], [2, 3], [4, 5], [6, 7]],
            ins=[cfin.opt()], outs=[cfout.opt()]), pfx + "ccf", 1)
    block.gpsimd(_)

    fbase = cfout.offset

    if layer == 0:
        def _(sy):
            ctx.wait(sy, pfx + "ccf")
            ctx.inc(sy.dma_start(
                out=bass.AP(tensor=extra_out.tensor, offset=extra_out.offset, ap=[[1, 2048]]),
                in_=bass.AP(tensor=cfout.tensor, offset=fbase, ap=[[1, 2048]])), pfx + "asm", 16)
        block.sync(_)
    else:
        def _(sy):
            for arm in sy.switch_core_id(n=128):
                av = arm.value if hasattr(arm, "value") else int(arm)
                par = av % 2
                sy.wait_ge(ctx.sem(pfx + "asm"), 48)
                sy.wait_ge(ctx.sem(pfx + "ccf"), 1)
                if par == 0:
                    sy.dma_start(
                        out=bass.AP(tensor=extra_out.tensor, offset=extra_out.offset + 1024,
                                    ap=[[1, 3072]]),
                        in_=bass.AP(tensor=cfout.tensor, offset=fbase + 4096,
                                    ap=[[1, 3072]])).then_inc(ctx.sem(pfx + "asm2"), 16)
                else:
                    sy.dma_start(
                        out=bass.AP(tensor=extra_out.tensor, offset=extra_out.offset,
                                    ap=[[1, 4096]]),
                        in_=bass.AP(tensor=cfout.tensor, offset=fbase,
                                    ap=[[1, 4096]])).then_inc(ctx.sem(pfx + "asm2"), 16)
            ctx.sem(pfx + "asm2")
            ctx.cnt[_SEM_ALIAS.get(pfx + "asm2", pfx + "asm2")] += 16
        block.sync(_)

    es.close()
    ctx.scope = old_scope


def build_fused(dbg=False):
    nc = bass.Bass("TRN2", target_bir_lowering=False, debug=False)
    io = {}

    def ext(name, shape, dtp):
        io[name] = nc.dram_tensor(name, shape, dtp, kind="ExternalInput").ap()

    def internal(name, shape, dtp):
        t = nc.dram_tensor(name, shape, dtp, kind="Internal").ap()
        io[name] = t
        return t

    ext("xT", [D, TPC], F32)
    ext("Wq", [D, D], F32); ext("Wk", [D, D], F32); ext("Wv", [D, D], F32)
    ext("ln1w", [128, C8], F32); ext("selw", [D, NH], F32); ext("selT", [NH, D], F32)
    ext("eyeb", [128, 128], BF16)
    ext("scaler", [1, 16], F32); ext("scalec", [16, 1], F32); ext("relb", [32, 16], F32)
    ext("OH", [32, 112], F32); ext("J128", [128, 128], F32); ext("eyef", [128, 128], F32)
    ext("alpha", [128, 1], F32); ext("beta", [128, 1], F32); ext("Wo", [D, D], F32)
    ext("ln2w", [128, C8], F32); ext("W1", [D, 4 * D], F32); ext("W2", [4 * D, D], F32)
    ext("gateT", [D, 8], F32)
    ext("mw1T", [8 * D, D], F32); ext("mw2T", [8 * D, D], F32); ext("mw3T", [8 * D, D], F32)
    ext("B1", [128, 64], F32); ext("B2", [128, 64], F32); ext("B3", [128, 64], F32)
    ext("sw1T", [D, 2 * D], F32); ext("sw2T", [D, 2 * D], F32); ext("sw3T", [2 * D, D], F32)
    ext("iota8", [128, 8], F32)
    ext("TRI", [128, 128], F32)
    ext("eyebm", [128, 128], BF16)
    ext("Wq1", [D, D], F32); ext("Wk1", [D, D], F32); ext("Wv1", [D, D], F32)
    ext("bq1", [128, C8], F32); ext("bk1", [128, C8], F32); ext("bv1", [128, C8], F32)
    ext("eyeb1", [128, 128], BF16)
    ext("Wo1", [D, D], F32); ext("bo1", [128, C8], F32)
    ext("WrT", [D, D], F32); ext("WkT", [D, D], F32); ext("WvT", [D, D], F32)
    ext("mur", [128, C8], F32); ext("muk", [128, C8], F32); ext("mask1", [128, 128], F32)

    internal("xn", [D, TPC], F32)
    internal("qh", [D, TPC], BF16)
    cc0t = []
    for t in range(NTT):
        cin = internal(f"cc0in{t}", [1, TBLK], BF16)
        cout = internal(f"cc0out{t}", [2, TBLK], BF16)
        cc0t.append((cin, cout))
        io[f"kh_t{t}"] = bass.AP(tensor=cin.tensor, offset=0, ap=[[TT, D], [1, TT]])
        io[f"vt_t{t}"] = bass.AP(tensor=cin.tensor, offset=KTILE, ap=[[NH * 65, TT], [1, NH * 65]])
    io["kh"] = io["kh_t0"]
    io["vt"] = io["vt_t0"]
    ccf0in = internal("ccf0in", [1, 1024], F32)
    ccf0out = internal("ccf0out", [2, 1024], F32)
    io["vsum"] = bass.AP(tensor=ccf0in.tensor, offset=0, ap=[[1, 1024]])
    khs = internal("khs", [D, NSLOT], BF16)
    vtms = internal("vtms", [NSLOT, NH * 65], BF16)
    vs2f = internal("vsum2f", [1, 2048], F32)
    io["vsum2"] = bass.AP(tensor=vs2f.tensor, offset=0, ap=[[1, 2048]])
    internal("gscratch", [16, 384], F32)
    internal("y", [D, TPC], F32)
    internal("s_scr", [D, TPC], F32)
    internal("xn_tm", [TPC, D], BF16)
    internal("xbkt", [8 * 768, D], BF16)
    internal("prodbkt", [8 * 768, D], BF16)
    internal("x1", [D, TPC], F32)
    io["x1T"] = io["x1"]
    internal("xn1", [D, TPC], F32)
    internal("Q1", [D, TPC], BF16)
    cc1t = []
    for t in range(NTT):
        cin = internal(f"cc1in{t}", [1, TBLK], BF16)
        cout = internal(f"cc1out{t}", [2, TBLK], BF16)
        cc1t.append((cin, cout))
        io[f"K1_t{t}"] = bass.AP(tensor=cin.tensor, offset=0, ap=[[TT, D], [1, TT]])
        io[f"V1t_t{t}"] = bass.AP(tensor=cin.tensor, offset=KTILE, ap=[[NH * 65, TT], [1, NH * 65]])
    io["K1"] = io["K1_t0"]
    io["V1t"] = io["V1t_t0"]
    ccf1in = internal("ccf1in", [1, 4096], F32)
    ccf1out = internal("ccf1out", [2, 4096], F32)
    K1s = internal("K1s", [D, NSLOT], BF16)
    V1ts = internal("V1ts", [NSLOT, NH * 65], BF16)
    xnprev = internal("xnprev", [1, 4096], F32)
    io["xnprev"] = xnprev
    io["out"] = nc.dram_tensor("out", [D, TPC], F32, kind="ExternalOutput").ap()

    es = ExitStack()
    with es:
        ctx = Ctx(nc, es, None)
        with nc.named_scope("s1_ln_qkv"):
            build_launch1(F=Fused(nc, ctx, io, entry=()))
        with nc.named_scope("e0_exchange"):
            emit_exchange(ctx, nc, "e0", ccf0in, ccf0out,
                          io["vsum2"], 0, sidecar_entry=("vsst",))
        with nc.named_scope("s2_attn0"):
            build_launch2(F=Fused(nc, ctx, io, entry=("vsst",)))
        with nc.named_scope("s3_mlp_moe"):
            build_launch3(F=Fused(nc, ctx, io, entry=("yst_st",)))
        with nc.named_scope("s4_ln1_qkv"):
            build_launch4(F=Fused(nc, ctx, io, entry=("fst",)))
        with nc.named_scope("e1_exchange"):
            emit_exchange(ctx, nc, "e1", ccf1in, ccf1out,
                          xnprev, 1, sidecar_entry=("xnst",))
        with nc.named_scope("s5_attn1_ffn"):
            build_launch5(F=Fused(nc, ctx, io,
                                  entry=("qhst", "vtst", "xnst")))
        if dbg:
            dbgs = [("xn", [D, TPC], F32), ("qh", [D, TPC], BF16),
                    ("khs", [D, NSLOT], BF16), ("vtms", [NSLOT, NH * 65], BF16),
                    ("vsum2f", [1, 2048], F32), ("y", [D, TPC], F32),
                    ("x1", [D, TPC], F32), ("xn1", [D, TPC], F32),
                    ("Q1", [D, TPC], BF16), ("K1s", [D, NSLOT], BF16),
                    ("V1ts", [NSLOT, NH * 65], BF16), ("xnprev", [1, 4096], F32),
                    ("s_scr", [D, TPC], F32), ("gatesT", [32, TPC], F32)]
            block = es.enter_context(nc.Block())
            ctx.block = block

            def _(sy):
                ctx.wait(sy, "fst")
                for nm, shp, dtp in dbgs:
                    o = nc.dram_tensor("dbg_" + nm, shp, dtp, kind="ExternalOutput").ap()
                    ctx.inc(sy.dma_start(out=o, in_=io[nm] if nm not in ("vsum2f",) else
                                         bass.AP(tensor=io["vsum2"].tensor, offset=0, ap=[[2048, 1], [1, 2048]])),
                            "dbgst", 16)
                ctx.wait(sy, "dbgst")
            block.sync(_)
    return nc


def shard_tokens(core):
    par = core % 2
    return np.concatenate([np.arange(512 * (2 * j + par), 512 * (2 * j + par) + 512)
                           for j in range(4)])


def head_selector():
    sel = np.zeros((D, NH), np.float32)
    sel[np.arange(D), np.arange(D) // HD] = 1.0
    return sel


def launch1_inputs(x, Wq, Wk, Wv, ln1_w):
    sel = head_selector()
    eye = np.eye(128).astype(bf16)
    maps = []
    for c in range(8):
        b = c // 2
        toks = shard_tokens(c)
        xT = np.ascontiguousarray(x[b][toks].T)
        ln1 = np.ascontiguousarray(np.asarray(ln1_w, np.float32).reshape(C8, 128).T)
        maps.append(dict(xT=xT, Wq=Wq, Wk=Wk, Wv=Wv, ln1w=ln1,
                         selw=sel, selT=np.ascontiguousarray(sel.T), eyeb=eye))
    return maps


def _bucket_fn(d):
    import math
    d = np.asarray(d)
    rpn = np.maximum(d, 0)
    max_exact = 16
    safe = np.maximum(rpn, 1).astype(np.float32)
    large = max_exact + (np.log(safe / max_exact) / math.log(128 / max_exact) * 16).astype(np.int32)
    large = np.minimum(large, 31)
    return np.where(rpn < max_exact, rpn, large)


def launch2_consts():
    OH = np.zeros((32, 112), np.float32)
    for d in range(1, 113):
        OH[_bucket_fn(d), d - 1] = 1.0
        OH[31, d - 1] -= 1.0
    J = np.zeros((128, 128), np.float32)
    J[np.arange(128), 127 - np.arange(128)] = 1.0
    eyef = np.eye(128, dtype=np.float32)
    return OH, J, eyef


def launch2_inputs(l1outs, inp):
    OH, J, eyef = launch2_consts()
    scale = np.asarray(inp["L0_attn_scale"], np.float32)
    relb = np.asarray(inp["L0_attn_rel_bias"], np.float32)
    Wo = np.asarray(inp["L0_attn_Wo"], np.float32)
    maps = []
    for c in range(8):
        par = c % 2
        pc = c ^ 1
        toks = shard_tokens(c)
        ptoks = shard_tokens(pc)
        khG = np.zeros((D, NSLOT), dtype=bf16)
        khG[:, toks] = l1outs[c]["kh"]
        khG[:, ptoks] = l1outs[pc]["kh"]
        vtG = np.zeros((NSLOT, NH * 65), dtype=bf16)
        vtG[toks] = l1outs[c]["vt"]
        vtG[ptoks] = l1outs[pc]["vt"]
        if par == 0:
            khs = np.zeros((D, NSLOT), dtype=bf16)
            khs[:, 512:] = khG[:, :NSLOT - 512]
            vtms = np.zeros((NSLOT, NH * 65), dtype=bf16)
            vtms[512:] = vtG[:NSLOT - 512]
        else:
            khs, vtms = khG, vtG
        vs2 = np.concatenate([l1outs[c]["vsum"], l1outs[pc]["vsum"]], axis=1)
        alpha = np.full((128, 1), 0.0 if par == 0 else 1.0, np.float32)
        beta = np.full((128, 1), 1.0 if par == 0 else 0.0, np.float32)
        maps.append(dict(xn=l1outs[c]["xn"], qh=l1outs[c]["qh"],
                         khs=np.ascontiguousarray(khs), vtms=np.ascontiguousarray(vtms),
                         vsum2=np.ascontiguousarray(vs2),
                         scaler=scale.reshape(1, 16), scalec=np.ascontiguousarray(scale.reshape(16, 1)),
                         relb=relb, OH=OH, J128=J, eyef=eyef,
                         alpha=alpha, beta=beta, Wo=Wo))
    return maps


def launch3_inputs(x, l1outs, l2outs, inp):
    OHn, J, eyef = launch2_consts()
    iota8 = np.tile(np.arange(8, dtype=np.float32), (128, 1))
    ln2 = np.ascontiguousarray(np.asarray(inp["L0_attn_ln2_w"], np.float32).reshape(C8, 128).T)
    W1 = np.asarray(inp["L0_attn_W1"], np.float32)
    W2 = np.asarray(inp["L0_attn_W2"], np.float32)
    gateT = np.ascontiguousarray(np.asarray(inp["L0_ffn_gate_w"], np.float32).T)
    w1 = np.asarray(inp["L0_ffn_w1"], np.float32)
    w2 = np.asarray(inp["L0_ffn_w2"], np.float32)
    w3 = np.asarray(inp["L0_ffn_w3"], np.float32)
    mw1T = np.ascontiguousarray(w1.transpose(0, 2, 1)).reshape(8 * D, D)
    mw2T = np.ascontiguousarray(w2.transpose(0, 2, 1)).reshape(8 * D, D)
    mw3T = np.ascontiguousarray(w3.transpose(0, 2, 1)).reshape(8 * D, D)
    def blay(b):
        return np.ascontiguousarray(np.asarray(b, np.float32).reshape(8, 8, 128).transpose(2, 0, 1).reshape(128, 64))
    B1, B2, B3 = blay(inp["L0_ffn_b1"]), blay(inp["L0_ffn_b2"]), blay(inp["L0_ffn_b3"])
    sw1T = np.ascontiguousarray(np.asarray(inp["L0_ffn_sw1"], np.float32).T)
    sw2T = np.ascontiguousarray(np.asarray(inp["L0_ffn_sw2"], np.float32).T)
    sw3T = np.ascontiguousarray(np.asarray(inp["L0_ffn_sw3"], np.float32).T)
    TRI = np.ascontiguousarray(np.triu(np.ones((128, 128), np.float32), 1))
    eyebm = np.eye(128).astype(bf16)
    maps = []
    for c in range(8):
        b = c // 2
        toks = shard_tokens(c)
        xT = np.ascontiguousarray(x[b][toks].T)
        maps.append(dict(xT=xT, xn=l1outs[c]["xn"], y=l2outs[c]["y"], ln2w=ln2,
                         W1=W1, W2=W2, gateT=gateT, mw1T=mw1T, mw2T=mw2T, mw3T=mw3T,
                         B1=B1, B2=B2, B3=B3, sw1T=sw1T, sw2T=sw2T, sw3T=sw3T,
                         eyef=eyef, iota8=iota8, xn_tm=l1outs[c]["xn_tm"],
                         TRI=TRI, eyebm=eyebm))
    return maps


def launch4_inputs(l3outs, inp):
    eye = np.eye(128).astype(bf16)
    def cat(w):
        return np.ascontiguousarray(np.asarray(w, np.float32).transpose(1, 0, 2).reshape(D, D))
    def bl(b):
        return np.ascontiguousarray(np.asarray(b, np.float32).reshape(-1).reshape(C8, 128).T)
    Wq1, Wk1, Wv1 = cat(inp["L1_attn_Wq"]), cat(inp["L1_attn_Wk"]), cat(inp["L1_attn_Wv"])
    bq1, bk1, bv1 = bl(inp["L1_attn_bq"]), bl(inp["L1_attn_bk"]), bl(inp["L1_attn_bv"])
    maps = []
    for c in range(8):
        maps.append(dict(x1T=l3outs[c]["x1"], Wq1=Wq1, Wk1=Wk1, Wv1=Wv1,
                         bq1=bq1, bk1=bk1, bv1=bv1, eyeb=eye))
    return maps


def launch5_inputs(l3outs, l4outs, inp):
    Wo1 = np.asarray(inp["L1_attn_Wo"], np.float32)
    bo1 = np.ascontiguousarray(np.asarray(inp["L1_attn_bo"], np.float32).reshape(C8, 128).T)
    WrT = np.ascontiguousarray(np.asarray(inp["L1_ffn_W_r"], np.float32).T)
    WkT = np.ascontiguousarray(np.asarray(inp["L1_ffn_W_k"], np.float32).T)
    WvT = np.ascontiguousarray(np.asarray(inp["L1_ffn_W_v"], np.float32).T)
    mur = np.ascontiguousarray(np.asarray(inp["L1_ffn_mu_r"], np.float32).reshape(C8, 128).T)
    muk = np.ascontiguousarray(np.asarray(inp["L1_ffn_mu_k"], np.float32).reshape(C8, 128).T)
    mask1 = np.zeros((128, 128), np.float32)
    for kk in range(128):
        mask1[kk, :kk] = -BIG
    maps = []
    for c in range(8):
        par = c % 2
        pc = c ^ 1
        toks = shard_tokens(c)
        ptoks = shard_tokens(pc)
        khG = np.zeros((D, NSLOT), dtype=bf16)
        khG[:, toks] = l4outs[c]["K1"]
        khG[:, ptoks] = l4outs[pc]["K1"]
        vtG = np.zeros((NSLOT, NH * 65), dtype=bf16)
        vtG[toks] = l4outs[c]["V1t"]
        vtG[ptoks] = l4outs[pc]["V1t"]
        if par == 0:
            khs = np.zeros((D, NSLOT), dtype=bf16)
            khs[:, 512:] = khG[:, :NSLOT - 512]
            vtms = np.zeros((NSLOT, NH * 65), dtype=bf16)
            vtms[512:] = vtG[:NSLOT - 512]
        else:
            khs, vtms = khG, vtG
        xnprev = np.zeros((D, 4), np.float32)
        for j in range(4):
            g0 = 512 * (2 * j + par)
            if g0 == 0:
                continue
            i = (j - 1) if par == 0 else j
            xnprev[:, j] = l4outs[pc]["xn1"][:, i * 512 + 511]
        maps.append(dict(x1T=l3outs[c]["x1"], xn1=l4outs[c]["xn1"], xnprev=xnprev,
                         Q1=l4outs[c]["Q1"], K1s=np.ascontiguousarray(khs),
                         V1ts=np.ascontiguousarray(vtms),
                         Wo1=Wo1, bo1=bo1, WrT=WrT, WkT=WkT, WvT=WvT,
                         mur=mur, muk=muk, mask1=mask1))
    return maps


def fused_inputs(inputs):
    x = np.ascontiguousarray(np.asarray(inputs["L0_attn_x"], np.float32))
    OHc, J, eyef = launch2_consts()
    sel = head_selector()
    eye = np.eye(128).astype(bf16)
    scale = np.asarray(inputs["L0_attn_scale"], np.float32)
    iota8 = np.tile(np.arange(8, dtype=np.float32), (128, 1))
    mask1 = np.zeros((128, 128), np.float32)
    for kk in range(128):
        mask1[kk, :kk] = -BIG

    def colv(v):
        return np.ascontiguousarray(np.asarray(v, np.float32).reshape(C8, 128).T)

    def cat(w):
        return np.ascontiguousarray(np.asarray(w, np.float32).transpose(1, 0, 2).reshape(D, D))

    def blay(b):
        return np.ascontiguousarray(np.asarray(b, np.float32).reshape(8, 8, 128).transpose(2, 0, 1).reshape(128, 64))

    com = dict(
        Wq=np.asarray(inputs["L0_attn_Wq"], np.float32),
        Wk=np.asarray(inputs["L0_attn_Wk"], np.float32),
        Wv=np.asarray(inputs["L0_attn_Wv"], np.float32),
        ln1w=colv(inputs["L0_attn_ln1_w"]), selw=sel,
        selT=np.ascontiguousarray(sel.T), eyeb=eye,
        scaler=scale.reshape(1, 16),
        scalec=np.ascontiguousarray(scale.reshape(16, 1)),
        relb=np.asarray(inputs["L0_attn_rel_bias"], np.float32),
        OH=OHc, J128=J, eyef=eyef,
        Wo=np.asarray(inputs["L0_attn_Wo"], np.float32),
        ln2w=colv(inputs["L0_attn_ln2_w"]),
        W1=np.asarray(inputs["L0_attn_W1"], np.float32),
        W2=np.asarray(inputs["L0_attn_W2"], np.float32),
        gateT=np.ascontiguousarray(np.asarray(inputs["L0_ffn_gate_w"], np.float32).T),
        mw1T=np.ascontiguousarray(np.asarray(inputs["L0_ffn_w1"], np.float32).transpose(0, 2, 1)).reshape(8 * D, D),
        mw2T=np.ascontiguousarray(np.asarray(inputs["L0_ffn_w2"], np.float32).transpose(0, 2, 1)).reshape(8 * D, D),
        mw3T=np.ascontiguousarray(np.asarray(inputs["L0_ffn_w3"], np.float32).transpose(0, 2, 1)).reshape(8 * D, D),
        B1=blay(inputs["L0_ffn_b1"]), B2=blay(inputs["L0_ffn_b2"]), B3=blay(inputs["L0_ffn_b3"]),
        sw1T=np.ascontiguousarray(np.asarray(inputs["L0_ffn_sw1"], np.float32).T),
        sw2T=np.ascontiguousarray(np.asarray(inputs["L0_ffn_sw2"], np.float32).T),
        sw3T=np.ascontiguousarray(np.asarray(inputs["L0_ffn_sw3"], np.float32).T),
        iota8=iota8,
        TRI=np.ascontiguousarray(np.triu(np.ones((128, 128), np.float32), 1)),
        eyebm=eye,
        Wq1=cat(inputs["L1_attn_Wq"]), Wk1=cat(inputs["L1_attn_Wk"]), Wv1=cat(inputs["L1_attn_Wv"]),
        bq1=colv(np.asarray(inputs["L1_attn_bq"], np.float32).reshape(-1)),
        bk1=colv(np.asarray(inputs["L1_attn_bk"], np.float32).reshape(-1)),
        bv1=colv(np.asarray(inputs["L1_attn_bv"], np.float32).reshape(-1)),
        eyeb1=eye,
        Wo1=np.asarray(inputs["L1_attn_Wo"], np.float32),
        bo1=colv(inputs["L1_attn_bo"]),
        WrT=np.ascontiguousarray(np.asarray(inputs["L1_ffn_W_r"], np.float32).T),
        WkT=np.ascontiguousarray(np.asarray(inputs["L1_ffn_W_k"], np.float32).T),
        WvT=np.ascontiguousarray(np.asarray(inputs["L1_ffn_W_v"], np.float32).T),
        mur=colv(inputs["L1_ffn_mu_r"]), muk=colv(inputs["L1_ffn_mu_k"]),
        mask1=mask1,
    )
    maps = []
    for c in range(8):
        b = c // 2
        par = c % 2
        toks = shard_tokens(c)
        m = dict(com)
        m["xT"] = np.ascontiguousarray(x[b][toks].T)
        m["alpha"] = np.full((128, 1), 0.0 if par == 0 else 1.0, np.float32)
        m["beta"] = np.full((128, 1), 1.0 if par == 0 else 0.0, np.float32)
        maps.append(m)
    return maps


def kernel_fused(**inputs):
    nc = _get_nc("fused", build_fused)
    maps = fused_inputs(inputs)
    res = run_bass_kernel_spmd(nc, maps, core_ids=list(range(8)))
    out = np.empty((4, 4096, 1024), np.float32)
    for c in range(8):
        b = c // 2
        toks = shard_tokens(c)
        out[b][toks] = np.asarray(res.results[c]["out"]).T
    return out


_NC_CACHE = {}


def _get_nc(name, builder):
    if name not in _NC_CACHE:
        _NC_CACHE[name] = builder()
    return _NC_CACHE[name]


import os as _os


def kernel(**inputs):
    if _os.environ.get("MK_UNFUSED", "") != "1":
        return kernel_fused(**inputs)
    x = np.ascontiguousarray(np.asarray(inputs["L0_attn_x"], np.float32))

    nc1 = _get_nc("l1", build_launch1)
    maps1 = launch1_inputs(x, np.asarray(inputs["L0_attn_Wq"], np.float32),
                           np.asarray(inputs["L0_attn_Wk"], np.float32),
                           np.asarray(inputs["L0_attn_Wv"], np.float32),
                           np.asarray(inputs["L0_attn_ln1_w"], np.float32))
    r1 = run_bass_kernel_spmd(nc1, maps1, core_ids=list(range(8)))
    l1outs = [{k: np.asarray(r1.results[i][k]) for k in ("xn", "qh", "kh", "vt", "vsum", "xn_tm")}
              for i in range(8)]

    nc2 = _get_nc("l2", build_launch2)
    maps2 = launch2_inputs(l1outs, inputs)
    r2 = run_bass_kernel_spmd(nc2, maps2, core_ids=list(range(8)))
    l2outs = [{"y": np.asarray(r2.results[i]["y"])} for i in range(8)]

    nc3 = _get_nc("l3", build_launch3)
    maps3 = launch3_inputs(x, l1outs, l2outs, inputs)
    r3 = run_bass_kernel_spmd(nc3, maps3, core_ids=list(range(8)))
    l3outs = [{"x1": np.asarray(r3.results[i]["x1"])} for i in range(8)]

    nc4 = _get_nc("l4", build_launch4)
    maps4 = launch4_inputs(l3outs, inputs)
    r4 = run_bass_kernel_spmd(nc4, maps4, core_ids=list(range(8)))
    l4outs = [{k: np.asarray(r4.results[i][k]) for k in ("xn1", "Q1", "K1", "V1t")}
              for i in range(8)]

    nc5 = _get_nc("l5", build_launch5)
    maps5 = launch5_inputs(l3outs, l4outs, inputs)
    r5 = run_bass_kernel_spmd(nc5, maps5, core_ids=list(range(8)))

    out = np.empty((4, 4096, 1024), np.float32)
    for c in range(8):
        b = c // 2
        toks = shard_tokens(c)
        out[b][toks] = np.asarray(r5.results[c]["out"]).T
    return out


# version 18 (final)
# speedup vs baseline: 1.0130x; 1.0130x over previous
import numpy as np
import ml_dtypes
from contextlib import ExitStack

import concourse.bass as bass
import concourse.mybir as mybir
from concourse.bass_utils import run_bass_kernel_spmd

F32 = mybir.dt.float32
BF16 = mybir.dt.bfloat16
AF = mybir.ActivationFunctionType
ALU = mybir.AluOpType
AX = mybir.AxisListType.X

D = 1024
C8 = 8
TPC = 2048
NTT = 4
TT = 512
NH = 16
HD = 64
NSLOT = 4096
BIG = 1.0e30

bf16 = ml_dtypes.bfloat16


_SEM_GROUPS = {
    "tms": ["mm_stats", "mm_rms", "bc_ready", "rmsbc_ready", "ss_mm", "mm_band",
            "mm_bandT", "mm_negmT", "mm_bc1", "mm_bc2", "mm_wt", "mm_rb",
            "glog_mm", "zbc", "zrms_mm"],
    "vms": ["vinit", "var_ready", "ab_ready", "xn_ready", "rms_ready",
            "xn2_ready", "rs_ready", "band_sb", "negm", "rowtmp_sb", "mtscl",
            "setup_v", "wt_done", "gmax", "goh", "recip",
            "patch_done", "vt_ready", "z2_done", "zrstd", "om_done",
            "fmix", "fcomb", "e0zero", "e1zero", "xbt_ready", "gcarup",
            "gpos", "gsc1", "ysumv", "gcar"],
    "sms": ["sqk", "sqk2", "sqk3", "std_ready", "rms_std", "ss_sqrt", "sc_abs",
            "gsm", "zsq", "zstd", "fsil3"],
    "dmisc": ["cld", "gwld", "bld", "gst", "muld", "hank", "dbgst"],
}
_SEM_ALIAS = {n: g for g, names in _SEM_GROUPS.items() for n in names}


class Ctx:

    def __init__(self, nc, es, block):
        self.nc, self.es, self.block = nc, es, block
        self.scope = es
        self.sems = {}
        self.cnt = {}
        self.xmarks = {}

    def sem(self, name):
        name = _SEM_ALIAS.get(name, name)
        if name not in self.sems:
            self.sems[name] = self.es.enter_context(self.nc.semaphore(name))
            self.cnt[name] = 0
        return self.sems[name]

    def inc(self, inst, name, k=1):
        name = _SEM_ALIAS.get(name, name)
        s = self.sem(name)
        inst.then_inc(s, k)
        self.cnt[name] += k
        return inst

    def wait(self, eng, name):
        name = _SEM_ALIAS.get(name, name)
        s = self.sem(name)
        if self.cnt[name] > 0:
            eng.wait_ge(s, self.cnt[name])

    def cnt_of(self, name):
        name = _SEM_ALIAS.get(name, name)
        self.sem(name)
        return self.cnt[name]

    def chain(self, eng, inst, name=None):
        if name is None:
            name = "chn_" + str(getattr(eng, "engine", "x")).split(".")[-1]
        self.inc(inst, name)
        eng.wait_ge(self.sem(name), self.cnt[name])


def sb(ctx, name, p, f, dt=F32):
    return ctx.scope.enter_context(ctx.nc.sbuf_tensor(name, [p, f], dt,
                                                      allow_name_mangling=True))


def psum(ctx, name, p, f, dt=F32):
    return ctx.scope.enter_context(ctx.nc.psum_tensor(name, [p, f], dt,
                                                      allow_name_mangling=True))


class Fused:

    def __init__(self, nc, ctx, io, entry=()):
        self.nc, self.ctx, self.io, self.entry = nc, ctx, io, entry


def stage_begin(F, builder_locals):
    if F is None:
        nc = bass.Bass("TRN2", target_bir_lowering=False, debug=False)
        es = ExitStack()
        block = es.enter_context(nc.Block())
        ctx = Ctx(nc, es, block)

        def dt(name, shape, dtp, kind):
            return nc.dram_tensor(name, shape, dtp, kind=kind).ap()

        def fin():
            es.close()
        return nc, es, block, ctx, dt, fin
    nc, ctx = F.nc, F.ctx
    es = ExitStack()
    old_scope = ctx.scope
    ctx.scope = es
    block = es.enter_context(nc.Block())
    ctx.block = block

    def dt(name, shape, dtp, kind):
        return F.io[name]

    def fin():
        es.close()
        ctx.scope = old_scope

    if F.entry:
        def _barrier(e):
            for s_ in F.entry:
                ctx.wait(e, s_)
        for en in ("sync", "gpsimd", "tensor", "vector", "scalar"):
            getattr(block, en)(_barrier)
    return nc, es, block, ctx, dt, fin


def emit_weight_load(ctx, g, w_dram, w_sb, sem="wld"):
    Kd, Nd = w_dram.shape
    kcn = Kd // 128
    src = w_dram.rearrange("(kc p) n -> p kc n", p=128)
    dst = w_sb[:].rearrange("p (kc n) -> p kc n", kc=kcn)
    ctx.inc(g.dma_start(out=dst, in_=src), sem, 16)


def emit_proj(ctx, name, w_sb, x_sb, x_cols, out_cb, kcn=C8, nd=D, dout_chunks=C8,
              xw=TT * NTT // NTT, pbanks=None, wait_before=(), wait_before_ev=(),
              extra_mm=None):
    nc_mm = name + "_mm"
    nc_ev = name + "_ev"
    mm_base = ctx.cnt[nc_mm] if nc_mm in ctx.cnt else 0
    ev_base = ctx.cnt[nc_ev] if nc_ev in ctx.cnt else 0
    ctx.sem(nc_mm), ctx.sem(nc_ev)
    mm_base = ctx.cnt[nc_mm]
    ev_base = ctx.cnt[nc_ev]
    s0, w = x_cols

    def _(t):
        for wsem in wait_before:
            ctx.wait(t, wsem)
        for c in range(dout_chunks):
            vtarget = ev_base + c - 1
            if vtarget > 0:
                t.wait_ge(ctx.sem(nc_ev), vtarget)
            pb = pbanks[c % 2]
            mi = None
            for kc in range(kcn):
                mi = t.matmul(pb[:, 0:w],
                              w_sb[:, kc * nd + c * 128: kc * nd + c * 128 + 128],
                              x_sb[:, kc * xw + s0: kc * xw + s0 + w],
                              start=(kc == 0), stop=(kc == kcn - 1))
            if extra_mm is not None:
                mi = extra_mm(t, c, pb)
            ctx.inc(mi, nc_mm)
    ctx.block.tensor(_)

    def _(v):
        for wsem in wait_before_ev:
            ctx.wait(v, wsem)
        for c in range(dout_chunks):
            v.wait_ge(ctx.sem(nc_mm), mm_base + c + 1)
            li = out_cb(v, c, pbanks[c % 2])
            ctx.inc(li, nc_ev)
    ctx.block.vector(_)


def build_launch1(upto="all", ntt=NTT, F=None):
    nc, es, block, ctx, dt, fin = stage_begin(F, None)
    xT = dt("xT", [D, TPC], F32, "ExternalInput")
    Wq = dt("Wq", [D, D], F32, "ExternalInput")
    Wk = dt("Wk", [D, D], F32, "ExternalInput")
    Wv = dt("Wv", [D, D], F32, "ExternalInput")
    ln1w = dt("ln1w", [128, C8], F32, "ExternalInput")
    selw = dt("selw", [D, NH], F32, "ExternalInput")
    selT = dt("selT", [NH, D], F32, "ExternalInput")
    eyeb = dt("eyeb", [128, 128], BF16, "ExternalInput")

    O_xn = dt("xn", [D, TPC], F32, "ExternalOutput")
    O_qh = dt("qh", [D, TPC], BF16, "ExternalOutput")
    O_kh = dt("kh", [D, TPC], BF16, "ExternalOutput")
    O_vt = dt("vt", [TPC, NH * 65], BF16, "ExternalOutput")
    O_vs = dt("vsum", [128, C8], F32, "ExternalOutput")
    O_xtm = dt("xn_tm", [TPC, D], BF16, "ExternalOutput")

    with es:
        x_tt = sb(ctx, "x_tt", 128, C8 * TT)
        xn_tt = sb(ctx, "xn_tt", 128, C8 * TT)
        sq = sb(ctx, "sq", 128, C8 * TT)
        praw = sb(ctx, "praw", 128, C8 * TT)
        rows = sb(ctx, "rows", 1, 4 * TT)
        rs16 = sb(ctx, "rs16", 16, TT)
        xn2 = sb(ctx, "xn2", 128, C8 * TT, BF16)
        ob_tt = sb(ctx, "ob_tt", 128, C8 * TT, BF16)
        vt_tt = sb(ctx, "vt_tt", 128, (TT // 128) * NH * 65, BF16)
        xn_bt = sb(ctx, "xn_bt", 128, C8 * TT, BF16)
        xtm_tt = sb(ctx, "xtm_tt", 128, (TT // 128) * D, BF16)
        wq_sb = sb(ctx, "wq_sb", 128, C8 * D, BF16)
        wk_sb = sb(ctx, "wk_sb", 128, C8 * D, BF16)
        wv_sb = sb(ctx, "wv_sb", 128, C8 * D, BF16)
        ln1_sb = sb(ctx, "ln1_sb", 128, C8)
        sel_sb = sb(ctx, "sel_sb", 128, C8 * NH)
        selT_sb = sb(ctx, "selT_sb", 16, 128 * C8)
        eye_sb = sb(ctx, "eye_sb", 128, 128, BF16)
        ones_sb = sb(ctx, "ones_sb", 128, 128)
        vsum_sb = sb(ctx, "vsum_sb", 128, C8)
        vtmp = sb(ctx, "vtmp", 128, C8)
        cst = sb(ctx, "cst", 128, 2)

        ps_a = psum(ctx, "ps_a", 128, TT)
        ps_b = psum(ctx, "ps_b", 128, TT)
        ps_r1 = psum(ctx, "ps_r1", 1, TT)
        ps_r2 = psum(ctx, "ps_r2", 1, TT)
        ps_bc1 = psum(ctx, "ps_bc1", 128, TT)
        ps_bc2 = psum(ctx, "ps_bc2", 128, TT)
        ps_ss = psum(ctx, "ps_ss", 16, TT)
        ps_t1 = psum(ctx, "ps_t1", 128, 128, BF16)

        def _(g):
            emit_weight_load(ctx, g, Wq, wq_sb)
            emit_weight_load(ctx, g, Wk, wk_sb)
            emit_weight_load(ctx, g, Wv, wv_sb)
        block.gpsimd(_)

        if F is not None:
            zsb = sb(ctx, "e0zsb", 128, 4160, BF16)
            zsf = sb(ctx, "e0zsf", 128, 32)
            emit_exchange_zero(ctx, block, F, "e0", "khs", "vtms", zsb, zsf, 0)

        def _(sy):
            ctx.inc(sy.dma_start(out=ln1_sb[:], in_=ln1w), "cld", 16)
            ctx.inc(sy.dma_start(out=sel_sb[:].rearrange("p (c h) -> p c h", c=C8),
                                 in_=selw.rearrange("(c p) h -> p c h", p=128)), "cld", 16)
            ctx.inc(sy.dma_start(out=selT_sb[:].rearrange("h (c p) -> h c p", c=C8),
                                 in_=selT.rearrange("h (c p) -> h c p", p=128)), "cld", 16)
            ctx.inc(sy.dma_start(out=eye_sb[:], in_=eyeb), "cld", 16)
        block.sync(_)

        def _(v):
            v.memset(ones_sb[:], 1.0)
            v.memset(vsum_sb[:], 0.0)
            v.memset(cst[:, 0:1], 1e-5)
            ctx.inc(v.memset(cst[:, 1:2], 1e-6), "vinit")
        block.vector(_)

        for tt in range(ntt):
            s0 = tt * TT

            def _(sy, s0=s0):
                ctx.wait(sy, "xn_ready")
                src = xT.rearrange("(c p) t -> p c t", p=128)[:, :, s0:s0 + TT]
                ctx.inc(sy.dma_start(out=x_tt[:].rearrange("p (c t) -> p c t", c=C8), in_=src), "xld", 16)
            block.sync(_)

            def _(s):
                ctx.wait(s, "xld")
                ctx.wait(s, "ss_mm")
                ctx.inc(s.activation(sq[:], x_tt[:], AF.Square, bias=0.0, scale=1.0), "sqk")
            block.scalar(_)

            def _(t):
                ctx.wait(t, "xld")
                ctx.wait(t, "vinit")
                ctx.wait(t, "var_ready")
                for c in range(C8):
                    t.matmul(ps_r1[:], ones_sb[:, 0:1], x_tt[:, c * TT:(c + 1) * TT],
                             start=(c == 0), stop=(c == C8 - 1))
                ctx.wait(t, "sqk")
                mi = None
                for c in range(C8):
                    mi = t.matmul(ps_r2[:], ones_sb[:, 0:1], sq[:, c * TT:(c + 1) * TT],
                                  start=(c == 0), stop=(c == C8 - 1))
                ctx.inc(mi, "mm_stats")
            block.tensor(_)

            def _(v):
                ctx.wait(v, "mm_stats")
                S_, SS = rows[:, 0:TT], rows[:, TT:2 * TT]
                A_, B_ = rows[:, 2 * TT:3 * TT], rows[:, 3 * TT:4 * TT]
                ctx.chain(v, v.tensor_single_scalar(S_, ps_r1[:], 1.0 / D, ALU.mult))
                ctx.chain(v, v.tensor_single_scalar(SS, ps_r2[:], 1.0 / D, ALU.mult))
                ctx.chain(v, v.tensor_tensor(A_, S_, S_, ALU.mult))
                ctx.inc(v.tensor_tensor(A_, SS, A_, ALU.subtract), "var_ready")
            block.vector(_)

            def _(s):
                ctx.wait(s, "var_ready")
                A_ = rows[:, 2 * TT:3 * TT]
                ctx.inc(s.activation(A_, A_, AF.Sqrt, bias=cst[0:1, 0:1], scale=1.0), "std_ready")
            block.scalar(_)

            def _(v):
                ctx.wait(v, "std_ready")
                S_ = rows[:, 0:TT]
                A_, B_ = rows[:, 2 * TT:3 * TT], rows[:, 3 * TT:4 * TT]
                ctx.chain(v, v.reciprocal(A_, A_))
                ctx.chain(v, v.tensor_tensor(B_, S_, A_, ALU.mult))
                ctx.inc(v.tensor_single_scalar(B_, B_, -1.0, ALU.mult), "ab_ready")
            block.vector(_)

            def _(t):
                ctx.wait(t, "ab_ready")
                ctx.wait(t, "xn2_ready")
                t.matmul(ps_bc1[:], ones_sb[0:1, 0:128], rows[0:1, 2 * TT:3 * TT], start=True, stop=True)
                ctx.inc(t.matmul(ps_bc2[:], ones_sb[0:1, 0:128], rows[0:1, 3 * TT:4 * TT], start=True, stop=True), "bc_ready")
            block.tensor(_)

            def _(v):
                ctx.wait(v, "bc_ready")
                ctx.wait(v, "xnst")
                ctx.wait(v, "xbt_ready")
                li = None
                for c in range(C8):
                    xc = x_tt[:, c * TT:(c + 1) * TT]
                    xnc = xn_tt[:, c * TT:(c + 1) * TT]
                    ctx.chain(v, v.tensor_tensor(xnc, xc, ps_bc1[:], ALU.mult))
                    li = v.tensor_tensor(xnc, xnc, ps_bc2[:], ALU.add)
                ctx.inc(li, "xn_ready")
            block.vector(_)

            def _(sy, s0=s0):
                ctx.wait(sy, "xn_ready")
                dst = O_xn.rearrange("(c p) t -> p c t", p=128)[:, :, s0:s0 + TT]
                ctx.inc(sy.dma_start(out=dst, in_=xn_tt[:].rearrange("p (c t) -> p c t", c=C8)), "xnst", 16)
            block.sync(_)

            def _(s):
                ctx.wait(s, "xn_ready")
                ctx.inc(s.activation(sq[:], xn_tt[:], AF.Square, bias=0.0, scale=1.0), "sqk2")
            block.scalar(_)

            def _(t):
                ctx.wait(t, "sqk2")
                mi = None
                for c in range(C8):
                    mi = t.matmul(ps_r2[:], ones_sb[:, 0:1], sq[:, c * TT:(c + 1) * TT],
                                  start=(c == 0), stop=(c == C8 - 1))
                ctx.inc(mi, "mm_rms")
            block.tensor(_)

            def _(s):
                ctx.wait(s, "mm_rms")
                ctx.inc(s.activation(rows[:, 2 * TT:3 * TT], ps_r2[:], AF.Sqrt, bias=cst[0:1, 1:2], scale=1.0 / D), "rms_std")
            block.scalar(_)

            def _(v):
                ctx.wait(v, "rms_std")
                ctx.inc(v.reciprocal(rows[:, 2 * TT:3 * TT], rows[:, 2 * TT:3 * TT]), "rms_ready")
            block.vector(_)

            def _(t):
                ctx.wait(t, "rms_ready")
                ctx.inc(t.matmul(ps_bc1[:], ones_sb[0:1, 0:128], rows[0:1, 2 * TT:3 * TT], start=True, stop=True), "rmsbc_ready")
            block.tensor(_)

            def _(v):
                ctx.wait(v, "rmsbc_ready")
                ctx.wait(v, "cld")
                li = None
                for c in range(C8):
                    li = v.scalar_tensor_tensor(xn2[:, c * TT:(c + 1) * TT],
                                                xn_tt[:, c * TT:(c + 1) * TT],
                                                ln1_sb[:, c:c + 1], ps_bc1[:],
                                                ALU.mult, ALU.mult)
                ctx.inc(li, "xn2_ready")
            block.vector(_)

            def _(v):
                ctx.wait(v, "xn_ready")
                ctx.wait(v, "xtp_mm")
                ctx.inc(v.tensor_copy(xn_bt[:], xn_tt[:]), "xbt_ready")
            block.vector(_)

            ctx.sem("xtp_mm"), ctx.sem("xtp_ev")
            xt_base = ctx.cnt["xtp_mm"]
            xe_base = ctx.cnt["xtp_ev"]

            def _(t, xt_base=xt_base, xe_base=xe_base):
                ctx.wait(t, "xbt_ready")
                ctx.wait(t, "vtp_ev")
                ctx.wait(t, "xtmst")
                n = 0
                for tc in range(TT // 128):
                    for c in range(C8):
                        if xe_base + n > 0:
                            t.wait_ge(ctx.sem("xtp_ev"), xe_base + n)
                        ctx.inc(t.transpose(ps_t1[:], xn_bt[:, c * TT + tc * 128: c * TT + tc * 128 + 128],
                                            eye_sb[:]), "xtp_mm")
                        n += 1
            block.tensor(_)

            def _(v, xt_base=xt_base):
                n = 0
                for tc in range(TT // 128):
                    for c in range(C8):
                        n += 1
                        v.wait_ge(ctx.sem("xtp_mm"), xt_base + n)
                        ctx.inc(v.tensor_copy(xtm_tt[:, tc * 1024 + c * 128: tc * 1024 + c * 128 + 128],
                                              ps_t1[:]), "xtp_ev")
            block.vector(_)

            def _(sy, s0=s0):
                ctx.wait(sy, "xtp_ev")
                dst = O_xtm[s0:s0 + TT, :].rearrange("(tc p) f -> p tc f", p=128)
                src = xtm_tt[:].rearrange("p (tc f) -> p tc f", tc=TT // 128)
                ctx.inc(sy.dma_start(out=dst, in_=src), "xtmst", 16)
            block.sync(_)

            qk_list = [] if upto == "xn" else [("q", wq_sb, O_qh), ("k", wk_sb, O_kh)]
            for nm, w_sb_, out_dram in qk_list:
                def evac(v, c, pb):
                    return v.tensor_copy(praw[:, c * TT:(c + 1) * TT], pb[:, 0:TT])
                emit_proj(ctx, "p" + nm, w_sb_, xn2, (0, TT), evac,
                          xw=TT, pbanks=(ps_a, ps_b),
                          wait_before=("xn2_ready", "wld", "qh_ev"),
                          wait_before_ev=("sqk3",))

                def _(s, nm=nm):
                    ctx.wait(s, "p" + nm + "_ev")
                    ctx.inc(s.activation(sq[:], praw[:], AF.Square, bias=0.0, scale=1.0), "sqk3")
                block.scalar(_)

                def _(t, nm=nm):
                    ctx.wait(t, "sqk3")
                    ctx.wait(t, "qh_ev")
                    mi = None
                    for c in range(C8):
                        mi = t.matmul(ps_ss[:], sel_sb[:, c * NH:(c + 1) * NH],
                                      sq[:, c * TT:(c + 1) * TT],
                                      start=(c == 0), stop=(c == C8 - 1))
                    ctx.inc(mi, "ss_mm")
                block.tensor(_)

                def _(s, nm=nm):
                    ctx.wait(s, "ss_mm")
                    ctx.inc(s.activation(rs16[:], ps_ss[:], AF.Sqrt, bias=cst[0:16, 1:2], scale=1.0), "ss_sqrt")
                block.scalar(_)

                def _(v, nm=nm):
                    ctx.wait(v, "ss_sqrt")
                    ctx.inc(v.reciprocal(rs16[:], rs16[:]), "rs_ready")
                block.vector(_)

                ctx.sem("rsbc_mm"), ctx.sem("qh_ev")
                rb_base = ctx.cnt["rsbc_mm"]
                qe_base = ctx.cnt["qh_ev"]

                def _(t, rb_base=rb_base, qe_base=qe_base):
                    ctx.wait(t, "rs_ready")
                    for c in range(C8):
                        if c >= 2:
                            t.wait_ge(ctx.sem("qh_ev"), qe_base + c - 1)
                        pb = ps_bc1 if c % 2 == 0 else ps_bc2
                        ctx.inc(t.matmul(pb[:], selT_sb[:, c * 128:(c + 1) * 128], rs16[:],
                                         start=True, stop=True), "rsbc_mm")
                block.tensor(_)

                def _(v, rb_base=rb_base):
                    ctx.wait(v, "qhst")
                    for c in range(C8):
                        v.wait_ge(ctx.sem("rsbc_mm"), rb_base + c + 1)
                        pb = ps_bc1 if c % 2 == 0 else ps_bc2
                        li = v.tensor_tensor(ob_tt[:, c * TT:(c + 1) * TT],
                                             praw[:, c * TT:(c + 1) * TT], pb[:], ALU.mult)
                        ctx.inc(li, "qh_ev")
                block.vector(_)

                def _(sy, s0=s0, out_dram=out_dram, nm=nm):
                    ctx.wait(sy, "qh_ev")
                    if F is not None and nm == "k":
                        dst = F.io[f"kh_t{s0 // TT}"].rearrange("(c p) q -> p c q", p=128)
                    else:
                        dst = out_dram.rearrange("(c p) t -> p c t", p=128)[:, :, s0:s0 + TT]
                    ctx.inc(sy.dma_start(out=dst, in_=ob_tt[:].rearrange("p (c t) -> p c t", c=C8)), "qhst", 16)
                block.sync(_)

            if upto in ("xn", "qk"):
                continue
            def vevac(v, c, pb):
                v.tensor_copy(ob_tt[:, c * TT:(c + 1) * TT], pb[:, 0:TT])
                ctx.chain(v, v.tensor_reduce(vtmp[:, c:c + 1], pb[:, 0:TT], axis=AX, op=ALU.add))
                return v.tensor_tensor(vsum_sb[:, c:c + 1], vsum_sb[:, c:c + 1], vtmp[:, c:c + 1], ALU.add)
            emit_proj(ctx, "pv", wv_sb, xn2, (0, TT), vevac,
                      xw=TT, pbanks=(ps_a, ps_b),
                      wait_before=("xn2_ready", "wld", "qhst"),
                      wait_before_ev=("qhst",))

            ctx.sem("vtp_mm"), ctx.sem("vtp_ev")
            tp_base = ctx.cnt["vtp_mm"]
            te_base = ctx.cnt["vtp_ev"]

            def _(t, tp_base=tp_base, te_base=te_base):
                ctx.wait(t, "pv_ev")
                ctx.wait(t, "vtst")
                ctx.wait(t, "xtp_ev")
                n = 0
                for tc in range(TT // 128):
                    for c in range(C8):
                        if n >= 1:
                            t.wait_ge(ctx.sem("vtp_ev"), te_base + n)
                        ctx.inc(t.transpose(ps_t1[:], ob_tt[:, c * TT + tc * 128: c * TT + tc * 128 + 128], eye_sb[:]), "vtp_mm")
                        n += 1
            block.tensor(_)

            def _(v, tp_base=tp_base):
                n = 0
                for tc in range(TT // 128):
                    base = tc * NH * 65
                    for c in range(C8):
                        n += 1
                        v.wait_ge(ctx.sem("vtp_mm"), tp_base + n)
                        h0, h1 = 2 * c, 2 * c + 1
                        v.tensor_copy(vt_tt[:, base + h0 * 65: base + h0 * 65 + 64], ps_t1[:, 0:64])
                        ctx.inc(v.tensor_copy(vt_tt[:, base + h1 * 65: base + h1 * 65 + 64], ps_t1[:, 64:128]), "vtp_ev")
                    li = v.tensor_copy(
                        vt_tt[:].rearrange("p (tc h x) -> p tc h x", tc=TT // 128, h=NH)[:, tc, :, 64:65],
                        ones_sb[:, 0:NH].rearrange("p (h x) -> p h x", x=1))
                ctx.inc(li, "vt_ready")
            block.vector(_)

            def _(sy, s0=s0):
                ctx.wait(sy, "vt_ready")
                if F is not None:
                    dst = F.io[f"vt_t{s0 // TT}"].rearrange("(tc p) f -> p tc f", p=128)
                else:
                    dst = O_vt[s0:s0 + TT, :].rearrange("(tc p) f -> p tc f", p=128)
                src = vt_tt[:].rearrange("p (tc f) -> p tc f", tc=TT // 128)
                ctx.inc(sy.dma_start(out=dst, in_=src), "vtst", 16)
            block.sync(_)
            if F is not None:
                emit_exchange_tile(ctx, block, F, "e0", tt, "cc0", "khs", "vtms",
                                   ctx.cnt["qhst"], ctx.cnt["vtst"])

        def _(sy):
            ctx.wait(sy, "vt_ready")
            if F is not None:
                for c in range(C8):
                    ctx.inc(sy.dma_start(out=bass.AP(tensor=O_vs.tensor, offset=O_vs.offset + c * 128,
                                                     ap=[[1, 128], [1, 1]]),
                                         in_=vsum_sb[:, c:c + 1]), "vsst", 16)
            else:
                ctx.inc(sy.dma_start(out=O_vs, in_=vsum_sb[:]), "vsst", 16)
            ctx.wait(sy, "xnst")
            ctx.wait(sy, "qhst")
            ctx.wait(sy, "vtst")
            ctx.wait(sy, "vsst")
        block.sync(_)

    fin()
    return nc



def build_launch2(F=None):
    nc, es, block, ctx, dt, fin = stage_begin(F, None)
    I_xn = dt("xn", [D, TPC], F32, "ExternalInput")
    I_qh = dt("qh", [D, TPC], BF16, "ExternalInput")
    I_khs = dt("khs", [D, NSLOT], BF16, "ExternalInput")
    I_vtm = dt("vtms", [NSLOT, NH * 65], BF16, "ExternalInput")
    I_vs2 = dt("vsum2", [128, 16], F32, "ExternalInput")
    I_sclr = dt("scaler", [1, 16], F32, "ExternalInput")
    I_sclc = dt("scalec", [16, 1], F32, "ExternalInput")
    I_relb = dt("relb", [32, 16], F32, "ExternalInput")
    I_oh = dt("OH", [32, 112], F32, "ExternalInput")
    I_j = dt("J128", [128, 128], F32, "ExternalInput")
    I_eyef = dt("eyef", [128, 128], F32, "ExternalInput")
    I_alpha = dt("alpha", [128, 1], F32, "ExternalInput")
    I_beta = dt("beta", [128, 1], F32, "ExternalInput")
    I_wo = dt("Wo", [D, D], F32, "ExternalInput")
    O_y = dt("y", [D, TPC], F32, "ExternalOutput")
    gdram = dt("gscratch", [16, 384], F32, "Internal")

    with es:
        kh_g = sb(ctx, "kh_g", 128, 4 * NSLOT, BF16)
        qh_g = sb(ctx, "qh_g", 128, 4 * TPC, BF16)
        vt_g = sb(ctx, "vt_g", 128, 32 * 520, BF16)
        ao = sb(ctx, "ao", 128, C8 * TPC, BF16)
        WT = sb(ctx, "WT", 128, NH * 256)
        Pt = sb(ctx, "Pt", 128, 2 * 512, BF16)
        rr = sb(ctx, "rr", 128, 512)
        rb_sb = sb(ctx, "rb_sb", 64, 512)
        wo_sb = sb(ctx, "wo_sb", 128, C8 * D, BF16)
        xnt = sb(ctx, "xnt", 128, C8 * TT)
        yst = sb(ctx, "yst", 128, C8 * TT)
        ones_sb = sb(ctx, "ones_sb", 128, 128)
        scl_row = sb(ctx, "scl_row", 1, 16)
        scl_col = sb(ctx, "scl_col", 16, 4)
        relb_sb = sb(ctx, "relb_sb", 32, 16)
        oh_sb = sb(ctx, "oh_sb", 32, 112)
        j_sb = sb(ctx, "j_sb", 128, 128)
        eyef_sb = sb(ctx, "eyef_sb", 128, 128)
        bandsb = sb(ctx, "bandsb", 112, 16)
        G_sb = sb(ctx, "G_sb", 16, 384)
        rowtmp = sb(ctx, "rowtmp", 1, 16)
        SCL = sb(ctx, "SCL", 128, 16)
        MT = sb(ctx, "MT", 128, 16)
        vs2 = sb(ctx, "vs2", 128, 16)
        vmb = sb(ctx, "vmb", 128, 8)
        ab_sb = sb(ctx, "ab_sb", 128, 2)
        wflip = sb(ctx, "wflip", 128, 256)

        psS_a = psum(ctx, "psS_a", 128, 512)
        psS_b = psum(ctx, "psS_b", 128, 512)
        O_a = psum(ctx, "O_a", 65, 512)
        O_b = psum(ctx, "O_b", 65, 512)
        O_c = psum(ctx, "O_c", 65, 512)
        O_d = psum(ctx, "O_d", 65, 512)
        psB = psum(ctx, "psB", 64, 512)
        ps_m = psum(ctx, "ps_m", 128, 512)

        def _(sy):
            if F is not None:
                sy.wait_ge(ctx.sem("e0asm"), 32)
            ctx.inc(sy.dma_start(out=scl_row[:], in_=I_sclr), "cld", 16)
            ctx.inc(sy.dma_start(out=scl_col[:, 0:1], in_=I_sclc), "cld", 16)
            ctx.inc(sy.dma_start(out=relb_sb[:], in_=I_relb), "cld", 16)
            ctx.inc(sy.dma_start(out=oh_sb[:], in_=I_oh), "cld", 16)
            ctx.inc(sy.dma_start(out=j_sb[:], in_=I_j), "cld", 16)
            ctx.inc(sy.dma_start(out=eyef_sb[:], in_=I_eyef), "cld", 16)
            if F is None:
                ctx.inc(sy.dma_start(out=vs2[:], in_=I_vs2), "cld", 16)
            ctx.inc(sy.dma_start(out=ab_sb[:, 0:1], in_=I_alpha), "cld", 16)
            ctx.inc(sy.dma_start(out=ab_sb[:, 1:2], in_=I_beta), "cld", 16)
        block.sync(_)

        def _(g):
            if F is not None:
                g.wait_ge(ctx.sem("e0asm"), 32)
            emit_weight_load(ctx, g, I_wo, wo_sb, sem="wld")
        block.gpsimd(_)

        if F is not None:
            def _(g):
                g.wait_ge(ctx.sem("e0asm"), 160)
                ctx.wait(g, "e0vs2cp")
                for cc_ in range(16):
                    ctx.inc(g.dma_start(out=vs2[:, cc_:cc_ + 1],
                                        in_=bass.AP(tensor=I_vs2.tensor,
                                                    offset=I_vs2.offset + cc_ * 128,
                                                    ap=[[1, 128], [1, 1]])), "vs2ld", 16)
            block.gpsimd(_)

        def _(v):
            if F is not None:
                v.wait_ge(ctx.sem("e0asm"), 32)
            v.memset(ones_sb[:], 1.0)
            ctx.inc(v.memset(G_sb[:], 0.0), "vinit")
        block.vector(_)

        def _(t):
            ctx.wait(t, "cld")
            ctx.wait(t, "vinit")
            ctx.inc(t.matmul(ps_m[0:112, 0:16], oh_sb[:], relb_sb[:], start=True, stop=True), "mm_band")
        block.tensor(_)

        def _(v):
            ctx.wait(v, "mm_band")
            ctx.inc(v.tensor_copy(bandsb[:], ps_m[0:112, 0:16]), "band_sb")
        block.vector(_)

        def _(t):
            ctx.wait(t, "band_sb")
            ctx.inc(t.transpose(O_a[0:16, 0:112], bandsb[:], eyef_sb[0:112, 0:112]), "mm_bandT")
        block.tensor(_)

        def _(s):
            ctx.wait(s, "cld")
            ctx.chain(s, s.activation(scl_col[:, 1:2], scl_col[:, 0:1], AF.Sign, bias=0.0, scale=1.0))
            ctx.inc(s.activation(scl_col[:, 2:3], scl_col[:, 0:1], AF.Abs, bias=0.0, scale=1.0), "sc_abs")
        block.scalar(_)

        def _(v):
            ctx.wait(v, "mm_bandT")
            ctx.wait(v, "sc_abs")
            ctx.chain(v, v.tensor_copy(G_sb[:, 128:240], O_a[0:16, 0:112]))
            ctx.chain(v, v.tensor_single_scalar(scl_col[:, 1:2], scl_col[:, 1:2], -BIG, ALU.mult))
            ctx.chain(v, v.tensor_scalar(G_sb[:, 0:128], ones_sb[0:16, 0:128], scl_col[:, 1:2], None, ALU.mult))
            ctx.chain(v, v.tensor_reduce(scl_col[:, 3:4], G_sb[:, 128:240], axis=AX, op=ALU.max,
                                         apply_absolute_value=True))
            ctx.chain(v, v.tensor_single_scalar(scl_col[:, 3:4], scl_col[:, 3:4], 1.0, ALU.add))
            ctx.chain(v, v.tensor_tensor(scl_col[:, 3:4], scl_col[:, 3:4], scl_col[:, 2:3], ALU.mult))
            ctx.inc(v.tensor_single_scalar(scl_col[:, 3:4], scl_col[:, 3:4], -1.0, ALU.mult), "negm")
        block.vector(_)

        def _(t):
            ctx.wait(t, "negm")
            ctx.inc(t.transpose(O_b[0:1, 0:16], scl_col[:, 3:4], eyef_sb[0:16, 0:16]), "mm_negmT")
        block.tensor(_)

        def _(v):
            ctx.wait(v, "mm_negmT")
            ctx.inc(v.tensor_copy(rowtmp[:], O_b[0:1, 0:16]), "rowtmp_sb")
        block.vector(_)

        def _(t):
            ctx.wait(t, "rowtmp_sb")
            ctx.inc(t.matmul(ps_m[:, 0:16], ones_sb[0:1, 0:128], rowtmp[:], start=True, stop=True), "mm_bc1")
            ctx.inc(t.matmul(ps_m[:, 16:32], ones_sb[0:1, 0:128], scl_row[:], start=True, stop=True), "mm_bc2")
        block.tensor(_)

        def _(v):
            ctx.wait(v, "mm_bc2")
            ctx.wait(v, "vs2ld")
            ctx.inc(v.tensor_copy(MT[:], ps_m[:, 0:16]), "mtscl")
            ctx.inc(v.tensor_copy(SCL[:], ps_m[:, 16:32]), "mtscl")
            ctx.chain(v, v.tensor_tensor(vmb[:], vs2[:, 0:8], vs2[:, 8:16], ALU.add))
            ctx.chain(v, v.tensor_scalar(vmb[:], vmb[:], ab_sb[:, 1:2], None, ALU.mult))
            ctx.inc(v.tensor_single_scalar(vmb[:], vmb[:], 1.0 / NSLOT, ALU.mult), "setup_v")
        block.vector(_)

        def _(sy):
            ctx.wait(sy, "negm")
            ctx.inc(sy.dma_start(out=gdram, in_=G_sb[:]), "gst", 16)
        block.sync(_)

        for h in range(NH):
            def _(sy, h=h):
                ctx.wait(sy, "gst")
                ctx.wait(sy, "wt_done")
                src = bass.AP(tensor=gdram.tensor, offset=h * 384, ap=[[1, 128], [1, 256]])
                ctx.inc(sy.dma_start(out=wflip[:], in_=src), "hank", 16)
            block.sync(_)

            def _(t, h=h):
                ctx.wait(t, "hank")
                ctx.wait(t, "wt_done")
                ctx.wait(t, "mtscl")
                ctx.inc(t.matmul(ps_m[:, 0:256], j_sb[:], wflip[:], start=True, stop=True), "mm_wt")
            block.tensor(_)

            def _(v, h=h):
                ctx.wait(v, "mm_wt")
                ctx.inc(v.tensor_copy(WT[:, h * 256:(h + 1) * 256], ps_m[:, 0:256]), "wt_done")
            block.vector(_)

        ctx.sem("smm"), ctx.sem("wadd"), ctx.sem("pexp"), ctx.sem("ppv"), ctx.sem("rb_used")

        rb_stage_b = ctx.cnt["rb_used"] if "rb_used" in ctx.cnt else 0
        ctx.sem("rb_used")
        rb_stage_b = ctx.cnt["rb_used"]
        def emit_epilogue2(ent):
            Ob_, h_, j_, pr_, ppv_end = ent

            rec_b = ctx.cnt_of("recip")

            def _(v):
                v.wait_ge(ctx.sem("ppv"), ppv_end)
                ctx.chain(v, v.tensor_single_scalar(rr[64:65, :], Ob_[64:65, :], 1e-30, ALU.add))
                ctx.inc(v.reciprocal(rr[64:65, :], rr[64:65, :]), "recip")
            block.vector(_)

            rbu_b = ctx.cnt_of("rb_used")

            def _(t):
                t.wait_ge(ctx.sem("recip"), rec_b + 1)
                if rbu_b > 0:
                    t.wait_ge(ctx.sem("rb_used"), rbu_b)
                ctx.inc(t.matmul(psB[:], ones_sb[64:65, 0:64], rr[64:65, :],
                                 start=True, stop=True), "mm_rb")
            block.tensor(_)

            mrb_b = ctx.cnt_of("mm_rb")

            def _(v):
                v.wait_ge(ctx.sem("mm_rb"), mrb_b)
                ctx.chain(v, v.tensor_copy(rb_sb[:], psB[:]))
                dst = ao[pr_:pr_ + 64, (h_ // 2) * TPC + j_ * 512: (h_ // 2) * TPC + j_ * 512 + 512]
                ctx.inc(v.tensor_tensor(dst, Ob_[0:64, :], rb_sb[:], ALU.mult), "rb_used")
            block.vector(_)

        NG = 2
        for g in range(NG):
            pend = []

            def _(sy, g=g):
                ctx.wait(sy, "rb_used")
                if F is not None:
                    sy.wait_ge(ctx.sem("e0asm"), 160)
                    sy.wait_ge(ctx.sem("e0asm2"), 128)
                src = I_khs.rearrange("(c p) s -> p c s", p=128)[:, 4 * g:4 * g + 4, :]
                ctx.inc(sy.dma_start(out=kh_g[:].rearrange("p (c s) -> p c s", c=4), in_=src), "gld", 16)
                srcq = I_qh.rearrange("(c p) s -> p c s", p=128)[:, 4 * g:4 * g + 4, :]
                ctx.inc(sy.dma_start(out=qh_g[:].rearrange("p (c s) -> p c s", c=4), in_=srcq), "gld", 16)
                srcv = I_vtm[:, g * 520:(g + 1) * 520].rearrange("(kc p) f -> p kc f", p=128)
                ctx.inc(sy.dma_start(out=vt_g[:].rearrange("p (kc f) -> p kc f", kc=32), in_=srcv), "gld", 16)
            block.sync(_)

            for hl in range(8):
                h = 8 * g + hl
                cl = hl // 2
                pr = (hl % 2) * 64
                for j in range(4):
                    G0 = 512 * (2 * j + 1)
                    nkc = 8 * j + 8
                    it = g * 32 + hl * 4 + j
                    Ob = (O_a, O_b, O_c, O_d)[it % 4]
                    smm_b = ctx.cnt["smm"]
                    wadd_b = ctx.cnt["wadd"]
                    pexp_b = ctx.cnt["pexp"]
                    ppv_b = ctx.cnt["ppv"]
                    geo = []
                    for kc in range(nkc):
                        c0 = 128 * kc - G0
                        geo.append((c0, max(0, c0), c0 >= -128))
                    nw = [0]
                    for kc in range(nkc):
                        nw.append(nw[-1] + (1 if geo[kc][2] else 0))

                    if len(pend) == 2:
                        emit_epilogue2(pend.pop(0))

                    def _(t, g=g, hl=hl, cl=cl, pr=pr, j=j, nkc=nkc, Ob=Ob, it=it,
                          smm_b=smm_b, pexp_b=pexp_b, ppv_b=ppv_b, geo=geo):
                        ctx.wait(t, "gld")
                        ctx.wait(t, "wt_done")
                        if pexp_b > 0:
                            t.wait_ge(ctx.sem("pexp"), pexp_b)
                        if it >= 4:
                            t.wait_ge(ctx.sem("rb_used"), rb_stage_b + it - 3)
                        for kc in range(nkc):
                            c0, qq0, has_w = geo[kc]
                            pb = psS_a if kc % 2 == 0 else psS_b
                            if kc >= 2:
                                t.wait_ge(ctx.sem("pexp"), pexp_b + kc - 1)
                            mi = t.matmul(pb[:, qq0:512],
                                          kh_g[pr:pr + 64, cl * NSLOT + 128 * kc: cl * NSLOT + 128 * kc + 128],
                                          qh_g[pr:pr + 64, cl * TPC + j * 512 + qq0: cl * TPC + (j + 1) * 512],
                                          start=True, stop=True)
                            ctx.inc(mi, "smm")
                            if kc >= 1:
                                pkc = kc - 1
                                qq0p = geo[pkc][1]
                                t.wait_ge(ctx.sem("pexp"), pexp_b + pkc + 1)
                                mi = t.matmul(Ob[:, qq0p:512],
                                              vt_g[:, pkc * 520 + hl * 65: pkc * 520 + hl * 65 + 65],
                                              Pt[:, (pkc % 2) * 512 + qq0p: (pkc % 2) * 512 + 512],
                                              start=(pkc == 0), stop=False)
                                ctx.inc(mi, "ppv")
                        kc = nkc - 1
                        qq0p = geo[kc][1]
                        t.wait_ge(ctx.sem("pexp"), pexp_b + kc + 1)
                        mi = t.matmul(Ob[:, qq0p:512],
                                      vt_g[:, kc * 520 + hl * 65: kc * 520 + hl * 65 + 65],
                                      Pt[:, (kc % 2) * 512 + qq0p: (kc % 2) * 512 + 512],
                                      start=(kc == 0), stop=True)
                        ctx.inc(mi, "ppv")
                    block.tensor(_)

                    def _(v, h=h, nkc=nkc, smm_b=smm_b, geo=geo):
                        for kc in range(nkc):
                            c0, qq0, has_w = geo[kc]
                            if not has_w:
                                continue
                            v.wait_ge(ctx.sem("smm"), smm_b + kc + 1)
                            pb = psS_a if kc % 2 == 0 else psS_b
                            u0 = max(0, -c0)
                            q0 = max(0, c0)
                            wdt = min(256 - u0, 512 - q0)
                            li = v.tensor_tensor(pb[:, q0:q0 + wdt], pb[:, q0:q0 + wdt],
                                                 WT[:, h * 256 + u0: h * 256 + u0 + wdt], ALU.add)
                            ctx.inc(li, "wadd")
                    block.vector(_)

                    def _(s, h=h, nkc=nkc, smm_b=smm_b, wadd_b=wadd_b,
                          pexp_b=pexp_b, ppv_b=ppv_b, geo=geo, nw=nw):
                        ctx.wait(s, "mtscl")
                        if ppv_b > 0:
                            s.wait_ge(ctx.sem("ppv"), ppv_b)
                        for kc in range(nkc):
                            c0, qq0, has_w = geo[kc]
                            pb = psS_a if kc % 2 == 0 else psS_b
                            if has_w:
                                s.wait_ge(ctx.sem("wadd"), wadd_b + nw[kc + 1])
                            else:
                                s.wait_ge(ctx.sem("smm"), smm_b + kc + 1)
                            if kc >= 2:
                                s.wait_ge(ctx.sem("ppv"), ppv_b + kc - 1)
                            li = s.activation(Pt[:, (kc % 2) * 512 + qq0: (kc % 2) * 512 + 512],
                                              pb[:, qq0:512], AF.Exp,
                                              bias=MT[:, h:h + 1], scale=SCL[:, h:h + 1])
                            ctx.inc(li, "pexp")
                    block.scalar(_)

                    pend.append((Ob, h, j, pr, ctx.cnt["ppv"]))

            while pend:
                emit_epilogue2(pend.pop(0))

        def _(v):
            ctx.wait(v, "rb_used")
            ctx.wait(v, "setup_v")
            li = None
            for c in range(C8):
                li = v.scalar_tensor_tensor(ao[:, c * TPC: c * TPC + 1],
                                            ao[:, c * TPC: c * TPC + 1],
                                            ab_sb[:, 0:1], vmb[:, c:c + 1],
                                            ALU.mult, ALU.add)
            ctx.inc(li, "patch_done")
        block.vector(_)

        for tt in range(NTT):
            s0 = tt * TT

            def _(sy, s0=s0):
                ctx.wait(sy, "wo_ev")
                src = I_xn.rearrange("(c p) t -> p c t", p=128)[:, :, s0:s0 + TT]
                ctx.inc(sy.dma_start(out=xnt[:].rearrange("p (c t) -> p c t", c=C8), in_=src), "xnld", 16)
            block.sync(_)

            def evac(v, c, pb, s0=s0):
                return v.tensor_tensor(yst[:, c * TT:(c + 1) * TT], pb[:, 0:TT],
                                       xnt[:, c * TT:(c + 1) * TT], ALU.add)
            emit_proj(ctx, "wo", wo_sb, ao, (s0, TT), evac,
                      xw=TPC, pbanks=(psS_a, psS_b),
                      wait_before=("patch_done", "wld", "pexp", "xnld"),
                      wait_before_ev=("xnld", "yst_st"))

            def _(sy, s0=s0):
                ctx.wait(sy, "wo_ev")
                dst = O_y.rearrange("(c p) t -> p c t", p=128)[:, :, s0:s0 + TT]
                ctx.inc(sy.dma_start(out=dst, in_=yst[:].rearrange("p (c t) -> p c t", c=C8)), "yst_st", 16)
            block.sync(_)

        def _(sy):
            ctx.wait(sy, "yst_st")
        block.sync(_)

    return nc



_DBG3 = False


def build_launch3(F=None):
    nc, es, block, ctx, dt, fin = stage_begin(F, None)
    I_xT = dt("xT", [D, TPC], F32, "ExternalInput")
    I_xn = dt("xn", [D, TPC], F32, "ExternalInput")
    I_y = dt("y", [D, TPC], F32, "ExternalInput")
    I_ln2 = dt("ln2w", [128, C8], F32, "ExternalInput")
    I_w1 = dt("W1", [D, 4 * D], F32, "ExternalInput")
    I_w2 = dt("W2", [4 * D, D], F32, "ExternalInput")
    I_gw = dt("gateT", [D, 8], F32, "ExternalInput")
    I_mw1 = dt("mw1T", [8 * D, D], F32, "ExternalInput")
    I_mw2 = dt("mw2T", [8 * D, D], F32, "ExternalInput")
    I_mw3 = dt("mw3T", [8 * D, D], F32, "ExternalInput")
    I_b1 = dt("B1", [128, 64], F32, "ExternalInput")
    I_b2 = dt("B2", [128, 64], F32, "ExternalInput")
    I_b3 = dt("B3", [128, 64], F32, "ExternalInput")
    I_sw1 = dt("sw1T", [D, 2 * D], F32, "ExternalInput")
    I_sw2 = dt("sw2T", [D, 2 * D], F32, "ExternalInput")
    I_sw3 = dt("sw3T", [2 * D, D], F32, "ExternalInput")
    I_eyef = dt("eyef", [128, 128], F32, "ExternalInput")
    I_iota8 = dt("iota8", [128, 8], F32, "ExternalInput")
    I_xtm = dt("xn_tm", [TPC, D], BF16, "ExternalInput")
    I_tri = dt("TRI", [128, 128], F32, "ExternalInput")
    I_eyebm = dt("eyebm", [128, 128], BF16, "ExternalInput")
    O_x1 = dt("x1", [D, TPC], F32, "ExternalOutput")
    sdram = dt("s_scr", [D, TPC], F32, "Internal")
    xbkt = dt("xbkt", [8 * 768, D], BF16, "Internal")
    prodbkt = dt("prodbkt", [8 * 768, D], BF16, "Internal")

    with es:
        rows = sb(ctx, "rows", 1, 2 * TT)
        ones_sb = sb(ctx, "ones_sb", 128, 128)
        ln2_sb = sb(ctx, "ln2_sb", 128, C8)
        cst = sb(ctx, "cst", 128, 2)
        eyef_sb = sb(ctx, "eyef_sb", 128, 128)
        iota8_sb = sb(ctx, "iota8_sb", 128, 8)

        psA = psum(ctx, "psA", 128, 512)
        psB_ = psum(ctx, "psB_", 128, 512)
        psC = psum(ctx, "psC", 128, 512)
        psD = psum(ctx, "psD", 128, 512)
        psE = psum(ctx, "psE", 128, 512)
        psF = psum(ctx, "psF", 128, 512)
        psG = psum(ctx, "psG", 128, 512)
        addru = sb(ctx, "addru", 128, 32, mybir.dt.uint32)
        eye_bs = sb(ctx, "eye_bs", 128, 128, BF16)

        def _(sy):
            ctx.inc(sy.dma_start(out=ln2_sb[:], in_=I_ln2), "cld", 16)
            ctx.inc(sy.dma_start(out=eyef_sb[:], in_=I_eyef), "cld", 16)
            ctx.inc(sy.dma_start(out=iota8_sb[:], in_=I_iota8), "cld", 16)
            ctx.inc(sy.dma_start(out=eye_bs[:], in_=I_eyebm), "cld", 16)
        block.sync(_)

        def _(v):
            v.memset(ones_sb[:], 1.0)
            v.memset(cst[:, 0:1], 1e-6)
            ctx.inc(v.memset(cst[:, 1:2], 0.0), "vinit")
        block.vector(_)

        zs = ExitStack()
        psR = zs.enter_context(nc.psum_tensor("psR", [1, 512], F32, allow_name_mangling=True))
        w1_sb = zs.enter_context(nc.sbuf_tensor("w1_sb", [128, C8 * 4 * D], BF16, allow_name_mangling=True))
        w2_sb = zs.enter_context(nc.sbuf_tensor("w2_sb", [128, 32 * D], BF16, allow_name_mangling=True))
        y_tt = zs.enter_context(nc.sbuf_tensor("y_tt", [128, C8 * TT], F32, allow_name_mangling=True))
        sq_z = zs.enter_context(nc.sbuf_tensor("sq_z", [128, C8 * TT], F32, allow_name_mangling=True))
        z2_tt = zs.enter_context(nc.sbuf_tensor("z2_tt", [128, C8 * TT], BF16, allow_name_mangling=True))
        h_tt = zs.enter_context(nc.sbuf_tensor("h_tt", [128, 32 * TT], BF16, allow_name_mangling=True))

        def _(g):
            emit_weight_load(ctx, g, I_w1, w1_sb, sem="wldz")
            emit_weight_load(ctx, g, I_w2, w2_sb, sem="wldz")
        block.gpsimd(_)

        ctx.sem("h_mm"), ctx.sem("h_act"), ctx.sem("smlp_mm"), ctx.sem("smlp_ev")
        for tt in range(NTT):
            s0 = tt * TT

            def _(sy, s0=s0):
                ctx.wait(sy, "sst")
                src = I_y.rearrange("(c p) t -> p c t", p=128)[:, :, s0:s0 + TT]
                ctx.inc(sy.dma_start(out=y_tt[:].rearrange("p (c t) -> p c t", c=C8), in_=src), "yld", 16)
            block.sync(_)

            def _(s):
                ctx.wait(s, "yld")
                ctx.wait(s, "zrms_mm")
                ctx.inc(s.activation(sq_z[:], y_tt[:], AF.Square, bias=0.0, scale=1.0), "zsq")
            block.scalar(_)

            def _(t):
                ctx.wait(t, "zsq")
                ctx.wait(t, "vinit")
                ctx.wait(t, "zstd")
                mi = None
                for c in range(C8):
                    mi = t.matmul(psR[:], ones_sb[:, 0:1], sq_z[:, c * TT:(c + 1) * TT],
                                  start=(c == 0), stop=(c == C8 - 1))
                ctx.inc(mi, "zrms_mm")
            block.tensor(_)

            def _(s):
                ctx.wait(s, "zrms_mm")
                ctx.inc(s.activation(rows[:, 0:TT], psR[:], AF.Sqrt, bias=cst[0:1, 0:1], scale=1.0 / D), "zstd")
            block.scalar(_)

            def _(v):
                ctx.wait(v, "zstd")
                ctx.inc(v.reciprocal(rows[:, 0:TT], rows[:, 0:TT]), "zrstd")
            block.vector(_)

            def _(t):
                ctx.wait(t, "zrstd")
                ctx.wait(t, "z2_done")
                ctx.inc(t.matmul(psE[:], ones_sb[0:1, 0:128], rows[0:1, 0:TT], start=True, stop=True), "zbc")
            block.tensor(_)

            def _(v):
                ctx.wait(v, "zbc")
                ctx.wait(v, "cld")
                li = None
                for c in range(C8):
                    li = v.scalar_tensor_tensor(z2_tt[:, c * TT:(c + 1) * TT],
                                                y_tt[:, c * TT:(c + 1) * TT],
                                                ln2_sb[:, c:c + 1], psE[:],
                                                ALU.mult, ALU.mult)
                ctx.inc(li, "z2_done")
            block.vector(_)

            hmm_b = ctx.cnt["h_mm"]
            hact_b = ctx.cnt["h_act"]

            def _(t, hmm_b=hmm_b, hact_b=hact_b):
                ctx.wait(t, "z2_done")
                ctx.wait(t, "wldz")
                ctx.wait(t, "smlp_mm")
                for hc in range(32):
                    if hact_b + hc - 1 > 0:
                        t.wait_ge(ctx.sem("h_act"), hact_b + hc - 1)
                    pb = psA if hc % 2 == 0 else psB_
                    mi = None
                    for kc in range(C8):
                        mi = t.matmul(pb[:, 0:TT],
                                      w1_sb[:, kc * 4 * D + hc * 128: kc * 4 * D + hc * 128 + 128],
                                      z2_tt[:, kc * TT:(kc + 1) * TT],
                                      start=(kc == 0), stop=(kc == C8 - 1))
                    ctx.inc(mi, "h_mm")
            block.tensor(_)

            def _(s, hmm_b=hmm_b):
                for hc in range(32):
                    s.wait_ge(ctx.sem("h_mm"), hmm_b + hc + 1)
                    pb = psA if hc % 2 == 0 else psB_
                    li = s.activation(h_tt[:, hc * TT:(hc + 1) * TT], pb[:, 0:TT],
                                      AF.Relu, bias=0.0, scale=1.0)
                    ctx.inc(li, "h_act")
            block.scalar(_)

            smm_b = ctx.cnt["smlp_mm"]
            sev_b = ctx.cnt["smlp_ev"]

            def _(t, smm_b=smm_b, sev_b=sev_b, hact_b=hact_b):
                t.wait_ge(ctx.sem("h_act"), hact_b + 32)
                for c in range(C8):
                    if sev_b + c - 1 > 0:
                        t.wait_ge(ctx.sem("smlp_ev"), sev_b + c - 1)
                    pb = psC if c % 2 == 0 else psD
                    mi = None
                    for hc in range(32):
                        mi = t.matmul(pb[:, 0:TT],
                                      w2_sb[:, hc * D + c * 128: hc * D + c * 128 + 128],
                                      h_tt[:, hc * TT:(hc + 1) * TT],
                                      start=(hc == 0), stop=(hc == 31))
                    ctx.inc(mi, "smlp_mm")
            block.tensor(_)

            def _(v, smm_b=smm_b):
                for c in range(C8):
                    v.wait_ge(ctx.sem("smlp_mm"), smm_b + c + 1)
                    pb = psC if c % 2 == 0 else psD
                    li = v.tensor_tensor(y_tt[:, c * TT:(c + 1) * TT], pb[:, 0:TT],
                                         y_tt[:, c * TT:(c + 1) * TT], ALU.add)
                    ctx.inc(li, "smlp_ev")
            block.vector(_)

            def _(sy, s0=s0):
                ctx.wait(sy, "smlp_ev")
                dst = sdram.rearrange("(c p) t -> p c t", p=128)[:, :, s0:s0 + TT]
                ctx.inc(sy.dma_start(out=dst, in_=y_tt[:].rearrange("p (c t) -> p c t", c=C8)), "sst", 16)
            block.sync(_)

        def _(sy):
            ctx.wait(sy, "sst")
        block.sync(_)
        zs.close()

        gacc = sb(ctx, "gacc", 128, C8 * TPC)

        CAP = 768
        gs = ExitStack()
        xn32 = gs.enter_context(nc.sbuf_tensor("xn32", [128, C8 * 128], F32, allow_name_mangling=True))
        gw_sb = gs.enter_context(nc.sbuf_tensor("gw_sb", [128, C8 * 8], F32, allow_name_mangling=True))
        lg = gs.enter_context(nc.sbuf_tensor("lg", [128, 8], F32, allow_name_mangling=True))
        stat = gs.enter_context(nc.sbuf_tensor("stat", [128, 4], F32, allow_name_mangling=True))
        mx8 = gs.enter_context(nc.sbuf_tensor("mx8", [128, 8], F32, allow_name_mangling=True))
        mi8 = gs.enter_context(nc.sbuf_tensor("mi8", [128, 8], mybir.dt.uint32, allow_name_mangling=True))
        mi8f = gs.enter_context(nc.sbuf_tensor("mi8f", [128, 8], F32, allow_name_mangling=True))
        oh2 = gs.enter_context(nc.sbuf_tensor("oh2", [128, 16], F32, allow_name_mangling=True))
        tmp8 = gs.enter_context(nc.sbuf_tensor("tmp8", [128, 8], F32, allow_name_mangling=True))
        posl = gs.enter_context(nc.sbuf_tensor("posl", [128, 8], F32, allow_name_mangling=True))
        carry = gs.enter_context(nc.sbuf_tensor("carry", [1, 16], F32, allow_name_mangling=True))
        tri_sb = gs.enter_context(nc.sbuf_tensor("tri_sb", [128, 128], F32, allow_name_mangling=True))
        xtm_sb = gs.enter_context(nc.sbuf_tensor("xtm_sb", [128, 1024], BF16, allow_name_mangling=True))
        xsc = gs.enter_context(nc.sbuf_tensor("xsc", [128, 2 * 1024], BF16, allow_name_mangling=True))

        def _(sy):
            ctx.wait(sy, "sst")
            ctx.wait(sy, "smlp_mm")
            src = I_gw.rearrange("(c p) e -> p c e", p=128)
            ctx.inc(sy.dma_start(out=gw_sb[:].rearrange("p (c e) -> p c e", c=C8), in_=src), "gwld", 16)
            ctx.inc(sy.dma_start(out=tri_sb[:], in_=I_tri), "gwld", 16)
        block.sync(_)

        def _(v):
            ctx.inc(v.memset(carry[:], 0.0), "gcar")
        block.vector(_)

        for tc in range(16):
            def _(sy, tc=tc):
                ctx.wait(sy, "glog_mm")
                ctx.wait(sy, "sst")
                ctx.wait(sy, "smlp_mm")
                src = I_xn.rearrange("(c p) t -> p c t", p=128)[:, :, tc * 128:(tc + 1) * 128]
                ctx.inc(sy.dma_start(out=xn32[:].rearrange("p (c t) -> p c t", c=C8), in_=src), "xn32ld", 16)
                ctx.wait(sy, "gsc1")
                ctx.inc(sy.dma_start(out=xtm_sb[:], in_=I_xtm[tc * 128:(tc + 1) * 128, :]), "xn32ld", 16)
            block.sync(_)

            def _(t, tc=tc):
                ctx.wait(t, "xn32ld")
                ctx.wait(t, "gwld")
                ctx.wait(t, "gsm")
                mi = None
                for c in range(C8):
                    mi = t.matmul(psG[0:128, 0:8], xn32[:, c * 128:(c + 1) * 128],
                                  gw_sb[:, c * 8:(c + 1) * 8],
                                  start=(c == 0), stop=(c == C8 - 1))
                ctx.inc(mi, "glog_mm")
            block.tensor(_)

            def _(v, tc=tc):
                ctx.wait(v, "glog_mm")
                ctx.inc(v.tensor_reduce(stat[:, 0:1], psG[0:128, 0:8], axis=AX, op=ALU.max, negate=True), "gmax")
            block.vector(_)

            def _(s, tc=tc):
                ctx.wait(s, "gmax")
                ctx.inc(s.activation(lg[:], psG[0:128, 0:8], AF.Exp,
                                     bias=stat[:, 0:1], scale=1.0, accum_out=stat[:, 1:2]), "gsm")
            block.scalar(_)

            def _(v, tc=tc):
                ctx.wait(v, "gsm")
                ctx.wait(v, "rt_mm")
                ctx.chain(v, v.reciprocal(stat[:, 1:2], stat[:, 1:2]))
                ctx.chain(v, v.tensor_scalar(lg[:], lg[:], stat[:, 1:2], None, ALU.mult))
                ctx.chain(v, v.max(mx8[:], lg[:]))
                ctx.chain(v, v.max_index(mi8[:], mx8[:], lg[:]))
                ctx.chain(v, v.tensor_copy(mi8f[:], mi8[:]))
                ctx.chain(v, v.tensor_scalar(oh2[:, 0:8], iota8_sb[:], mi8f[:, 0:1], None, ALU.is_equal))
                ctx.inc(v.tensor_scalar(oh2[:, 8:16], iota8_sb[:], mi8f[:, 1:2], None, ALU.is_equal), "goh")
            block.vector(_)

            def _(t, tc=tc):
                ctx.wait(t, "goh")
                ctx.wait(t, "gpos")
                ctx.wait(t, "gcarup")
                ctx.wait(t, "z2_done")
                t.matmul(psE[:, 0:16], tri_sb[:], oh2[:], start=True, stop=False)
                mi = t.matmul(psE[:, 0:16], ones_sb[0:1, 0:128], carry[:], start=False, stop=True)
                ctx.inc(mi, "rt_mm")
                ctx.inc(t.matmul(psF[0:1, 0:16], ones_sb[:, 0:1], oh2[:], start=True, stop=True), "rt_cs")
            block.tensor(_)

            def _(v, tc=tc):
                ctx.wait(v, "rt_mm")
                ctx.wait(v, "rt_cs")
                ctx.chain(v, v.tensor_tensor(tmp8[:], psE[:, 0:8], oh2[:, 0:8], ALU.mult))
                ctx.chain(v, v.tensor_reduce(posl[:, 0:1], tmp8[:], axis=AX, op=ALU.add))
                ctx.chain(v, v.tensor_tensor(tmp8[:], psE[:, 8:16], oh2[:, 8:16], ALU.mult))
                ctx.chain(v, v.tensor_reduce(posl[:, 1:2], tmp8[:], axis=AX, op=ALU.add))
                ctx.chain(v, v.tensor_single_scalar(posl[:, 2:3], mi8f[:, 0:1], float(CAP), ALU.mult))
                ctx.chain(v, v.tensor_tensor(posl[:, 3:4], posl[:, 2:3], posl[:, 0:1], ALU.add))
                ctx.chain(v, v.tensor_single_scalar(posl[:, 2:3], posl[:, 2:3], 383.0, ALU.add))
                ctx.chain(v, v.tensor_tensor(posl[:, 3:4], posl[:, 3:4], posl[:, 2:3], ALU.min))
                ctx.chain(v, v.tensor_copy(addru[:, 2 * tc: 2 * tc + 1], posl[:, 3:4]))
                ctx.chain(v, v.tensor_single_scalar(posl[:, 4:5], mi8f[:, 1:2], float(CAP), ALU.mult))
                ctx.chain(v, v.tensor_tensor(posl[:, 5:6], posl[:, 4:5], posl[:, 1:2], ALU.add))
                ctx.chain(v, v.tensor_single_scalar(posl[:, 5:6], posl[:, 5:6], 384.0, ALU.add))
                ctx.chain(v, v.tensor_single_scalar(posl[:, 4:5], posl[:, 4:5], 767.0, ALU.add))
                ctx.chain(v, v.tensor_tensor(posl[:, 5:6], posl[:, 5:6], posl[:, 4:5], ALU.min))
                ctx.chain(v, v.tensor_copy(addru[:, 2 * tc + 1: 2 * tc + 2], posl[:, 5:6]))
                ctx.inc(v.tensor_tensor(carry[:], carry[:], psF[0:1, 0:16], ALU.add), "gcarup")
                ctx.inc(v.tensor_copy(posl[:, 6:7], posl[:, 5:6]), "gpos")
                ctx.wait(v, "gsct")
                ctx.chain(v, v.tensor_scalar(xsc[:, 0:1024], xtm_sb[:], mx8[:, 0:1], None, ALU.mult))
                ctx.inc(v.tensor_scalar(xsc[:, 1024:2048], xtm_sb[:], mx8[:, 1:2], None, ALU.mult), "gsc1")
            block.vector(_)

            def _(g, tc=tc):
                ctx.wait(g, "gsc1")
                g.indirect_dma_start(
                    xbkt, bass.IndirectOffsetOnAxis(ap=addru[:, 2 * tc: 2 * tc + 1], axis=0),
                    xsc[:, 0:1024], None).then_inc(ctx.sem("gsct"), 16)
                g.indirect_dma_start(
                    xbkt, bass.IndirectOffsetOnAxis(ap=addru[:, 2 * tc + 1: 2 * tc + 2], axis=0),
                    xsc[:, 1024:2048], None).then_inc(ctx.sem("gsct"), 16)
                ctx.cnt[_SEM_ALIAS.get("gsct", "gsct")] += 32
            block.gpsimd(_)

        gs.close()

        if _DBG3:
            def _(sy):
                ctx.wait(sy, "gsct")
                o1 = nc.dram_tensor("dbg_addru", [128, 32], mybir.dt.uint32, kind="ExternalOutput").ap()
                ctx.inc(sy.dma_start(out=o1, in_=addru[:]), "dbgst", 16)
                o2 = nc.dram_tensor("dbg_xbkt", [8 * 768, D], BF16, kind="ExternalOutput").ap()
                ctx.inc(sy.dma_start(out=o2, in_=xbkt), "dbgst", 16)
            block.sync(_)

        ss_ = ExitStack()
        xn_bf = ss_.enter_context(nc.sbuf_tensor("xn_bf", [128, C8 * TPC], BF16, allow_name_mangling=True))
        swc = ss_.enter_context(nc.sbuf_tensor("swc", [128, C8 * 128 * 2], BF16, allow_name_mangling=True))
        zg = ss_.enter_context(nc.sbuf_tensor("zg", [128, 16 * TPC], BF16, allow_name_mangling=True))
        ztmp = ss_.enter_context(nc.sbuf_tensor("ztmp", [128, TPC], F32, allow_name_mangling=True))
        sw3c = ss_.enter_context(nc.sbuf_tensor("sw3c", [128, 16 * 128], BF16, allow_name_mangling=True))

        def _(g):
            ctx.wait(g, "sst")
            ctx.wait(g, "gcarup")
            ctx.wait(g, "gsct")
            src = I_xn.rearrange("(c p) t -> p c t", p=128)
            ctx.inc(g.dma_start(out=xn_bf[:].rearrange("p (c t) -> p c t", c=C8), in_=src), "xnbf", 16)
        block.gpsimd(_)

        ctx.sem("z1mm"), ctx.sem("z1act"), ctx.sem("z1ev"), ctx.sem("swld")
        for ic in range(16):
            def _(g, ic=ic):
                ctx.wait(g, "z1mm")
                ctx.wait(g, "gsct")
                ctx.wait(g, "gcarup")
                src1 = I_sw1[:, ic * 128:(ic + 1) * 128].rearrange("(c p) n -> p c n", p=128)
                ctx.inc(g.dma_start(out=swc[:, 0:C8 * 128].rearrange("p (c n) -> p c n", c=C8), in_=src1), "swld", 16)
                src2 = I_sw2[:, ic * 128:(ic + 1) * 128].rearrange("(c p) n -> p c n", p=128)
                ctx.inc(g.dma_start(out=swc[:, C8 * 128:].rearrange("p (c n) -> p c n", c=C8), in_=src2), "swld", 16)
            block.gpsimd(_)

            z1_b = ctx.cnt["z1mm"]
            z1a_b = ctx.cnt["z1act"]
            z1e_b = ctx.cnt["z1ev"]

            def _(t, ic=ic, z1_b=z1_b, z1a_b=z1a_b, z1e_b=z1e_b):
                ctx.wait(t, "swld")
                ctx.wait(t, "xnbf")
                ctx.wait(t, "h_act")
                ctx.wait(t, "smlp_ev")
                ctx.wait(t, "rt_mm")
                for tt in range(NTT):
                    if z1a_b + tt > 0:
                        t.wait_ge(ctx.sem("z1act"), z1a_b + tt)
                    if z1e_b + tt > 0:
                        t.wait_ge(ctx.sem("z1ev"), z1e_b + tt)
                    for w_ in range(2):
                        pb = psA if w_ == 0 else psB_
                        mi = None
                        for kc in range(C8):
                            mi = t.matmul(pb[:, 0:TT],
                                          swc[:, w_ * C8 * 128 + kc * 128: w_ * C8 * 128 + kc * 128 + 128],
                                          xn_bf[:, kc * TPC + tt * TT: kc * TPC + (tt + 1) * TT],
                                          start=(kc == 0), stop=(kc == C8 - 1))
                        ctx.inc(mi, "z1mm")
            block.tensor(_)

            def _(s, ic=ic, z1_b=z1_b):
                for tt in range(NTT):
                    s.wait_ge(ctx.sem("z1mm"), z1_b + tt * 2 + 1)
                    li = s.activation(ztmp[:, tt * TT:(tt + 1) * TT], psA[:, 0:TT],
                                      AF.Silu, bias=0.0, scale=1.0)
                    ctx.inc(li, "z1act")
            block.scalar(_)

            def _(v, ic=ic, z1_b=z1_b, z1a_b=z1a_b):
                for tt in range(NTT):
                    v.wait_ge(ctx.sem("z1act"), z1a_b + tt + 1)
                    v.wait_ge(ctx.sem("z1mm"), z1_b + tt * 2 + 2)
                    li = v.tensor_tensor(zg[:, ic * TPC + tt * TT: ic * TPC + (tt + 1) * TT],
                                         ztmp[:, tt * TT:(tt + 1) * TT], psB_[:, 0:TT], ALU.mult)
                    ctx.inc(li, "z1ev")
            block.vector(_)

        ctx.sem("s3mm"), ctx.sem("s3ev"), ctx.sem("s3ld")
        for c in range(C8):
            def _(g, c=c):
                ctx.wait(g, "s3mm")
                src = I_sw3[:, c * 128:(c + 1) * 128].rearrange("(ic p) n -> p ic n", p=128)
                ctx.inc(g.dma_start(out=sw3c[:].rearrange("p (ic n) -> p ic n", ic=16), in_=src), "s3ld", 16)
            block.gpsimd(_)

            s3_b = ctx.cnt["s3mm"]
            s3e_b = ctx.cnt["s3ev"]

            def _(t, c=c, s3_b=s3_b, s3e_b=s3e_b):
                ctx.wait(t, "s3ld")
                ctx.wait(t, "z1ev")
                ctx.wait(t, "smlp_ev")
                for tt in range(NTT):
                    if s3e_b + tt - 1 > 0:
                        t.wait_ge(ctx.sem("s3ev"), s3e_b + tt - 1)
                    pb = psC if tt % 2 == 0 else psD
                    mi = None
                    for ic in range(16):
                        mi = t.matmul(pb[:, 0:TT],
                                      sw3c[:, ic * 128:(ic + 1) * 128],
                                      zg[:, ic * TPC + tt * TT: ic * TPC + (tt + 1) * TT],
                                      start=(ic == 0), stop=(ic == 15))
                    ctx.inc(mi, "s3mm")
            block.tensor(_)

            def _(v, c=c, s3_b=s3_b):
                for tt in range(NTT):
                    v.wait_ge(ctx.sem("s3mm"), s3_b + tt + 1)
                    pb = psC if tt % 2 == 0 else psD
                    li = v.tensor_copy(gacc[:, c * TPC + tt * TT: c * TPC + (tt + 1) * TT], pb[:, 0:TT])
                    ctx.inc(li, "s3ev")
            block.vector(_)
        ss_.close()

        if _DBG3:
            def _(sy):
                ctx.wait(sy, "s3ev")
                oS = nc.dram_tensor("dbg_gaccS", [128, C8 * TPC], F32, kind="ExternalOutput").ap()
                ctx.inc(sy.dma_start(out=oS, in_=gacc[:]), "dbgS", 16)
            block.sync(_)

        ms = ExitStack()
        psT = ms.enter_context(nc.psum_tensor("psT", [128, 128], BF16, allow_name_mangling=True))
        we1 = ms.enter_context(nc.sbuf_tensor("we1", [128, C8 * D], BF16, allow_name_mangling=True))
        we3 = ms.enter_context(nc.sbuf_tensor("we3", [128, C8 * D], BF16, allow_name_mangling=True))
        we2 = ms.enter_context(nc.sbuf_tensor("we2", [128, C8 * D], BF16, allow_name_mangling=True))
        xg_tm = ms.enter_context(nc.sbuf_tensor("xg_tm", [128, 6 * 1024], BF16, allow_name_mangling=True))
        xgT = ms.enter_context(nc.sbuf_tensor("xgT", [128, C8 * CAP], BF16, allow_name_mangling=True))
        hsb = ms.enter_context(nc.sbuf_tensor("hsb", [128, C8 * CAP], BF16, allow_name_mangling=True))
        x2sb = ms.enter_context(nc.sbuf_tensor("x2sb", [128, CAP], F32, allow_name_mangling=True))
        prodF = ms.enter_context(nc.sbuf_tensor("prodF", [128, CAP], BF16, allow_name_mangling=True))
        ptm = ms.enter_context(nc.sbuf_tensor("ptm", [128, 6 * 1024], BF16, allow_name_mangling=True))
        b123 = ms.enter_context(nc.sbuf_tensor("b123", [128, 3 * 64], F32, allow_name_mangling=True))

        def _(sy):
            ctx.wait(sy, "s3mm")
            ctx.wait(sy, "s3ev")
            ctx.wait(sy, "z1ev")
            ctx.wait(sy, "gsct")
            ctx.wait(sy, "gcarup")
            ctx.inc(sy.dma_start(out=b123[:, 0:64], in_=I_b1), "bld", 16)
            ctx.inc(sy.dma_start(out=b123[:, 64:128], in_=I_b2), "bld", 16)
            ctx.inc(sy.dma_start(out=b123[:, 128:192], in_=I_b3), "bld", 16)
        block.sync(_)

        for sem in ("ewld", "ew2ld", "p1mm", "hact", "x2mm", "x2ev", "x3mm",
                    "pev", "ptp_mm", "ptp_ev", "pst", "xgld", "xg2_mm", "xg2_ev"):
            ctx.sem(sem)
        SUBW = (512, 256)
        SUBO = (0, 512)

        for e in range(8):
            def _(g, e=e):
                ctx.wait(g, "x2mm")
                ctx.wait(g, "x3mm")
                if e == 0:
                    ctx.wait(g, "s3mm")
                    ctx.wait(g, "s3ev")
                    ctx.wait(g, "gsct")
                    ctx.wait(g, "gcarup")
                ctx.wait(g, "p1mm")
                emit_weight_load(ctx, g, bass.AP(tensor=I_mw1.tensor, offset=e * D * D,
                                                 ap=[[D, D], [1, D]]), we1, sem="ewld")
                emit_weight_load(ctx, g, bass.AP(tensor=I_mw3.tensor, offset=e * D * D,
                                                 ap=[[D, D], [1, D]]), we3, sem="ewld")
                emit_weight_load(ctx, g, bass.AP(tensor=I_mw2.tensor, offset=e * D * D,
                                                 ap=[[D, D], [1, D]]), we2, sem="ew2ld")
            block.gpsimd(_)

            def _(sy, e=e):
                ctx.wait(sy, "gsct")
                ctx.wait(sy, "xg2_mm")
                src = xbkt[e * CAP:(e + 1) * CAP, :].rearrange("(tc p) f -> p tc f", p=128)
                ctx.inc(sy.dma_start(out=xg_tm[:].rearrange("p (tc f) -> p tc f", tc=6), in_=src), "xgld", 16)
            block.sync(_)

            x2_b0 = ctx.cnt["xg2_mm"]
            x2e_b0 = ctx.cnt["xg2_ev"]

            def _(t, e=e, x2_b0=x2_b0, x2e_b0=x2e_b0):
                ctx.wait(t, "xgld")
                ctx.wait(t, "x3mm")
                n = 0
                for tc6 in range(6):
                    for c in range(C8):
                        if x2e_b0 + n > 0:
                            t.wait_ge(ctx.sem("xg2_ev"), x2e_b0 + n)
                        ctx.inc(t.transpose(psT[:], xg_tm[:, tc6 * 1024 + c * 128: tc6 * 1024 + c * 128 + 128],
                                            eye_bs[:]), "xg2_mm")
                        n += 1
            block.tensor(_)

            def _(v, e=e, x2_b0=x2_b0):
                n = 0
                for tc6 in range(6):
                    for c in range(C8):
                        n += 1
                        v.wait_ge(ctx.sem("xg2_mm"), x2_b0 + n)
                        ctx.inc(v.tensor_copy(xgT[:, c * CAP + tc6 * 128: c * CAP + tc6 * 128 + 128],
                                              psT[:]), "xg2_ev")
            block.vector(_)

            p1_b = ctx.cnt["p1mm"]
            ha_b = ctx.cnt["hact"]

            def _(t, e=e, p1_b=p1_b, ha_b=ha_b):
                ctx.wait(t, "ewld")
                ctx.wait(t, "xg2_ev")
                ctx.wait(t, "hact")
                ctx.wait(t, "z1act")
                ctx.wait(t, "z1ev")
                n = 0
                for hc in range(C8):
                    for sub in range(2):
                        if ha_b + n - 1 > 0:
                            t.wait_ge(ctx.sem("hact"), ha_b + n - 1)
                        pb = psA if n % 2 == 0 else psB_
                        mi = None
                        for kc in range(C8):
                            mi = t.matmul(pb[:, 0:SUBW[sub]],
                                          we1[:, kc * D + hc * 128: kc * D + hc * 128 + 128],
                                          xgT[:, kc * CAP + SUBO[sub]: kc * CAP + SUBO[sub] + SUBW[sub]],
                                          start=(kc == 0), stop=(kc == C8 - 1))
                        ctx.inc(mi, "p1mm")
                        n += 1
            block.tensor(_)

            def _(s, e=e, p1_b=p1_b):
                ctx.wait(s, "bld")
                n = 0
                for hc in range(C8):
                    for sub in range(2):
                        n += 1
                        s.wait_ge(ctx.sem("p1mm"), p1_b + n)
                        pb = psA if (n - 1) % 2 == 0 else psB_
                        li = s.activation(hsb[:, hc * CAP + SUBO[sub]: hc * CAP + SUBO[sub] + SUBW[sub]],
                                          pb[:, 0:SUBW[sub]], AF.Silu,
                                          bias=b123[:, e * 8 + hc: e * 8 + hc + 1], scale=1.0)
                        ctx.inc(li, "hact")
            block.scalar(_)


            x2_b = ctx.cnt["x2mm"]
            x3_b = ctx.cnt["x3mm"]
            pe_b = ctx.cnt["pev"]
            pt_b = ctx.cnt["ptp_mm"]
            pte_b = ctx.cnt["ptp_ev"]

            def _(t, e=e, x2_b=x2_b, x3_b=x3_b, pe_b=pe_b, pt_b=pt_b, pte_b=pte_b, ha_b=ha_b):
                ctx.wait(t, "ew2ld")
                t.wait_ge(ctx.sem("hact"), ha_b + 16)
                n = 0
                npt = 0
                for c in range(C8):
                    for sub in range(2):
                        if x2_b + n - 1 > 0:
                            t.wait_ge(ctx.sem("x2ev"), x2_b + n - 1)
                        pba = psC if n % 2 == 0 else psD
                        mi = None
                        for hc in range(C8):
                            mi = t.matmul(pba[:, 0:SUBW[sub]],
                                          we2[:, hc * D + c * 128: hc * D + c * 128 + 128],
                                          hsb[:, hc * CAP + SUBO[sub]: hc * CAP + SUBO[sub] + SUBW[sub]],
                                          start=(hc == 0), stop=(hc == C8 - 1))
                        ctx.inc(mi, "x2mm")
                        if pe_b + n - 1 > 0:
                            t.wait_ge(ctx.sem("pev"), pe_b + n - 1)
                        pbb = psE if n % 2 == 0 else psF
                        mi = None
                        for kc in range(C8):
                            mi = t.matmul(pbb[:, 0:SUBW[sub]],
                                          we3[:, kc * D + c * 128: kc * D + c * 128 + 128],
                                          xgT[:, kc * CAP + SUBO[sub]: kc * CAP + SUBO[sub] + SUBW[sub]],
                                          start=(kc == 0), stop=(kc == C8 - 1))
                        ctx.inc(mi, "x3mm")
                        n += 1
                    t.wait_ge(ctx.sem("pev"), pe_b + n)
                    for tc6 in range(6):
                        if pte_b + npt > 0:
                            t.wait_ge(ctx.sem("ptp_ev"), pte_b + npt)
                        ctx.inc(t.transpose(psT[:], prodF[:, tc6 * 128: tc6 * 128 + 128], eye_bs[:]),
                                "ptp_mm")
                        npt += 1
            block.tensor(_)

            def _(v, e=e, x2_b=x2_b, x3_b=x3_b, pe_b=pe_b, pt_b=pt_b):
                ctx.wait(v, "pst")
                n = 0
                npt = 0
                for c in range(C8):
                    for sub in range(2):
                        v.wait_ge(ctx.sem("x2mm"), x2_b + n + 1)
                        pba = psC if n % 2 == 0 else psD
                        li = v.tensor_scalar(x2sb[:, SUBO[sub]: SUBO[sub] + SUBW[sub]],
                                             pba[:, 0:SUBW[sub]],
                                             b123[:, 64 + e * 8 + c: 64 + e * 8 + c + 1], None, ALU.add)
                        ctx.inc(li, "x2ev")
                        v.wait_ge(ctx.sem("x2ev"), ctx.cnt["x2ev"])
                        v.wait_ge(ctx.sem("x3mm"), x3_b + n + 1)
                        pbb = psE if n % 2 == 0 else psF
                        li = v.scalar_tensor_tensor(prodF[:, SUBO[sub]: SUBO[sub] + SUBW[sub]],
                                                    pbb[:, 0:SUBW[sub]],
                                                    b123[:, 128 + e * 8 + c: 128 + e * 8 + c + 1],
                                                    x2sb[:, SUBO[sub]: SUBO[sub] + SUBW[sub]],
                                                    ALU.add, ALU.mult)
                        ctx.inc(li, "pev")
                        n += 1
                    for tc6 in range(6):
                        npt += 1
                        v.wait_ge(ctx.sem("ptp_mm"), pt_b + npt)
                        ctx.inc(v.tensor_copy(ptm[:, tc6 * 1024 + c * 128: tc6 * 1024 + c * 128 + 128],
                                              psT[:]), "ptp_ev")
            block.vector(_)

            def _(sy, e=e):
                ctx.wait(sy, "ptp_ev")
                dst = prodbkt[e * CAP:(e + 1) * CAP, :].rearrange("(tc p) f -> p tc f", p=128)
                ctx.inc(sy.dma_start(out=dst, in_=ptm[:].rearrange("p (tc f) -> p tc f", tc=6)), "pst", 16)
            block.sync(_)

        if _DBG3:
            def _(sy):
                ctx.wait(sy, "pst")
                o3 = nc.dram_tensor("dbg_prodbkt", [8 * 768, D], BF16, kind="ExternalOutput").ap()
                ctx.inc(sy.dma_start(out=o3, in_=prodbkt), "dbgst", 16)
            block.sync(_)

        ys = ExitStack()
        g0 = ys.enter_context(nc.sbuf_tensor("g0", [128, 1024], BF16, allow_name_mangling=True))
        g1 = ys.enter_context(nc.sbuf_tensor("g1", [128, 1024], BF16, allow_name_mangling=True))
        ysum = ys.enter_context(nc.sbuf_tensor("ysum", [128, 1024], F32, allow_name_mangling=True))
        for sem in ("ygth", "ysumv", "ytp_mm", "yacc"):
            ctx.sem(sem)
        for tc in range(16):
            def _(g_, tc=tc):
                ctx.wait(g_, "pst")
                ctx.wait(g_, "ysumv")
                g_.indirect_dma_start(
                    g0[:], None, prodbkt,
                    bass.IndirectOffsetOnAxis(ap=addru[:, 2 * tc: 2 * tc + 1], axis=0)).then_inc(ctx.sem("ygth"), 16)
                g_.indirect_dma_start(
                    g1[:], None, prodbkt,
                    bass.IndirectOffsetOnAxis(ap=addru[:, 2 * tc + 1: 2 * tc + 2], axis=0)).then_inc(ctx.sem("ygth"), 16)
                ctx.cnt[_SEM_ALIAS.get("ygth", "ygth")] += 32
            block.gpsimd(_)

            def _(v, tc=tc):
                ctx.wait(v, "ygth")
                ctx.wait(v, "ytp_mm")
                ctx.inc(v.tensor_tensor(ysum[:], g0[:], g1[:], ALU.add), "ysumv")
            block.vector(_)

            yt_b = ctx.cnt["ytp_mm"]
            ya_b = ctx.cnt["yacc"]

            def _(t, tc=tc, ya_b=ya_b):
                ctx.wait(t, "ysumv")
                n = 0
                for c in range(C8):
                    if ya_b + n > 0:
                        t.wait_ge(ctx.sem("yacc"), ya_b + n)
                    ctx.inc(t.transpose(psG[:, 0:128], ysum[:, c * 128:(c + 1) * 128], eyef_sb[:]),
                            "ytp_mm")
                    n += 1
            block.tensor(_)

            def _(v, tc=tc, yt_b=yt_b):
                if _DBG3 and tc == 0:
                    ctx.wait(v, "dbgS")
                n = 0
                for c in range(C8):
                    n += 1
                    v.wait_ge(ctx.sem("ytp_mm"), yt_b + n)
                    gsl = slice(c * TPC + tc * 128, c * TPC + (tc + 1) * 128)
                    li = v.tensor_tensor(gacc[:, gsl], gacc[:, gsl], psG[:, 0:128], ALU.add)
                    ctx.inc(li, "yacc")
            block.vector(_)
        if _DBG3:
            def _(sy):
                ctx.wait(sy, "yacc")
                o4 = nc.dram_tensor("dbg_gaccY", [128, C8 * TPC], F32, kind="ExternalOutput").ap()
                ctx.inc(sy.dma_start(out=o4, in_=gacc[:]), "dbgst", 16)
            block.sync(_)
        ys.close()
        ms.close()

        fs = ExitStack()
        x_f = fs.enter_context(nc.sbuf_tensor("x_f", [128, C8 * TT], F32, allow_name_mangling=True))
        s_f = fs.enter_context(nc.sbuf_tensor("s_f", [128, C8 * TT], F32, allow_name_mangling=True))
        o_f = fs.enter_context(nc.sbuf_tensor("o_f", [128, C8 * TT], F32, allow_name_mangling=True))

        ctx.sem("fsil3"), ctx.sem("fcomb"), ctx.sem("fst")
        def _(s):
            ctx.wait(s, "yacc")
            li = None
            for c in range(C8):
                for tt in range(NTT):
                    li = s.activation(gacc[:, c * TPC + tt * TT: c * TPC + (tt + 1) * TT],
                                      gacc[:, c * TPC + tt * TT: c * TPC + (tt + 1) * TT],
                                      AF.Silu, bias=0.0, scale=1.0)
            ctx.inc(li, "fsil3")
        block.scalar(_)

        for tt in range(NTT):
            s0 = tt * TT

            def _(sy, s0=s0):
                ctx.wait(sy, "fcomb")
                ctx.wait(sy, "yacc")
                srcx = I_xT.rearrange("(c p) t -> p c t", p=128)[:, :, s0:s0 + TT]
                ctx.inc(sy.dma_start(out=x_f[:].rearrange("p (c t) -> p c t", c=C8), in_=srcx), "fxld", 16)
                srcs = sdram.rearrange("(c p) t -> p c t", p=128)[:, :, s0:s0 + TT]
                ctx.inc(sy.dma_start(out=s_f[:].rearrange("p (c t) -> p c t", c=C8), in_=srcs), "fxld", 16)
            block.sync(_)

            def _(v, s0=s0, tt=tt):
                ctx.wait(v, "fxld")
                ctx.wait(v, "fsil3")
                ctx.wait(v, "fst")
                li = None
                for c in range(C8):
                    ctx.chain(v, v.tensor_tensor(o_f[:, c * TT:(c + 1) * TT],
                                                 s_f[:, c * TT:(c + 1) * TT],
                                                 gacc[:, c * TPC + s0: c * TPC + s0 + TT], ALU.mult))
                    li = v.tensor_tensor(o_f[:, c * TT:(c + 1) * TT],
                                         o_f[:, c * TT:(c + 1) * TT],
                                         x_f[:, c * TT:(c + 1) * TT], ALU.add)
                ctx.inc(li, "fcomb")
            block.vector(_)

            def _(sy, s0=s0):
                ctx.wait(sy, "fcomb")
                dst = O_x1.rearrange("(c p) t -> p c t", p=128)[:, :, s0:s0 + TT]
                ctx.inc(sy.dma_start(out=dst, in_=o_f[:].rearrange("p (c t) -> p c t", c=C8)), "fst", 16)
            block.sync(_)

        def _(sy):
            ctx.wait(sy, "fst")
        block.sync(_)
        fs.close()

    fin()
    return nc



def build_launch4(F=None):
    nc, es, block, ctx, dt, fin = stage_begin(F, None)
    xT = dt("x1T", [D, TPC], F32, "ExternalInput")
    Wq = dt("Wq1", [D, D], F32, "ExternalInput")
    Wk = dt("Wk1", [D, D], F32, "ExternalInput")
    Wv = dt("Wv1", [D, D], F32, "ExternalInput")
    Bq = dt("bq1", [128, C8], F32, "ExternalInput")
    Bk = dt("bk1", [128, C8], F32, "ExternalInput")
    Bv = dt("bv1", [128, C8], F32, "ExternalInput")
    eyeb = dt("eyeb1", [128, 128], BF16, "ExternalInput") if F is not None else dt("eyeb", [128, 128], BF16, "ExternalInput")

    O_xn = dt("xn1", [D, TPC], F32, "ExternalOutput")
    O_qh = dt("Q1", [D, TPC], BF16, "ExternalOutput")
    O_kh = dt("K1", [D, TPC], BF16, "ExternalOutput")
    O_vt = dt("V1t", [TPC, NH * 65], BF16, "ExternalOutput")
    ccf1 = dt("ccf1in", [1, 4096], F32, "Internal") if F is not None else None

    with es:
        x_tt = sb(ctx, "x_tt", 128, C8 * TT)
        xn_tt = sb(ctx, "xn_tt", 128, C8 * TT)
        sq = sb(ctx, "sq", 128, C8 * TT)
        rows = sb(ctx, "rows", 1, 4 * TT)
        xn2 = sb(ctx, "xn2", 128, C8 * TT, BF16)
        ob_tt = sb(ctx, "ob_tt", 128, C8 * TT, BF16)
        vt_tt = sb(ctx, "vt_tt", 128, (TT // 128) * NH * 65, BF16)
        xn_bt = sb(ctx, "xn_bt", 128, C8 * TT, BF16)
        xtm_tt = sb(ctx, "xtm_tt", 128, (TT // 128) * D, BF16)
        wq_sb = sb(ctx, "wq_sb", 128, C8 * D, BF16)
        wk_sb = sb(ctx, "wk_sb", 128, C8 * D, BF16)
        wv_sb = sb(ctx, "wv_sb", 128, C8 * D, BF16)
        bqkv = sb(ctx, "bqkv", 128, 3 * C8)
        eye_sb = sb(ctx, "eye_sb", 128, 128, BF16)
        ones_sb = sb(ctx, "ones_sb", 128, 128)
        cst = sb(ctx, "cst", 128, 1)

        ps_a = psum(ctx, "ps_a", 128, TT)
        ps_b = psum(ctx, "ps_b", 128, TT)
        ps_r1 = psum(ctx, "ps_r1", 1, TT)
        ps_r2 = psum(ctx, "ps_r2", 1, TT)
        ps_bc1 = psum(ctx, "ps_bc1", 128, TT)
        ps_bc2 = psum(ctx, "ps_bc2", 128, TT)
        ps_t1 = psum(ctx, "ps_t1", 128, 128, BF16)

        def _(g):
            emit_weight_load(ctx, g, Wq, wq_sb)
            emit_weight_load(ctx, g, Wk, wk_sb)
            emit_weight_load(ctx, g, Wv, wv_sb)
        block.gpsimd(_)

        if F is not None:
            zsb = sb(ctx, "e1zsb", 128, 4160, BF16)
            zsf = sb(ctx, "e1zsf", 128, 32)
            emit_exchange_zero(ctx, block, F, "e1", "K1s", "V1ts", zsb, zsf, 1)

        def _(sy):
            ctx.inc(sy.dma_start(out=bqkv[:, 0:C8], in_=Bq), "cld", 16)
            ctx.inc(sy.dma_start(out=bqkv[:, C8:2 * C8], in_=Bk), "cld", 16)
            ctx.inc(sy.dma_start(out=bqkv[:, 2 * C8:3 * C8], in_=Bv), "cld", 16)
            ctx.inc(sy.dma_start(out=eye_sb[:], in_=eyeb), "cld", 16)
        block.sync(_)

        def _(v):
            v.memset(ones_sb[:], 1.0)
            ctx.inc(v.memset(cst[:, 0:1], 1e-5), "vinit")
        block.vector(_)

        for tt in range(NTT):
            s0 = tt * TT

            def _(sy, s0=s0):
                ctx.wait(sy, "xn2_ready")
                src = xT.rearrange("(c p) t -> p c t", p=128)[:, :, s0:s0 + TT]
                ctx.inc(sy.dma_start(out=x_tt[:].rearrange("p (c t) -> p c t", c=C8), in_=src), "xld", 16)
            block.sync(_)

            def _(s):
                ctx.wait(s, "xld")
                ctx.wait(s, "mm_stats")
                ctx.inc(s.activation(sq[:], x_tt[:], AF.Square, bias=0.0, scale=1.0), "sqk")
            block.scalar(_)

            def _(t):
                ctx.wait(t, "xld")
                ctx.wait(t, "vinit")
                ctx.wait(t, "var_ready")
                for c in range(C8):
                    t.matmul(ps_r1[:], ones_sb[:, 0:1], x_tt[:, c * TT:(c + 1) * TT],
                             start=(c == 0), stop=(c == C8 - 1))
                ctx.wait(t, "sqk")
                mi = None
                for c in range(C8):
                    mi = t.matmul(ps_r2[:], ones_sb[:, 0:1], sq[:, c * TT:(c + 1) * TT],
                                  start=(c == 0), stop=(c == C8 - 1))
                ctx.inc(mi, "mm_stats")
            block.tensor(_)

            def _(v):
                ctx.wait(v, "mm_stats")
                S_, SS = rows[:, 0:TT], rows[:, TT:2 * TT]
                A_, B_ = rows[:, 2 * TT:3 * TT], rows[:, 3 * TT:4 * TT]
                ctx.chain(v, v.tensor_single_scalar(S_, ps_r1[:], 1.0 / D, ALU.mult))
                ctx.chain(v, v.tensor_single_scalar(SS, ps_r2[:], 1.0 / D, ALU.mult))
                ctx.chain(v, v.tensor_tensor(A_, S_, S_, ALU.mult))
                ctx.inc(v.tensor_tensor(A_, SS, A_, ALU.subtract), "var_ready")
            block.vector(_)

            def _(s):
                ctx.wait(s, "var_ready")
                ctx.inc(s.activation(rows[:, 2 * TT:3 * TT], rows[:, 2 * TT:3 * TT],
                                     AF.Sqrt, bias=cst[0:1, 0:1], scale=1.0), "std_ready")
            block.scalar(_)

            def _(v):
                ctx.wait(v, "std_ready")
                S_ = rows[:, 0:TT]
                A_, B_ = rows[:, 2 * TT:3 * TT], rows[:, 3 * TT:4 * TT]
                ctx.chain(v, v.reciprocal(A_, A_))
                ctx.chain(v, v.tensor_tensor(B_, S_, A_, ALU.mult))
                ctx.inc(v.tensor_single_scalar(B_, B_, -1.0, ALU.mult), "ab_ready")
            block.vector(_)

            def _(t):
                ctx.wait(t, "ab_ready")
                ctx.wait(t, "xn2_ready")
                t.matmul(ps_bc1[:], ones_sb[0:1, 0:128], rows[0:1, 2 * TT:3 * TT], start=True, stop=True)
                ctx.inc(t.matmul(ps_bc2[:], ones_sb[0:1, 0:128], rows[0:1, 3 * TT:4 * TT],
                                 start=True, stop=True), "bc_ready")
            block.tensor(_)

            def _(v):
                ctx.wait(v, "bc_ready")
                ctx.wait(v, "xnst")
                li = None
                for c in range(C8):
                    xc = x_tt[:, c * TT:(c + 1) * TT]
                    xnc = xn_tt[:, c * TT:(c + 1) * TT]
                    ctx.chain(v, v.tensor_tensor(xnc, xc, ps_bc1[:], ALU.mult))
                    li = v.tensor_tensor(xnc, xnc, ps_bc2[:], ALU.add)
                ctx.inc(li, "xn_ready")
            block.vector(_)

            def _(sy, s0=s0, tt=tt):
                ctx.wait(sy, "xn_ready")
                dst = O_xn.rearrange("(c p) t -> p c t", p=128)[:, :, s0:s0 + TT]
                ctx.inc(sy.dma_start(out=dst, in_=xn_tt[:].rearrange("p (c t) -> p c t", c=C8)), "xnst", 16)
                if F is not None:
                    for c in range(C8):
                        dstb = bass.AP(tensor=ccf1.tensor, offset=ccf1.offset + tt * 1024 + c * 128,
                                       ap=[[1, 128], [1, 1]])
                        ctx.inc(sy.dma_start(out=dstb, in_=xn_tt[:, c * TT + TT - 1: c * TT + TT]), "xnst", 16)
            block.sync(_)

            def _(v):
                ctx.wait(v, "xn_ready")
                ctx.inc(v.tensor_copy(xn2[:], xn_tt[:]), "xn2_ready")
            block.vector(_)

            for idx, (nm, w_sb_, out_dram) in enumerate((("q", wq_sb, O_qh), ("k", wk_sb, O_kh))):
                def evac(v, c, pb, idx=idx):
                    return v.tensor_scalar(ob_tt[:, c * TT:(c + 1) * TT], pb[:, 0:TT],
                                           bqkv[:, idx * C8 + c: idx * C8 + c + 1], None, ALU.add)
                emit_proj(ctx, "p" + nm, w_sb_, xn2, (0, TT), evac,
                          xw=TT, pbanks=(ps_a, ps_b),
                          wait_before=("xn2_ready", "wld", "qhst"),
                          wait_before_ev=("qhst", "cld"))

                def _(sy, s0=s0, out_dram=out_dram, nm=nm):
                    ctx.wait(sy, "p" + nm + "_ev")
                    if F is not None and nm == "k":
                        dst = F.io[f"K1_t{s0 // TT}"].rearrange("(c p) q -> p c q", p=128)
                    else:
                        dst = out_dram.rearrange("(c p) t -> p c t", p=128)[:, :, s0:s0 + TT]
                    ctx.inc(sy.dma_start(out=dst, in_=ob_tt[:].rearrange("p (c t) -> p c t", c=C8)), "qhst", 16)
                block.sync(_)

            def vevac(v, c, pb):
                return v.tensor_scalar(ob_tt[:, c * TT:(c + 1) * TT], pb[:, 0:TT],
                                       bqkv[:, 2 * C8 + c: 2 * C8 + c + 1], None, ALU.add)
            emit_proj(ctx, "pv", wv_sb, xn2, (0, TT), vevac,
                      xw=TT, pbanks=(ps_a, ps_b),
                      wait_before=("xn2_ready", "wld", "qhst"),
                      wait_before_ev=("qhst", "cld"))

            ctx.sem("vtp_mm"), ctx.sem("vtp_ev")
            tp_base = ctx.cnt["vtp_mm"]
            te_base = ctx.cnt["vtp_ev"]

            def _(t, tp_base=tp_base, te_base=te_base):
                ctx.wait(t, "pv_ev")
                ctx.wait(t, "vtst")
                n = 0
                for tc in range(TT // 128):
                    for c in range(C8):
                        if n >= 1:
                            t.wait_ge(ctx.sem("vtp_ev"), te_base + n)
                        ctx.inc(t.transpose(ps_t1[:], ob_tt[:, c * TT + tc * 128: c * TT + tc * 128 + 128],
                                            eye_sb[:]), "vtp_mm")
                        n += 1
            block.tensor(_)

            def _(v, tp_base=tp_base):
                n = 0
                for tc in range(TT // 128):
                    base = tc * NH * 65
                    for c in range(C8):
                        n += 1
                        v.wait_ge(ctx.sem("vtp_mm"), tp_base + n)
                        h0, h1 = 2 * c, 2 * c + 1
                        v.tensor_copy(vt_tt[:, base + h0 * 65: base + h0 * 65 + 64], ps_t1[:, 0:64])
                        ctx.inc(v.tensor_copy(vt_tt[:, base + h1 * 65: base + h1 * 65 + 64],
                                              ps_t1[:, 64:128]), "vtp_ev")
                    li = v.tensor_copy(
                        vt_tt[:].rearrange("p (tc h x) -> p tc h x", tc=TT // 128, h=NH)[:, tc, :, 64:65],
                        ones_sb[:, 0:NH].rearrange("p (h x) -> p h x", x=1))
                ctx.inc(li, "vt_ready")
            block.vector(_)

            def _(sy, s0=s0):
                ctx.wait(sy, "vt_ready")
                if F is not None:
                    dst = F.io[f"V1t_t{s0 // TT}"].rearrange("(tc p) f -> p tc f", p=128)
                else:
                    dst = O_vt[s0:s0 + TT, :].rearrange("(tc p) f -> p tc f", p=128)
                src = vt_tt[:].rearrange("p (tc f) -> p tc f", tc=TT // 128)
                ctx.inc(sy.dma_start(out=dst, in_=src), "vtst", 16)
            block.sync(_)
            if F is not None:
                emit_exchange_tile(ctx, block, F, "e1", tt, "cc1", "K1s", "V1ts",
                                   ctx.cnt["qhst"], ctx.cnt["vtst"])

        def _(sy):
            ctx.wait(sy, "xnst")
            ctx.wait(sy, "qhst")
            ctx.wait(sy, "vtst")
        block.sync(_)

    fin()
    return nc




def build_launch5(F=None):
    nc, es, block, ctx, dt, fin = stage_begin(F, None)
    I_x1 = dt("x1T", [D, TPC], F32, "ExternalInput")
    I_xn = dt("xn1", [D, TPC], F32, "ExternalInput")
    I_xp = dt("xnprev", [D, 4], F32, "ExternalInput")
    I_qh = dt("Q1", [D, TPC], BF16, "ExternalInput")
    I_khs = dt("K1s", [D, NSLOT], BF16, "ExternalInput")
    I_vtm = dt("V1ts", [NSLOT, NH * 65], BF16, "ExternalInput")
    I_wo = dt("Wo1", [D, D], F32, "ExternalInput")
    I_bo = dt("bo1", [128, C8], F32, "ExternalInput")
    I_wr = dt("WrT", [D, D], F32, "ExternalInput")
    I_wk = dt("WkT", [D, D], F32, "ExternalInput")
    I_wv = dt("WvT", [D, D], F32, "ExternalInput")
    I_mur = dt("mur", [128, C8], F32, "ExternalInput")
    I_muk = dt("muk", [128, C8], F32, "ExternalInput")
    I_mask = dt("mask1", [128, 128], F32, "ExternalInput")
    O_out = dt("out", [D, TPC], F32, "ExternalOutput")

    with es:
        s1 = sb(ctx, "s1", 128, C8 * TPC)
        bo_sb = sb(ctx, "bo_sb", 128, C8)
        ones_sb = sb(ctx, "ones_sb", 128, 128)
        ats = ExitStack()
        kh_g = ats.enter_context(nc.sbuf_tensor("kh_g", [128, 4 * NSLOT], BF16, allow_name_mangling=True))
        qh_g = ats.enter_context(nc.sbuf_tensor("qh_g", [128, 4 * TPC], BF16, allow_name_mangling=True))
        vt_g = ats.enter_context(nc.sbuf_tensor("vt_g", [128, 32 * 520], BF16, allow_name_mangling=True))
        ao = ats.enter_context(nc.sbuf_tensor("ao", [128, C8 * TPC], BF16, allow_name_mangling=True))
        Pt = ats.enter_context(nc.sbuf_tensor("Pt", [128, 2 * 512], BF16, allow_name_mangling=True))
        rr = ats.enter_context(nc.sbuf_tensor("rr", [128, 512], F32, allow_name_mangling=True))
        rb_sb = ats.enter_context(nc.sbuf_tensor("rb_sb", [64, 512], F32, allow_name_mangling=True))
        wo_sb = ats.enter_context(nc.sbuf_tensor("wo_sb", [128, C8 * D], BF16, allow_name_mangling=True))
        mask_sb = ats.enter_context(nc.sbuf_tensor("mask_sb", [128, 128], F32, allow_name_mangling=True))

        psS_a = psum(ctx, "psS_a", 128, 512)
        psS_b = psum(ctx, "psS_b", 128, 512)
        O_a = psum(ctx, "O_a", 65, 512)
        O_b = psum(ctx, "O_b", 65, 512)
        O_c = psum(ctx, "O_c", 65, 512)
        O_d = psum(ctx, "O_d", 65, 512)
        psB = psum(ctx, "psB", 64, 512)
        ps_m = psum(ctx, "ps_m", 128, 512)

        def _(sy):
            if F is not None:
                sy.wait_ge(ctx.sem("e1asm"), 48)
            ctx.inc(sy.dma_start(out=mask_sb[:], in_=I_mask), "cld", 16)
            ctx.inc(sy.dma_start(out=bo_sb[:], in_=I_bo), "cld", 16)
        block.sync(_)

        def _(g):
            if F is not None:
                g.wait_ge(ctx.sem("e1asm"), 48)
            emit_weight_load(ctx, g, I_wo, wo_sb, sem="wld")
        block.gpsimd(_)

        def _(v):
            if F is not None:
                v.wait_ge(ctx.sem("e1asm"), 48)
            ctx.inc(v.memset(ones_sb[:], 1.0), "vinit")
        block.vector(_)

        for sem in ("smm", "madd", "pexp", "ppv", "rb_used", "fr_mm", "fk_mm",
                    "fst", "fcomb", "wo1_ev", "fk_ev", "fr_ev"):
            ctx.sem(sem)
        rb_stage_b = ctx.cnt["rb_used"]

        def emit_epilogue5(ent):
            Ob_, h_, j_, pr_, ppv_end = ent

            rec_b = ctx.cnt_of("recip")

            def _(v):
                v.wait_ge(ctx.sem("ppv"), ppv_end)
                ctx.chain(v, v.tensor_single_scalar(rr[64:65, :], Ob_[64:65, :], 1e-30, ALU.add))
                ctx.inc(v.reciprocal(rr[64:65, :], rr[64:65, :]), "recip")
            block.vector(_)

            rbu_b = ctx.cnt_of("rb_used")

            def _(t):
                t.wait_ge(ctx.sem("recip"), rec_b + 1)
                if rbu_b > 0:
                    t.wait_ge(ctx.sem("rb_used"), rbu_b)
                ctx.inc(t.matmul(psB[:], ones_sb[64:65, 0:64], rr[64:65, :],
                                 start=True, stop=True), "mm_rb")
            block.tensor(_)

            mrb_b = ctx.cnt_of("mm_rb")

            def _(v):
                v.wait_ge(ctx.sem("mm_rb"), mrb_b)
                ctx.chain(v, v.tensor_copy(rb_sb[:], psB[:]))
                dst = ao[pr_:pr_ + 64, (h_ // 2) * TPC + j_ * 512: (h_ // 2) * TPC + j_ * 512 + 512]
                ctx.inc(v.tensor_tensor(dst, Ob_[0:64, :], rb_sb[:], ALU.mult), "rb_used")
            block.vector(_)

        NG = 2
        for g in range(NG):
            pend = []

            def _(sy, g=g):
                ctx.wait(sy, "rb_used")
                if F is not None:
                    sy.wait_ge(ctx.sem("e1asm"), 176)
                    sy.wait_ge(ctx.sem("e1asm2"), 128)
                src = I_khs.rearrange("(c p) s -> p c s", p=128)[:, 4 * g:4 * g + 4, :]
                ctx.inc(sy.dma_start(out=kh_g[:].rearrange("p (c s) -> p c s", c=4), in_=src), "gld", 16)
                srcq = I_qh.rearrange("(c p) s -> p c s", p=128)[:, 4 * g:4 * g + 4, :]
                ctx.inc(sy.dma_start(out=qh_g[:].rearrange("p (c s) -> p c s", c=4), in_=srcq), "gld", 16)
                srcv = I_vtm[:, g * 520:(g + 1) * 520].rearrange("(kc p) f -> p kc f", p=128)
                ctx.inc(sy.dma_start(out=vt_g[:].rearrange("p (kc f) -> p kc f", kc=32), in_=srcv), "gld", 16)
            block.sync(_)

            for hl in range(8):
                h = 8 * g + hl
                cl = hl // 2
                pr = (hl % 2) * 64
                for j in range(4):
                    G0 = 512 * (2 * j + 1)
                    nkc = 8 * j + 8
                    it = g * 32 + hl * 4 + j
                    Ob = (O_a, O_b, O_c, O_d)[it % 4]
                    smm_b = ctx.cnt["smm"]
                    madd_b = ctx.cnt["madd"]
                    pexp_b = ctx.cnt["pexp"]
                    ppv_b = ctx.cnt["ppv"]
                    geo = []
                    for kc in range(nkc):
                        c0 = 128 * kc - G0
                        geo.append((c0, max(0, c0), 0 <= c0 <= 384))
                    nw = [0]
                    for kc in range(nkc):
                        nw.append(nw[-1] + (1 if geo[kc][2] else 0))

                    if len(pend) == 2:
                        emit_epilogue5(pend.pop(0))

                    def _(t, g=g, hl=hl, cl=cl, pr=pr, j=j, nkc=nkc, Ob=Ob, it=it,
                          smm_b=smm_b, pexp_b=pexp_b, ppv_b=ppv_b, geo=geo):
                        ctx.wait(t, "gld")
                        ctx.wait(t, "vinit")
                        if pexp_b > 0:
                            t.wait_ge(ctx.sem("pexp"), pexp_b)
                        if it >= 4:
                            t.wait_ge(ctx.sem("rb_used"), rb_stage_b + it - 3)
                        for kc in range(nkc):
                            c0, qq0, has_w = geo[kc]
                            pb = psS_a if kc % 2 == 0 else psS_b
                            if kc >= 2:
                                t.wait_ge(ctx.sem("pexp"), pexp_b + kc - 1)
                            mi = t.matmul(pb[:, qq0:512],
                                          kh_g[pr:pr + 64, cl * NSLOT + 128 * kc: cl * NSLOT + 128 * kc + 128],
                                          qh_g[pr:pr + 64, cl * TPC + j * 512 + qq0: cl * TPC + (j + 1) * 512],
                                          start=True, stop=True)
                            ctx.inc(mi, "smm")
                            if kc >= 1:
                                pkc = kc - 1
                                qq0p = geo[pkc][1]
                                t.wait_ge(ctx.sem("pexp"), pexp_b + pkc + 1)
                                mi = t.matmul(Ob[:, qq0p:512],
                                              vt_g[:, pkc * 520 + hl * 65: pkc * 520 + hl * 65 + 65],
                                              Pt[:, (pkc % 2) * 512 + qq0p: (pkc % 2) * 512 + 512],
                                              start=(pkc == 0), stop=False)
                                ctx.inc(mi, "ppv")
                        kc = nkc - 1
                        qq0p = geo[kc][1]
                        t.wait_ge(ctx.sem("pexp"), pexp_b + kc + 1)
                        mi = t.matmul(Ob[:, qq0p:512],
                                      vt_g[:, kc * 520 + hl * 65: kc * 520 + hl * 65 + 65],
                                      Pt[:, (kc % 2) * 512 + qq0p: (kc % 2) * 512 + 512],
                                      start=(kc == 0), stop=True)
                        ctx.inc(mi, "ppv")
                    block.tensor(_)

                    def _(v, nkc=nkc, smm_b=smm_b, geo=geo):
                        ctx.wait(v, "cld")
                        for kc in range(nkc):
                            c0, qq0, has_w = geo[kc]
                            if not has_w:
                                continue
                            v.wait_ge(ctx.sem("smm"), smm_b + kc + 1)
                            pb = psS_a if kc % 2 == 0 else psS_b
                            li = v.tensor_tensor(pb[:, c0:c0 + 128], pb[:, c0:c0 + 128],
                                                 mask_sb[:], ALU.add)
                            ctx.inc(li, "madd")
                    block.vector(_)

                    def _(s, nkc=nkc, smm_b=smm_b, madd_b=madd_b,
                          pexp_b=pexp_b, ppv_b=ppv_b, geo=geo, nw=nw):
                        if ppv_b > 0:
                            s.wait_ge(ctx.sem("ppv"), ppv_b)
                        for kc in range(nkc):
                            c0, qq0, has_w = geo[kc]
                            pb = psS_a if kc % 2 == 0 else psS_b
                            if has_w:
                                s.wait_ge(ctx.sem("madd"), madd_b + nw[kc + 1])
                            else:
                                s.wait_ge(ctx.sem("smm"), smm_b + kc + 1)
                            if kc >= 2:
                                s.wait_ge(ctx.sem("ppv"), ppv_b + kc - 1)
                            li = s.activation(Pt[:, (kc % 2) * 512 + qq0: (kc % 2) * 512 + 512],
                                              pb[:, qq0:512], AF.Exp, bias=0.0, scale=0.125)
                            ctx.inc(li, "pexp")
                    block.scalar(_)

                    pend.append((Ob, h, j, pr, ctx.cnt["ppv"]))

            while pend:
                emit_epilogue5(pend.pop(0))

        for tt in range(NTT):
            s0 = tt * TT

            def evac(v, c, pb, s0=s0):
                return v.tensor_scalar(s1[:, c * TPC + s0: c * TPC + s0 + TT], pb[:, 0:TT],
                                       bo_sb[:, c:c + 1], None, ALU.add)
            emit_proj(ctx, "wo1", wo_sb, ao, (s0, TT), evac,
                      xw=TPC, pbanks=(psS_a, psS_b),
                      wait_before=("rb_used", "wld", "pexp"),
                      wait_before_ev=("cld",))

        ats.close()

        fs = ExitStack()
        wr_sb = fs.enter_context(nc.sbuf_tensor("wr_sb", [128, C8 * D], BF16, allow_name_mangling=True))
        wk_sb = fs.enter_context(nc.sbuf_tensor("wk_sb", [128, C8 * D], BF16, allow_name_mangling=True))
        wv_sb = fs.enter_context(nc.sbuf_tensor("wv_sb", [128, C8 * D], BF16, allow_name_mangling=True))
        xn_tt = fs.enter_context(nc.sbuf_tensor("xn_tt", [128, C8 * (TT + 1)], F32, allow_name_mangling=True))
        mus = fs.enter_context(nc.sbuf_tensor("mus", [128, 4 * C8], F32, allow_name_mangling=True))
        xr_tt = fs.enter_context(nc.sbuf_tensor("xr_tt", [128, C8 * TT], BF16, allow_name_mangling=True))
        xk_tt = fs.enter_context(nc.sbuf_tensor("xk_tt", [128, C8 * TT], BF16, allow_name_mangling=True))
        r_tt = fs.enter_context(nc.sbuf_tensor("r_tt", [128, C8 * TT], F32, allow_name_mangling=True))
        ka_tt = fs.enter_context(nc.sbuf_tensor("ka_tt", [128, C8 * TT], BF16, allow_name_mangling=True))
        tmp_tt = fs.enter_context(nc.sbuf_tensor("tmp_tt", [128, C8 * TT], F32, allow_name_mangling=True))
        x1_tt = fs.enter_context(nc.sbuf_tensor("x1_tt", [128, C8 * TT], F32, allow_name_mangling=True))

        def _(g_):
            ctx.wait(g_, "rb_used")
            ctx.wait(g_, "wo1_mm")
            emit_weight_load(ctx, g_, I_wr, wr_sb, sem="fwld")
            emit_weight_load(ctx, g_, I_wk, wk_sb, sem="fwld")
            emit_weight_load(ctx, g_, I_wv, wv_sb, sem="fwld")
        block.gpsimd(_)

        def _(sy):
            ctx.wait(sy, "wo1_mm")
            ctx.inc(sy.dma_start(out=mus[:, 0:C8], in_=I_mur), "muld", 16)
            ctx.inc(sy.dma_start(out=mus[:, C8:2 * C8], in_=I_muk), "muld", 16)
        block.sync(_)

        def _(v):
            ctx.wait(v, "muld")
            ctx.chain(v, v.tensor_scalar(mus[:, 2 * C8:3 * C8], mus[:, 0:C8], -1.0, 1.0, ALU.mult, ALU.add))
            ctx.inc(v.tensor_scalar(mus[:, 3 * C8:4 * C8], mus[:, C8:2 * C8], -1.0, 1.0, ALU.mult, ALU.add), "om_done")
        block.vector(_)

        for tt in range(NTT):
            s0 = tt * TT

            def _(sy, s0=s0, tt=tt):
                ctx.wait(sy, "fcomb")
                ctx.wait(sy, "fst")
                ctx.wait(sy, "e1asm")
                ctx.wait(sy, "e1asm2")
                ctx.wait(sy, "e1xpcp")
                if F is not None:
                    srcp = bass.AP(tensor=I_xp.tensor, offset=I_xp.offset + tt * 1024,
                                   ap=[[1, 128], [128, C8], [1, 1]])
                else:
                    srcp = I_xp.rearrange("(c p) j -> p c j", p=128)[:, :, tt:tt + 1]
                dstp = xn_tt[:].rearrange("p (c t) -> p c t", c=C8)[:, :, 0:1]
                ctx.inc(sy.dma_start(out=dstp, in_=srcp, allow_slow_non_contiguous=True), "fxnld", 16)
                src = I_xn.rearrange("(c p) t -> p c t", p=128)[:, :, s0:s0 + TT]
                dst = xn_tt[:].rearrange("p (c t) -> p c t", c=C8)[:, :, 1:TT + 1]
                ctx.inc(sy.dma_start(out=dst, in_=src), "fxnld", 16)
                srcx = I_x1.rearrange("(c p) t -> p c t", p=128)[:, :, s0:s0 + TT]
                ctx.inc(sy.dma_start(out=x1_tt[:].rearrange("p (c t) -> p c t", c=C8), in_=srcx), "fxnld", 16)
            block.sync(_)

            def _(v):
                ctx.wait(v, "fxnld")
                ctx.wait(v, "om_done")
                ctx.wait(v, "fr_mm")
                ctx.wait(v, "fk_mm")
                ctx.wait(v, "fst")
                li = None
                for c in range(C8):
                    xc = xn_tt[:, c * (TT + 1) + 1: c * (TT + 1) + 1 + TT]
                    xp = xn_tt[:, c * (TT + 1): c * (TT + 1) + TT]
                    ctx.chain(v, v.tensor_scalar(tmp_tt[:, c * TT:(c + 1) * TT], xc,
                                                 mus[:, c:c + 1], None, ALU.mult))
                    ctx.chain(v, v.scalar_tensor_tensor(xr_tt[:, c * TT:(c + 1) * TT], xp,
                                                        mus[:, 2 * C8 + c: 2 * C8 + c + 1],
                                                        tmp_tt[:, c * TT:(c + 1) * TT],
                                                        ALU.mult, ALU.add))
                    ctx.chain(v, v.tensor_scalar(tmp_tt[:, c * TT:(c + 1) * TT], xc,
                                                 mus[:, C8 + c: C8 + c + 1], None, ALU.mult))
                    li = v.scalar_tensor_tensor(xk_tt[:, c * TT:(c + 1) * TT], xp,
                                                mus[:, 3 * C8 + c: 3 * C8 + c + 1],
                                                tmp_tt[:, c * TT:(c + 1) * TT],
                                                ALU.mult, ALU.add)
                    if c < C8 - 1:
                        ctx.chain(v, li)
                ctx.inc(li, "fmix")
            block.vector(_)

            def evr(s_, c, pb):
                return s_.activation(r_tt[:, c * TT:(c + 1) * TT], pb[:, 0:TT],
                                     AF.Sigmoid, bias=0.0, scale=1.0)
            emit_proj_act(ctx, "fr", wr_sb, xr_tt, (0, TT), evr,
                          xw=TT, pbanks=(psS_a, psS_b),
                          wait_before=("fmix", "fwld", "wo1_ev"))

            def evk(s_, c, pb):
                ctx.chain(s_, s_.activation(tmp_tt[:, c * TT:(c + 1) * TT], pb[:, 0:TT],
                                            AF.Relu, bias=0.0, scale=1.0))
                return s_.activation(ka_tt[:, c * TT:(c + 1) * TT], tmp_tt[:, c * TT:(c + 1) * TT],
                                     AF.Square, bias=0.0, scale=1.0)
            emit_proj_act(ctx, "fk", wk_sb, xk_tt, (0, TT), evk,
                          xw=TT, pbanks=(psS_a, psS_b),
                          wait_before=("fr_ev",))

            ctx.sem("fv_mm"), ctx.sem("fv_ev"), ctx.sem("fsil")
            fv_b = ctx.cnt["fv_mm"]
            fve_b = ctx.cnt["fv_ev"]
            fsil_b = ctx.cnt["fsil"]

            def _(t, fv_b=fv_b, fve_b=fve_b):
                ctx.wait(t, "fk_ev")
                for c in range(C8):
                    if fve_b + c - 1 > 0:
                        t.wait_ge(ctx.sem("fv_ev"), fve_b + c - 1)
                    pb = psS_a if c % 2 == 0 else psS_b
                    mi = None
                    for kc in range(C8):
                        mi = t.matmul(pb[:, 0:TT],
                                      wv_sb[:, kc * D + c * 128: kc * D + c * 128 + 128],
                                      ka_tt[:, kc * TT:(kc + 1) * TT],
                                      start=(kc == 0), stop=(kc == C8 - 1))
                    ctx.inc(mi, "fv_mm")
            block.tensor(_)

            def _(v, fv_b=fv_b):
                for c in range(C8):
                    v.wait_ge(ctx.sem("fv_mm"), fv_b + c + 1)
                    pb = psS_a if c % 2 == 0 else psS_b
                    li = v.tensor_tensor(tmp_tt[:, c * TT:(c + 1) * TT], pb[:, 0:TT],
                                         r_tt[:, c * TT:(c + 1) * TT], ALU.mult)
                    ctx.inc(li, "fv_ev")
            block.vector(_)

            def _(s_, fve_b=fve_b):
                for c in range(C8):
                    s_.wait_ge(ctx.sem("fv_ev"), fve_b + c + 1)
                    li = s_.activation(tmp_tt[:, c * TT:(c + 1) * TT], tmp_tt[:, c * TT:(c + 1) * TT],
                                       AF.Silu, bias=0.0, scale=1.0)
                    ctx.inc(li, "fsil")
            block.scalar(_)

            def _(v, s0=s0, fsil_b=fsil_b):
                ctx.wait(v, "fst")
                li = None
                for c in range(C8):
                    v.wait_ge(ctx.sem("fsil"), fsil_b + c + 1)
                    ctx.chain(v, v.tensor_tensor(tmp_tt[:, c * TT:(c + 1) * TT],
                                                 tmp_tt[:, c * TT:(c + 1) * TT],
                                                 s1[:, c * TPC + s0: c * TPC + s0 + TT], ALU.mult))
                    li = v.tensor_tensor(x1_tt[:, c * TT:(c + 1) * TT],
                                         x1_tt[:, c * TT:(c + 1) * TT],
                                         tmp_tt[:, c * TT:(c + 1) * TT], ALU.add)
                    if c < C8 - 1:
                        ctx.chain(v, li)
                ctx.inc(li, "fcomb")
            block.vector(_)

            def _(sy, s0=s0):
                ctx.wait(sy, "fcomb")
                dst = O_out.rearrange("(c p) t -> p c t", p=128)[:, :, s0:s0 + TT]
                ctx.inc(sy.dma_start(out=dst, in_=x1_tt[:].rearrange("p (c t) -> p c t", c=C8)), "fst", 16)
            block.sync(_)

        def _(sy):
            ctx.wait(sy, "fst")
        block.sync(_)
        fs.close()

    fin()
    return nc


def emit_proj_act(ctx, name, w_sb, x_sb, x_cols, out_cb, kcn=C8, nd=D, dout_chunks=C8,
                  xw=TT, pbanks=None, wait_before=()):
    nc_mm = name + "_mm"
    nc_ev = name + "_ev"
    ctx.sem(nc_mm), ctx.sem(nc_ev)
    mm_base = ctx.cnt[nc_mm]
    ev_base = ctx.cnt[nc_ev]
    s0, w = x_cols

    def _(t):
        for wsem in wait_before:
            ctx.wait(t, wsem)
        for c in range(dout_chunks):
            vtarget = ev_base + c - 1
            if vtarget > 0:
                t.wait_ge(ctx.sem(nc_ev), vtarget)
            pb = pbanks[c % 2]
            mi = None
            for kc in range(kcn):
                mi = t.matmul(pb[:, 0:w],
                              w_sb[:, kc * nd + c * 128: kc * nd + c * 128 + 128],
                              x_sb[:, kc * xw + s0: kc * xw + s0 + w],
                              start=(kc == 0), stop=(kc == kcn - 1))
            ctx.inc(mi, nc_mm)
    ctx.block.tensor(_)

    def _(s_):
        for c in range(dout_chunks):
            s_.wait_ge(ctx.sem(nc_mm), mm_base + c + 1)
            li = out_cb(s_, c, pbanks[c % 2])
            ctx.inc(li, nc_ev)
    ctx.block.scalar(_)




KOFF = 0
VOFF = D * TPC
NB = D * TPC + TPC * NH * 65
KTILE = D * TT
VTILE = TT * NH * 65
TBLK = KTILE + VTILE


def emit_exchange_zero(ctx, block, F, pfx, khs_name, vtms_name, zsb, zsf, layer):
    khs, vtms = F.io[khs_name], F.io[vtms_name]

    def _(v):
        v.memset(zsb[:], 0.0)
        ctx.inc(v.memset(zsf[:], 0.0), pfx + "zero")
    block.vector(_)

    def _(g):
        ctx.wait(g, pfx + "zero")
        ctx.inc(g.dma_start(
            out=khs.rearrange("(c p) s -> p c s", p=128)[:, :, 0:512],
            in_=zsb[:].rearrange("p (c t) -> p c t", c=C8)[:, :, 0:512]), pfx + "asm", 16)
        ctx.inc(g.dma_start(
            out=vtms[0:512, :].rearrange("(tc p) f -> p tc f", p=128),
            in_=zsb[:].rearrange("p (tc f) -> p tc f", tc=4)), pfx + "asm", 16)
        if layer == 1:
            xp = F.io["xnprev"]
            ctx.inc(g.dma_start(
                out=bass.AP(tensor=xp.tensor, offset=xp.offset,
                            ap=[[8, 128], [1, 8]]),
                in_=zsf[:, 0:8]), pfx + "asm", 16)
    block.gpsimd(_)


def emit_exchange_tile(ctx, block, F, pfx, t, ccin_pfx, khs_name, vtms_name, qm, vm):
    nc = ctx.nc
    khs, vtms = F.io[khs_name], F.io[vtms_name]
    cin = F.io[f"{ccin_pfx}in{t}"]
    cout = F.io[f"{ccin_pfx}out{t}"]
    zcnt = 32 if pfx == "e0" else 48

    def _(g):
        g.wait_ge(ctx.sem("qhst"), qm)
        g.wait_ge(ctx.sem("vtst"), vm)
        g.wait_ge(ctx.sem(pfx + "asm"), zcnt)
        ctx.inc(g.collective_compute(
            "AllGather", ALU.bypass,
            replica_groups=[[0, 1], [2, 3], [4, 5], [6, 7]],
            ins=[cin.opt()], outs=[cout.opt()]), pfx + "cct", 1)
        ctx.inc(g.dma_start(
            out=bass.AP(tensor=khs.tensor, offset=khs.offset + 512 + 1024 * t,
                        ap=[[NSLOT, D], [1, 512]]),
            in_=bass.AP(tensor=cin.tensor, offset=cin.offset,
                        ap=[[512, D], [1, 512]])), pfx + "asm", 16)
        ctx.inc(g.dma_start(
            out=bass.AP(tensor=vtms.tensor, offset=vtms.offset + (512 + 1024 * t) * 1040,
                        ap=[[1, VTILE]]),
            in_=bass.AP(tensor=cin.tensor, offset=cin.offset + KTILE,
                        ap=[[1, VTILE]])), pfx + "asm", 16)
    block.gpsimd(_)

    def _(g):
        for arm in g.switch_core_id(n=128):
            av = arm.value if hasattr(arm, "value") else int(arm)
            par = av % 2
            prow = TBLK if par == 0 else 0
            if par == 0:
                if t >= 3:
                    continue
                g.wait_ge(ctx.sem(pfx + "asm"), zcnt)
                g.wait_ge(ctx.sem(pfx + "cct"), t + 1)
                kdst = khs.offset + 1024 * (t + 1)
                vdst = vtms.offset + (1024 * (t + 1)) * 1040
                if t < 2:
                    g.dma_start(
                        out=bass.AP(tensor=khs.tensor, offset=kdst,
                                    ap=[[NSLOT, D], [1, 512]]),
                        in_=bass.AP(tensor=cout.tensor, offset=prow,
                                    ap=[[512, D], [1, 512]])).then_inc(ctx.sem(pfx + "asm2"), 16)
                    g.dma_start(
                        out=bass.AP(tensor=vtms.tensor, offset=vdst,
                                    ap=[[1, VTILE]]),
                        in_=bass.AP(tensor=cout.tensor, offset=prow + KTILE,
                                    ap=[[1, VTILE]])).then_inc(ctx.sem(pfx + "asm2"), 16)
                else:
                    for h in range(2):
                        g.dma_start(
                            out=bass.AP(tensor=khs.tensor, offset=kdst + 512 * h * NSLOT,
                                        ap=[[NSLOT, 512], [1, 512]]),
                            in_=bass.AP(tensor=cout.tensor, offset=prow + 512 * h * 512,
                                        ap=[[512, 512], [1, 512]])).then_inc(ctx.sem(pfx + "asm2"), 16)
                        g.dma_start(
                            out=bass.AP(tensor=vtms.tensor, offset=vdst + h * (VTILE // 2),
                                        ap=[[1, VTILE // 2]]),
                            in_=bass.AP(tensor=cout.tensor, offset=prow + KTILE + h * (VTILE // 2),
                                        ap=[[1, VTILE // 2]])).then_inc(ctx.sem(pfx + "asm2"), 16)
            else:
                g.wait_ge(ctx.sem(pfx + "asm"), zcnt)
                g.wait_ge(ctx.sem(pfx + "cct"), t + 1)
                g.dma_start(
                    out=bass.AP(tensor=khs.tensor, offset=khs.offset + 1024 * t,
                                ap=[[NSLOT, D], [1, 512]]),
                    in_=bass.AP(tensor=cout.tensor, offset=prow,
                                ap=[[512, D], [1, 512]])).then_inc(ctx.sem(pfx + "asm2"), 16)
                g.dma_start(
                    out=bass.AP(tensor=vtms.tensor, offset=vtms.offset + (1024 * t) * 1040,
                                ap=[[1, VTILE]]),
                    in_=bass.AP(tensor=cout.tensor, offset=prow + KTILE,
                                ap=[[1, VTILE]])).then_inc(ctx.sem(pfx + "asm2"), 16)
        ctx.sem(pfx + "asm2")
        ctx.cnt[_SEM_ALIAS.get(pfx + "asm2", pfx + "asm2")] += 32
    block.gpsimd(_)


def emit_exchange(ctx, nc, pfx, cfin, cfout, extra_out, layer, sidecar_entry):
    es = ExitStack()
    old_scope = ctx.scope
    ctx.scope = es
    block = es.enter_context(nc.Block())
    ctx.block = block

    def _(g):
        for s_ in sidecar_entry:
            ctx.wait(g, s_)
        ctx.inc(g.collective_compute(
            "AllGather", ALU.bypass,
            replica_groups=[[0, 1], [2, 3], [4, 5], [6, 7]],
            ins=[cfin.opt()], outs=[cfout.opt()]), pfx + "ccf", 1)
    block.gpsimd(_)

    fbase = cfout.offset

    if layer == 0:
        def _(sy):
            ctx.wait(sy, pfx + "ccf")
            ctx.inc(sy.dma_start(
                out=bass.AP(tensor=extra_out.tensor, offset=extra_out.offset, ap=[[1, 2048]]),
                in_=bass.AP(tensor=cfout.tensor, offset=fbase, ap=[[1, 2048]])), pfx + "vs2cp", 16)
        block.sync(_)
    else:
        def _(sy):
            for arm in sy.switch_core_id(n=128):
                av = arm.value if hasattr(arm, "value") else int(arm)
                par = av % 2
                sy.wait_ge(ctx.sem(pfx + "asm"), 48)
                sy.wait_ge(ctx.sem(pfx + "ccf"), 1)
                if par == 0:
                    sy.dma_start(
                        out=bass.AP(tensor=extra_out.tensor, offset=extra_out.offset + 1024,
                                    ap=[[1, 3072]]),
                        in_=bass.AP(tensor=cfout.tensor, offset=fbase + 4096,
                                    ap=[[1, 3072]])).then_inc(ctx.sem(pfx + "xpcp"), 16)
                else:
                    sy.dma_start(
                        out=bass.AP(tensor=extra_out.tensor, offset=extra_out.offset,
                                    ap=[[1, 4096]]),
                        in_=bass.AP(tensor=cfout.tensor, offset=fbase,
                                    ap=[[1, 4096]])).then_inc(ctx.sem(pfx + "xpcp"), 16)
            ctx.sem(pfx + "xpcp")
            ctx.cnt[_SEM_ALIAS.get(pfx + "xpcp", pfx + "xpcp")] += 16
        block.sync(_)

    es.close()
    ctx.scope = old_scope


def build_fused(dbg=False):
    nc = bass.Bass("TRN2", target_bir_lowering=False, debug=False)
    io = {}

    def ext(name, shape, dtp):
        io[name] = nc.dram_tensor(name, shape, dtp, kind="ExternalInput").ap()

    def internal(name, shape, dtp):
        t = nc.dram_tensor(name, shape, dtp, kind="Internal").ap()
        io[name] = t
        return t

    ext("xT", [D, TPC], F32)
    ext("Wq", [D, D], F32); ext("Wk", [D, D], F32); ext("Wv", [D, D], F32)
    ext("ln1w", [128, C8], F32); ext("selw", [D, NH], F32); ext("selT", [NH, D], F32)
    ext("eyeb", [128, 128], BF16)
    ext("scaler", [1, 16], F32); ext("scalec", [16, 1], F32); ext("relb", [32, 16], F32)
    ext("OH", [32, 112], F32); ext("J128", [128, 128], F32); ext("eyef", [128, 128], F32)
    ext("alpha", [128, 1], F32); ext("beta", [128, 1], F32); ext("Wo", [D, D], F32)
    ext("ln2w", [128, C8], F32); ext("W1", [D, 4 * D], F32); ext("W2", [4 * D, D], F32)
    ext("gateT", [D, 8], F32)
    ext("mw1T", [8 * D, D], F32); ext("mw2T", [8 * D, D], F32); ext("mw3T", [8 * D, D], F32)
    ext("B1", [128, 64], F32); ext("B2", [128, 64], F32); ext("B3", [128, 64], F32)
    ext("sw1T", [D, 2 * D], F32); ext("sw2T", [D, 2 * D], F32); ext("sw3T", [2 * D, D], F32)
    ext("iota8", [128, 8], F32)
    ext("TRI", [128, 128], F32)
    ext("eyebm", [128, 128], BF16)
    ext("Wq1", [D, D], F32); ext("Wk1", [D, D], F32); ext("Wv1", [D, D], F32)
    ext("bq1", [128, C8], F32); ext("bk1", [128, C8], F32); ext("bv1", [128, C8], F32)
    ext("eyeb1", [128, 128], BF16)
    ext("Wo1", [D, D], F32); ext("bo1", [128, C8], F32)
    ext("WrT", [D, D], F32); ext("WkT", [D, D], F32); ext("WvT", [D, D], F32)
    ext("mur", [128, C8], F32); ext("muk", [128, C8], F32); ext("mask1", [128, 128], F32)

    internal("xn", [D, TPC], F32)
    internal("qh", [D, TPC], BF16)
    cc0t = []
    for t in range(NTT):
        cin = internal(f"cc0in{t}", [1, TBLK], BF16)
        cout = internal(f"cc0out{t}", [2, TBLK], BF16)
        cc0t.append((cin, cout))
        io[f"kh_t{t}"] = bass.AP(tensor=cin.tensor, offset=0, ap=[[TT, D], [1, TT]])
        io[f"vt_t{t}"] = bass.AP(tensor=cin.tensor, offset=KTILE, ap=[[NH * 65, TT], [1, NH * 65]])
    io["kh"] = io["kh_t0"]
    io["vt"] = io["vt_t0"]
    ccf0in = internal("ccf0in", [1, 1024], F32)
    ccf0out = internal("ccf0out", [2, 1024], F32)
    io["vsum"] = bass.AP(tensor=ccf0in.tensor, offset=0, ap=[[1, 1024]])
    khs = internal("khs", [D, NSLOT], BF16)
    vtms = internal("vtms", [NSLOT, NH * 65], BF16)
    vs2f = internal("vsum2f", [1, 2048], F32)
    io["vsum2"] = bass.AP(tensor=vs2f.tensor, offset=0, ap=[[1, 2048]])
    internal("gscratch", [16, 384], F32)
    internal("y", [D, TPC], F32)
    internal("s_scr", [D, TPC], F32)
    internal("xn_tm", [TPC, D], BF16)
    internal("xbkt", [8 * 768, D], BF16)
    internal("prodbkt", [8 * 768, D], BF16)
    internal("x1", [D, TPC], F32)
    io["x1T"] = io["x1"]
    internal("xn1", [D, TPC], F32)
    internal("Q1", [D, TPC], BF16)
    cc1t = []
    for t in range(NTT):
        cin = internal(f"cc1in{t}", [1, TBLK], BF16)
        cout = internal(f"cc1out{t}", [2, TBLK], BF16)
        cc1t.append((cin, cout))
        io[f"K1_t{t}"] = bass.AP(tensor=cin.tensor, offset=0, ap=[[TT, D], [1, TT]])
        io[f"V1t_t{t}"] = bass.AP(tensor=cin.tensor, offset=KTILE, ap=[[NH * 65, TT], [1, NH * 65]])
    io["K1"] = io["K1_t0"]
    io["V1t"] = io["V1t_t0"]
    ccf1in = internal("ccf1in", [1, 4096], F32)
    ccf1out = internal("ccf1out", [2, 4096], F32)
    K1s = internal("K1s", [D, NSLOT], BF16)
    V1ts = internal("V1ts", [NSLOT, NH * 65], BF16)
    xnprev = internal("xnprev", [1, 4096], F32)
    io["xnprev"] = xnprev
    io["out"] = nc.dram_tensor("out", [D, TPC], F32, kind="ExternalOutput").ap()

    es = ExitStack()
    with es:
        ctx = Ctx(nc, es, None)
        with nc.named_scope("s1_ln_qkv"):
            build_launch1(F=Fused(nc, ctx, io, entry=()))
        with nc.named_scope("e0_exchange"):
            emit_exchange(ctx, nc, "e0", ccf0in, ccf0out,
                          io["vsum2"], 0, sidecar_entry=("vsst",))
        with nc.named_scope("s2_attn0"):
            build_launch2(F=Fused(nc, ctx, io, entry=("vsst",)))
        with nc.named_scope("s3_mlp_moe"):
            build_launch3(F=Fused(nc, ctx, io, entry=("yst_st",)))
        with nc.named_scope("s4_ln1_qkv"):
            build_launch4(F=Fused(nc, ctx, io, entry=("fst",)))
        with nc.named_scope("e1_exchange"):
            emit_exchange(ctx, nc, "e1", ccf1in, ccf1out,
                          xnprev, 1, sidecar_entry=("xnst",))
        with nc.named_scope("s5_attn1_ffn"):
            build_launch5(F=Fused(nc, ctx, io,
                                  entry=("qhst", "vtst", "xnst")))
        if dbg:
            dbgs = [("xn", [D, TPC], F32), ("qh", [D, TPC], BF16),
                    ("khs", [D, NSLOT], BF16), ("vtms", [NSLOT, NH * 65], BF16),
                    ("vsum2f", [1, 2048], F32), ("y", [D, TPC], F32),
                    ("x1", [D, TPC], F32), ("xn1", [D, TPC], F32),
                    ("Q1", [D, TPC], BF16), ("K1s", [D, NSLOT], BF16),
                    ("V1ts", [NSLOT, NH * 65], BF16), ("xnprev", [1, 4096], F32),
                    ("s_scr", [D, TPC], F32), ("gatesT", [32, TPC], F32)]
            block = es.enter_context(nc.Block())
            ctx.block = block

            def _(sy):
                ctx.wait(sy, "fst")
                for nm, shp, dtp in dbgs:
                    o = nc.dram_tensor("dbg_" + nm, shp, dtp, kind="ExternalOutput").ap()
                    ctx.inc(sy.dma_start(out=o, in_=io[nm] if nm not in ("vsum2f",) else
                                         bass.AP(tensor=io["vsum2"].tensor, offset=0, ap=[[2048, 1], [1, 2048]])),
                            "dbgst", 16)
                ctx.wait(sy, "dbgst")
            block.sync(_)
    return nc


def shard_tokens(core):
    par = core % 2
    return np.concatenate([np.arange(512 * (2 * j + par), 512 * (2 * j + par) + 512)
                           for j in range(4)])


def head_selector():
    sel = np.zeros((D, NH), np.float32)
    sel[np.arange(D), np.arange(D) // HD] = 1.0
    return sel


def launch1_inputs(x, Wq, Wk, Wv, ln1_w):
    sel = head_selector()
    eye = np.eye(128).astype(bf16)
    maps = []
    for c in range(8):
        b = c // 2
        toks = shard_tokens(c)
        xT = np.ascontiguousarray(x[b][toks].T)
        ln1 = np.ascontiguousarray(np.asarray(ln1_w, np.float32).reshape(C8, 128).T)
        maps.append(dict(xT=xT, Wq=Wq, Wk=Wk, Wv=Wv, ln1w=ln1,
                         selw=sel, selT=np.ascontiguousarray(sel.T), eyeb=eye))
    return maps


def _bucket_fn(d):
    import math
    d = np.asarray(d)
    rpn = np.maximum(d, 0)
    max_exact = 16
    safe = np.maximum(rpn, 1).astype(np.float32)
    large = max_exact + (np.log(safe / max_exact) / math.log(128 / max_exact) * 16).astype(np.int32)
    large = np.minimum(large, 31)
    return np.where(rpn < max_exact, rpn, large)


def launch2_consts():
    OH = np.zeros((32, 112), np.float32)
    for d in range(1, 113):
        OH[_bucket_fn(d), d - 1] = 1.0
        OH[31, d - 1] -= 1.0
    J = np.zeros((128, 128), np.float32)
    J[np.arange(128), 127 - np.arange(128)] = 1.0
    eyef = np.eye(128, dtype=np.float32)
    return OH, J, eyef


def launch2_inputs(l1outs, inp):
    OH, J, eyef = launch2_consts()
    scale = np.asarray(inp["L0_attn_scale"], np.float32)
    relb = np.asarray(inp["L0_attn_rel_bias"], np.float32)
    Wo = np.asarray(inp["L0_attn_Wo"], np.float32)
    maps = []
    for c in range(8):
        par = c % 2
        pc = c ^ 1
        toks = shard_tokens(c)
        ptoks = shard_tokens(pc)
        khG = np.zeros((D, NSLOT), dtype=bf16)
        khG[:, toks] = l1outs[c]["kh"]
        khG[:, ptoks] = l1outs[pc]["kh"]
        vtG = np.zeros((NSLOT, NH * 65), dtype=bf16)
        vtG[toks] = l1outs[c]["vt"]
        vtG[ptoks] = l1outs[pc]["vt"]
        if par == 0:
            khs = np.zeros((D, NSLOT), dtype=bf16)
            khs[:, 512:] = khG[:, :NSLOT - 512]
            vtms = np.zeros((NSLOT, NH * 65), dtype=bf16)
            vtms[512:] = vtG[:NSLOT - 512]
        else:
            khs, vtms = khG, vtG
        vs2 = np.concatenate([l1outs[c]["vsum"], l1outs[pc]["vsum"]], axis=1)
        alpha = np.full((128, 1), 0.0 if par == 0 else 1.0, np.float32)
        beta = np.full((128, 1), 1.0 if par == 0 else 0.0, np.float32)
        maps.append(dict(xn=l1outs[c]["xn"], qh=l1outs[c]["qh"],
                         khs=np.ascontiguousarray(khs), vtms=np.ascontiguousarray(vtms),
                         vsum2=np.ascontiguousarray(vs2),
                         scaler=scale.reshape(1, 16), scalec=np.ascontiguousarray(scale.reshape(16, 1)),
                         relb=relb, OH=OH, J128=J, eyef=eyef,
                         alpha=alpha, beta=beta, Wo=Wo))
    return maps


def launch3_inputs(x, l1outs, l2outs, inp):
    OHn, J, eyef = launch2_consts()
    iota8 = np.tile(np.arange(8, dtype=np.float32), (128, 1))
    ln2 = np.ascontiguousarray(np.asarray(inp["L0_attn_ln2_w"], np.float32).reshape(C8, 128).T)
    W1 = np.asarray(inp["L0_attn_W1"], np.float32)
    W2 = np.asarray(inp["L0_attn_W2"], np.float32)
    gateT = np.ascontiguousarray(np.asarray(inp["L0_ffn_gate_w"], np.float32).T)
    w1 = np.asarray(inp["L0_ffn_w1"], np.float32)
    w2 = np.asarray(inp["L0_ffn_w2"], np.float32)
    w3 = np.asarray(inp["L0_ffn_w3"], np.float32)
    mw1T = np.ascontiguousarray(w1.transpose(0, 2, 1)).reshape(8 * D, D)
    mw2T = np.ascontiguousarray(w2.transpose(0, 2, 1)).reshape(8 * D, D)
    mw3T = np.ascontiguousarray(w3.transpose(0, 2, 1)).reshape(8 * D, D)
    def blay(b):
        return np.ascontiguousarray(np.asarray(b, np.float32).reshape(8, 8, 128).transpose(2, 0, 1).reshape(128, 64))
    B1, B2, B3 = blay(inp["L0_ffn_b1"]), blay(inp["L0_ffn_b2"]), blay(inp["L0_ffn_b3"])
    sw1T = np.ascontiguousarray(np.asarray(inp["L0_ffn_sw1"], np.float32).T)
    sw2T = np.ascontiguousarray(np.asarray(inp["L0_ffn_sw2"], np.float32).T)
    sw3T = np.ascontiguousarray(np.asarray(inp["L0_ffn_sw3"], np.float32).T)
    TRI = np.ascontiguousarray(np.triu(np.ones((128, 128), np.float32), 1))
    eyebm = np.eye(128).astype(bf16)
    maps = []
    for c in range(8):
        b = c // 2
        toks = shard_tokens(c)
        xT = np.ascontiguousarray(x[b][toks].T)
        maps.append(dict(xT=xT, xn=l1outs[c]["xn"], y=l2outs[c]["y"], ln2w=ln2,
                         W1=W1, W2=W2, gateT=gateT, mw1T=mw1T, mw2T=mw2T, mw3T=mw3T,
                         B1=B1, B2=B2, B3=B3, sw1T=sw1T, sw2T=sw2T, sw3T=sw3T,
                         eyef=eyef, iota8=iota8, xn_tm=l1outs[c]["xn_tm"],
                         TRI=TRI, eyebm=eyebm))
    return maps


def launch4_inputs(l3outs, inp):
    eye = np.eye(128).astype(bf16)
    def cat(w):
        return np.ascontiguousarray(np.asarray(w, np.float32).transpose(1, 0, 2).reshape(D, D))
    def bl(b):
        return np.ascontiguousarray(np.asarray(b, np.float32).reshape(-1).reshape(C8, 128).T)
    Wq1, Wk1, Wv1 = cat(inp["L1_attn_Wq"]), cat(inp["L1_attn_Wk"]), cat(inp["L1_attn_Wv"])
    bq1, bk1, bv1 = bl(inp["L1_attn_bq"]), bl(inp["L1_attn_bk"]), bl(inp["L1_attn_bv"])
    maps = []
    for c in range(8):
        maps.append(dict(x1T=l3outs[c]["x1"], Wq1=Wq1, Wk1=Wk1, Wv1=Wv1,
                         bq1=bq1, bk1=bk1, bv1=bv1, eyeb=eye))
    return maps


def launch5_inputs(l3outs, l4outs, inp):
    Wo1 = np.asarray(inp["L1_attn_Wo"], np.float32)
    bo1 = np.ascontiguousarray(np.asarray(inp["L1_attn_bo"], np.float32).reshape(C8, 128).T)
    WrT = np.ascontiguousarray(np.asarray(inp["L1_ffn_W_r"], np.float32).T)
    WkT = np.ascontiguousarray(np.asarray(inp["L1_ffn_W_k"], np.float32).T)
    WvT = np.ascontiguousarray(np.asarray(inp["L1_ffn_W_v"], np.float32).T)
    mur = np.ascontiguousarray(np.asarray(inp["L1_ffn_mu_r"], np.float32).reshape(C8, 128).T)
    muk = np.ascontiguousarray(np.asarray(inp["L1_ffn_mu_k"], np.float32).reshape(C8, 128).T)
    mask1 = np.zeros((128, 128), np.float32)
    for kk in range(128):
        mask1[kk, :kk] = -BIG
    maps = []
    for c in range(8):
        par = c % 2
        pc = c ^ 1
        toks = shard_tokens(c)
        ptoks = shard_tokens(pc)
        khG = np.zeros((D, NSLOT), dtype=bf16)
        khG[:, toks] = l4outs[c]["K1"]
        khG[:, ptoks] = l4outs[pc]["K1"]
        vtG = np.zeros((NSLOT, NH * 65), dtype=bf16)
        vtG[toks] = l4outs[c]["V1t"]
        vtG[ptoks] = l4outs[pc]["V1t"]
        if par == 0:
            khs = np.zeros((D, NSLOT), dtype=bf16)
            khs[:, 512:] = khG[:, :NSLOT - 512]
            vtms = np.zeros((NSLOT, NH * 65), dtype=bf16)
            vtms[512:] = vtG[:NSLOT - 512]
        else:
            khs, vtms = khG, vtG
        xnprev = np.zeros((D, 4), np.float32)
        for j in range(4):
            g0 = 512 * (2 * j + par)
            if g0 == 0:
                continue
            i = (j - 1) if par == 0 else j
            xnprev[:, j] = l4outs[pc]["xn1"][:, i * 512 + 511]
        maps.append(dict(x1T=l3outs[c]["x1"], xn1=l4outs[c]["xn1"], xnprev=xnprev,
                         Q1=l4outs[c]["Q1"], K1s=np.ascontiguousarray(khs),
                         V1ts=np.ascontiguousarray(vtms),
                         Wo1=Wo1, bo1=bo1, WrT=WrT, WkT=WkT, WvT=WvT,
                         mur=mur, muk=muk, mask1=mask1))
    return maps


def fused_inputs(inputs):
    x = np.ascontiguousarray(np.asarray(inputs["L0_attn_x"], np.float32))
    OHc, J, eyef = launch2_consts()
    sel = head_selector()
    eye = np.eye(128).astype(bf16)
    scale = np.asarray(inputs["L0_attn_scale"], np.float32)
    iota8 = np.tile(np.arange(8, dtype=np.float32), (128, 1))
    mask1 = np.zeros((128, 128), np.float32)
    for kk in range(128):
        mask1[kk, :kk] = -BIG

    def colv(v):
        return np.ascontiguousarray(np.asarray(v, np.float32).reshape(C8, 128).T)

    def cat(w):
        return np.ascontiguousarray(np.asarray(w, np.float32).transpose(1, 0, 2).reshape(D, D))

    def blay(b):
        return np.ascontiguousarray(np.asarray(b, np.float32).reshape(8, 8, 128).transpose(2, 0, 1).reshape(128, 64))

    com = dict(
        Wq=np.asarray(inputs["L0_attn_Wq"], np.float32),
        Wk=np.asarray(inputs["L0_attn_Wk"], np.float32),
        Wv=np.asarray(inputs["L0_attn_Wv"], np.float32),
        ln1w=colv(inputs["L0_attn_ln1_w"]), selw=sel,
        selT=np.ascontiguousarray(sel.T), eyeb=eye,
        scaler=scale.reshape(1, 16),
        scalec=np.ascontiguousarray(scale.reshape(16, 1)),
        relb=np.asarray(inputs["L0_attn_rel_bias"], np.float32),
        OH=OHc, J128=J, eyef=eyef,
        Wo=np.asarray(inputs["L0_attn_Wo"], np.float32),
        ln2w=colv(inputs["L0_attn_ln2_w"]),
        W1=np.asarray(inputs["L0_attn_W1"], np.float32),
        W2=np.asarray(inputs["L0_attn_W2"], np.float32),
        gateT=np.ascontiguousarray(np.asarray(inputs["L0_ffn_gate_w"], np.float32).T),
        mw1T=np.ascontiguousarray(np.asarray(inputs["L0_ffn_w1"], np.float32).transpose(0, 2, 1)).reshape(8 * D, D),
        mw2T=np.ascontiguousarray(np.asarray(inputs["L0_ffn_w2"], np.float32).transpose(0, 2, 1)).reshape(8 * D, D),
        mw3T=np.ascontiguousarray(np.asarray(inputs["L0_ffn_w3"], np.float32).transpose(0, 2, 1)).reshape(8 * D, D),
        B1=blay(inputs["L0_ffn_b1"]), B2=blay(inputs["L0_ffn_b2"]), B3=blay(inputs["L0_ffn_b3"]),
        sw1T=np.ascontiguousarray(np.asarray(inputs["L0_ffn_sw1"], np.float32).T),
        sw2T=np.ascontiguousarray(np.asarray(inputs["L0_ffn_sw2"], np.float32).T),
        sw3T=np.ascontiguousarray(np.asarray(inputs["L0_ffn_sw3"], np.float32).T),
        iota8=iota8,
        TRI=np.ascontiguousarray(np.triu(np.ones((128, 128), np.float32), 1)),
        eyebm=eye,
        Wq1=cat(inputs["L1_attn_Wq"]), Wk1=cat(inputs["L1_attn_Wk"]), Wv1=cat(inputs["L1_attn_Wv"]),
        bq1=colv(np.asarray(inputs["L1_attn_bq"], np.float32).reshape(-1)),
        bk1=colv(np.asarray(inputs["L1_attn_bk"], np.float32).reshape(-1)),
        bv1=colv(np.asarray(inputs["L1_attn_bv"], np.float32).reshape(-1)),
        eyeb1=eye,
        Wo1=np.asarray(inputs["L1_attn_Wo"], np.float32),
        bo1=colv(inputs["L1_attn_bo"]),
        WrT=np.ascontiguousarray(np.asarray(inputs["L1_ffn_W_r"], np.float32).T),
        WkT=np.ascontiguousarray(np.asarray(inputs["L1_ffn_W_k"], np.float32).T),
        WvT=np.ascontiguousarray(np.asarray(inputs["L1_ffn_W_v"], np.float32).T),
        mur=colv(inputs["L1_ffn_mu_r"]), muk=colv(inputs["L1_ffn_mu_k"]),
        mask1=mask1,
    )
    maps = []
    for c in range(8):
        b = c // 2
        par = c % 2
        toks = shard_tokens(c)
        m = dict(com)
        m["xT"] = np.ascontiguousarray(x[b][toks].T)
        m["alpha"] = np.full((128, 1), 0.0 if par == 0 else 1.0, np.float32)
        m["beta"] = np.full((128, 1), 1.0 if par == 0 else 0.0, np.float32)
        maps.append(m)
    return maps


def kernel_fused(**inputs):
    nc = _get_nc("fused", build_fused)
    maps = fused_inputs(inputs)
    res = run_bass_kernel_spmd(nc, maps, core_ids=list(range(8)))
    out = np.empty((4, 4096, 1024), np.float32)
    for c in range(8):
        b = c // 2
        toks = shard_tokens(c)
        out[b][toks] = np.asarray(res.results[c]["out"]).T
    return out


_NC_CACHE = {}


def _get_nc(name, builder):
    if name not in _NC_CACHE:
        _NC_CACHE[name] = builder()
    return _NC_CACHE[name]


import os as _os


def kernel(**inputs):
    if _os.environ.get("MK_UNFUSED", "") != "1":
        return kernel_fused(**inputs)
    x = np.ascontiguousarray(np.asarray(inputs["L0_attn_x"], np.float32))

    nc1 = _get_nc("l1", build_launch1)
    maps1 = launch1_inputs(x, np.asarray(inputs["L0_attn_Wq"], np.float32),
                           np.asarray(inputs["L0_attn_Wk"], np.float32),
                           np.asarray(inputs["L0_attn_Wv"], np.float32),
                           np.asarray(inputs["L0_attn_ln1_w"], np.float32))
    r1 = run_bass_kernel_spmd(nc1, maps1, core_ids=list(range(8)))
    l1outs = [{k: np.asarray(r1.results[i][k]) for k in ("xn", "qh", "kh", "vt", "vsum", "xn_tm")}
              for i in range(8)]

    nc2 = _get_nc("l2", build_launch2)
    maps2 = launch2_inputs(l1outs, inputs)
    r2 = run_bass_kernel_spmd(nc2, maps2, core_ids=list(range(8)))
    l2outs = [{"y": np.asarray(r2.results[i]["y"])} for i in range(8)]

    nc3 = _get_nc("l3", build_launch3)
    maps3 = launch3_inputs(x, l1outs, l2outs, inputs)
    r3 = run_bass_kernel_spmd(nc3, maps3, core_ids=list(range(8)))
    l3outs = [{"x1": np.asarray(r3.results[i]["x1"])} for i in range(8)]

    nc4 = _get_nc("l4", build_launch4)
    maps4 = launch4_inputs(l3outs, inputs)
    r4 = run_bass_kernel_spmd(nc4, maps4, core_ids=list(range(8)))
    l4outs = [{k: np.asarray(r4.results[i][k]) for k in ("xn1", "Q1", "K1", "V1t")}
              for i in range(8)]

    nc5 = _get_nc("l5", build_launch5)
    maps5 = launch5_inputs(l3outs, l4outs, inputs)
    r5 = run_bass_kernel_spmd(nc5, maps5, core_ids=list(range(8)))

    out = np.empty((4, 4096, 1024), np.float32)
    for c in range(8):
        b = c // 2
        toks = shard_tokens(c)
        out[b][toks] = np.asarray(r5.results[c]["out"]).T
    return out
